# version 05
# baseline (speedup 1.0000x reference)
import math
from functools import partial
import numpy as np
import ml_dtypes
from contextlib import ExitStack

import concourse.bass as bass
import concourse.mybir as mybir
from concourse.alu_op_type import AluOpType as OP
from concourse.bass_utils import run_bass_kernel_spmd

F32 = mybir.dt.float32
BF16 = mybir.dt.bfloat16
AF = mybir.ActivationFunctionType

D = 1024
NCH = 8
NH_A = 8
DH_IN = 64
WIN = 256
DI = 2730
DIP = 2816
NJ = 22
DEPTH_IN = 6
NEG = -1e30
DBG_SKIP = set()
LNC_A = -0.5 * math.log(128.0)

bfd = ml_dtypes.bfloat16
ENGINES = ("tensor", "vector", "scalar", "gpsimd", "sync")


def i_mm(E, out, lhsT, rhs, start, stop):
    return E.matmul(out, lhsT, rhs, start=start, stop=stop)

def i_tr(E, out, src, iden):
    return E.transpose(out, src, iden)

def i_act(E, out, src, func, bias=0.0, scale=1.0, accum=None):
    return E.activation(out, src, func, bias=bias, scale=scale, accum_out=accum)

def i_tt(E, out, a, b, op):
    return E.tensor_tensor(out, a, b, op)

def i_ts(E, out, a, s1, s2, op0, op1):
    if s2 is None and op1 is None:
        return E.tensor_scalar(out, a, s1, None, op0)
    return E.tensor_scalar(out, a, s1, s2, op0, op1)

def i_stt(E, out, a, s, b, op0, op1):
    return E.scalar_tensor_tensor(out, a, s, b, op0, op1)

def i_scan(E, out, d0, d1, init, op0, op1):
    return E.tensor_tensor_scan(out, d0, d1, init, op0, op1)

def i_recip(E, out, src):
    return E.reciprocal(out, src)

def i_copy(E, out, src):
    return E.tensor_copy(out, src)

def i_memset(E, ap, c):
    return E.memset(ap, c)

def i_dma(E, out, src):
    return E.dma_start(out, src)

def i_wait(E, h, v):
    return E.wait_ge(h, v)

def i_drain(E):
    return E.drain()


class Sem:
    def __init__(self, handle, name):
        self.h = handle
        self.name = name
        self.n = 0


class Ctx:
    def __init__(self, nc):
        self.nc = nc
        self.code = {e: [] for e in ENGINES}
        self.last_wait = {}
        self.es = ExitStack()
        self.n_stages = 0
        self.n_instr = 0

    def sem(self, name):
        return Sem(self.es.enter_context(self.nc.semaphore(name)), name)

    def emit(self, eng, fn):
        assert isinstance(fn, partial)
        self.n_instr += 1
        self.code[eng].append(fn)

    def wait(self, eng, sem, val):
        if val is None or val <= 0:
            return
        if val > sem.n:
            raise RuntimeError(f"wait {sem.name}>={val} but only {sem.n} emitted")
        key = (eng, sem.name)
        if self.last_wait.get(key, 0) >= val:
            return
        self.last_wait[key] = val
        self.code[eng].append(partial(i_wait, h=sem.h, v=val))

    def wait_marks(self, eng, marks):
        for sem, val in marks:
            self.wait(eng, sem, val)

    def dr(self, eng):
        self.code[eng].append(partial(i_drain))

    def inc(self, eng, fn, sem, k=1):
        assert isinstance(fn, partial)
        self.n_instr += 1
        sem.n += k
        v = sem.n

        def run(E, fn=fn, h=sem.h, k=k):
            fn(E).then_inc(h, k)
        self.code[eng].append(run)
        return v

    def flush(self):
        code = self.code
        self.code = {e: [] for e in ENGINES}
        if not any(code.values()):
            return
        self.n_stages += 1
        with self.nc.allow_low_precision(reason="bf16 storage by design"), \
                self.nc.Block() as blk:
            if code["gpsimd"]:
                @blk.gpsimd
                def _(E, ops=code["gpsimd"]):
                    for f in ops:
                        f(E)
            if code["sync"]:
                @blk.sync
                def _(E, ops=code["sync"]):
                    for f in ops:
                        f(E)
            if code["tensor"]:
                @blk.tensor
                def _(E, ops=code["tensor"]):
                    for f in ops:
                        f(E)
            if code["scalar"]:
                @blk.scalar
                def _(E, ops=code["scalar"]):
                    for f in ops:
                        f(E)
            if code["vector"]:
                @blk.vector
                def _(E, ops=code["vector"]):
                    for f in ops:
                        f(E)


class PBuf:
    __slots__ = ("t", "dtype", "free")

    def __init__(self, t, dtype):
        self.t = t
        self.dtype = dtype
        self.free = []


class PsumPool:
    def __init__(self, ctx):
        self.ctx = ctx
        nc = ctx.nc
        self.f32 = [PBuf(ctx.es.enter_context(nc.psum_tensor(f"psf{i}", [128, 512], F32)), F32)
                    for i in range(6)]
        self.b16 = [PBuf(ctx.es.enter_context(nc.psum_tensor(f"psb{i}", [128, 1024], BF16)), BF16)
                    for i in range(2)]

    def get(self, dtype=F32):
        pool = self.f32 if dtype == F32 else self.b16
        if not pool:
            raise RuntimeError("psum pool exhausted")
        b = pool.pop(0)
        self.ctx.wait_marks("tensor", b.free)
        b.free = []
        return b

    def put(self, b, sem, val):
        b.free.append((sem, val))
        (self.f32 if b.dtype == F32 else self.b16).append(b)


class WStream:
    def __init__(self, ctx, gd, name, shape, slots=3, stack=None):
        self.ctx = ctx
        stk = stack if stack is not None else ctx.es
        self.tiles = [stk.enter_context(ctx.nc.sbuf_tensor(f"{name}{i}", list(shape), BF16))
                      for i in range(slots)]
        self.sems = [ctx.sem(f"{name}{i}s") for i in range(slots)]
        self.slots = slots
        self.i = 0
        self.ready = [0] * slots
        self.free = [[] for _ in range(slots)]

    def load(self, dram_ap, ncols=None):
        s = self.i % self.slots
        self.i += 1
        self.ctx.wait_marks("gpsimd", self.free[s])
        dest = self.tiles[s][:, :] if ncols is None else self.tiles[s][:, 0:ncols]
        self.ready[s] = self.ctx.inc("gpsimd", partial(i_dma, out=dest, src=dram_ap),
                                     self.sems[s], 16)
        return s

    def use(self, s):
        self.ctx.wait("tensor", self.sems[s], self.ready[s])
        return self.tiles[s]

    def done(self, s, sem, val):
        self.free[s] = [(sem, val)]


def _cols(v):
    v = np.asarray(v, np.float32)
    k = v.size // 128
    return np.ascontiguousarray(v.reshape(k, 128).T)


def _rotary_tables(S):
    hd = np.arange(0, DH_IN, 2, dtype=np.float64)
    inv_freq = 1.0 / 10000.0 ** (hd / DH_IN)
    t = np.arange(S, dtype=np.float64)
    fh = t[:, None] * inv_freq[None, :]
    freqs = np.concatenate([fh, fh], -1)
    base = (hd + 0.4 * DH_IN) / (1.4 * DH_IN)
    power = (t - S // 2) / (WIN // 2)
    sch = base[None, :] ** power[:, None]
    scale = np.concatenate([sch, sch], -1)
    cos, sin = np.cos(freqs), np.sin(freqs)
    mk = lambda a: np.ascontiguousarray(np.concatenate([a.T, a.T], 0).astype(bfd))
    return (mk(cos * scale), mk(sin * scale),
            mk(cos / scale), mk(sin / scale))


def _masks():
    il = np.arange(128)[:, None]
    jl = np.arange(512)[None, :]
    m0 = np.where((jl >= il) & (jl <= il + 256), 0.0, NEG)
    m1 = np.where((jl >= il + 128) & (jl <= il + 384), 0.0, NEG)
    jl2 = np.arange(256)[None, :]
    m0w = np.where(jl2 <= il, 0.0, NEG)
    m1w = np.where(jl2 <= il + 128, 0.0, NEG)
    tb = np.arange(128)[:, None]
    sb2 = np.arange(128)[None, :]
    mbin = (tb <= sb2).astype(np.float32)
    c = lambda a: np.ascontiguousarray(a.astype(bfd))
    return c(m0), c(m1), c(m0w), c(m1w), c(mbin)


def host_prepare(inputs, S, n_cores):
    f32 = np.float32
    sh = {}
    x = np.asarray(inputs["L0_attn_a_x"], f32)
    per_core = [{"xin": np.ascontiguousarray(x[c, :S, :].T)} for c in range(n_cores)]
    T = lambda a: np.ascontiguousarray(np.asarray(a, f32).T)

    sh["a_wqT"] = T(inputs["L0_attn_a_Wq"])
    sh["a_wkT"] = T(inputs["L0_attn_a_Wk"])
    sh["a_wvT"] = T(inputs["L0_attn_a_Wv"])
    sh["a_wogT"] = T(inputs["L0_attn_a_Wog"])
    sh["a_woutT"] = T(inputs["L0_attn_a_Wout"])
    sh["a_wifT"] = np.ascontiguousarray(
        np.concatenate([T(inputs["L0_attn_a_Wi"]), T(inputs["L0_attn_a_Wf"])], axis=1))
    bif = np.zeros((128, 1), f32)
    bif[0:8, 0] = np.asarray(inputs["L0_attn_a_bi"], f32) / 15.0
    bif[32:40, 0] = np.asarray(inputs["L0_attn_a_bf"], f32) / 15.0
    sh["a_bif"] = bif
    sh["a_lnw"] = _cols(inputs["L0_attn_a_ln_w"])

    sh["b_wqT"] = T(inputs["L0_attn_b_Wq"])
    sh["b_wkT"] = T(inputs["L0_attn_b_Wk"])
    sh["b_wvT"] = T(inputs["L0_attn_b_Wv"])
    sh["b_woT"] = T(inputs["L0_attn_b_Wo"])
    for nm in ("bq", "bk", "bv", "bo"):
        sh["b_" + nm] = _cols(inputs["L0_attn_b_" + nm])

    sh["f_wqkv"] = np.ascontiguousarray(np.asarray(inputs["L0_ffn_Wqkv"], f32))
    sh["f_wo"] = np.ascontiguousarray(np.asarray(inputs["L0_ffn_Wo"], f32))
    w1 = np.asarray(inputs["L0_ffn_W1"], f32)
    w1p = np.zeros((DEPTH_IN, D, 2 * DIP), f32)
    w1p[:, :, :DI] = w1[:, :, :DI]
    w1p[:, :, DIP:DIP + DI] = w1[:, :, DI:]
    sh["f_w1"] = np.ascontiguousarray(w1p)
    w2 = np.asarray(inputs["L0_ffn_W2"], f32)
    w2p = np.zeros((DEPTH_IN, DIP, D), f32)
    w2p[:, :DI, :] = w2
    sh["f_w2"] = np.ascontiguousarray(w2p)
    b1 = np.asarray(inputs["L0_ffn_b1"], f32)
    b1a = np.zeros((DEPTH_IN, DIP), f32); b1a[:, :DI] = b1[:, :DI]
    b1g = np.zeros((DEPTH_IN, DIP), f32); b1g[:, :DI] = b1[:, DI:]
    hstk = lambda lst: np.ascontiguousarray(np.concatenate(lst, axis=1))
    sh["f_b1a"] = hstk([_cols(b1a[i]) for i in range(DEPTH_IN)])
    sh["f_b1g"] = hstk([_cols(b1g[i]) for i in range(DEPTH_IN)])
    sh["f_b2"] = hstk([_cols(np.asarray(inputs["L0_ffn_b2"], f32)[i]) for i in range(DEPTH_IN)])
    sh["f_g"] = hstk([_cols(np.asarray(inputs["L0_ffn_ln_g"], f32)[i]) for i in range(DEPTH_IN)])
    sh["f_bln"] = hstk([_cols(np.asarray(inputs["L0_ffn_ln_b"], f32)[i]) for i in range(DEPTH_IN)])
    qs = np.asarray(inputs["L0_ffn_q_scale"], f32)
    ks = np.asarray(inputs["L0_ffn_k_scale"], f32)
    sh["f_qs"] = np.ascontiguousarray(np.concatenate([qs, qs], 1).T)
    sh["f_ks"] = np.ascontiguousarray(np.concatenate([ks, ks], 1).T)

    Cq, Sq, Ck, Sk = _rotary_tables(S)
    sh["rot_cq"], sh["rot_sq"], sh["rot_ck"], sh["rot_sk"] = Cq, Sq, Ck, Sk
    sh["m0"], sh["m1"], sh["m0w"], sh["m1w"], sh["mbinA"] = _masks()
    sh["id_bf"] = np.ascontiguousarray(np.eye(128, dtype=bfd))
    sh["id_f32"] = np.ascontiguousarray(np.eye(128, dtype=np.float32))
    sh["ones_1"] = np.full((128, 128), 1.0, bfd)
    sh["ones_gn"] = np.full((128, 128), 1.0 / 128.0, bfd)
    sh["ones_ln"] = np.full((128, 128), 1.0 / 1024.0, bfd)
    bd = np.zeros((128, 128), f32)
    bd[0:64, 0:64] = 1.0; bd[64:128, 64:128] = 1.0
    sh["bdiag64"] = np.ascontiguousarray(bd.astype(bfd))
    pm = np.zeros((128, 128), f32)
    for p in range(128):
        base = (p // 64) * 64
        dl = p % 64
        sgn = -1.0 if dl < 32 else 1.0
        sig = base + (dl + 32) % 64
        pm[sig, p] = sgn
    sh["rotperm"] = np.ascontiguousarray(pm)
    selA = np.zeros((NH_A, 128, 128), f32)
    selV = np.zeros((NH_A, 128, 128), f32)
    for h in range(NH_A):
        selA[h, h, :] = 1; selA[h, 32 + h, :] = 1
        selV[h, 64 + h, :] = 1; selV[h, 96 + h, :] = 1
    sh["selA"] = np.ascontiguousarray(selA.astype(bfd))
    sh["selV"] = np.ascontiguousarray(selV.astype(bfd))
    return sh, per_core


class Builder:
    def __init__(self, S, sublayers=("attn_a", "inner1", "attn_b", "inner2"),
                 n_inner=DEPTH_IN):
        self.S = S
        self.NS = S // 512
        self.NW = S // WIN
        self.NT = S // 128
        self.sublayers = sublayers
        self.n_inner = n_inner
        self.nc = bass.Bass("TRN2", target_bir_lowering=False)
        self.ctx = Ctx(self.nc)
        self._name_seq = 0
        self.feeds = {}
        self._build()

    def din(self, name, shape, dtype=F32):
        return self.nc.dram_tensor(name, list(shape), dtype, kind="ExternalInput").ap()

    def sb(self, name, shape, dtype, stack=None):
        stk = stack if stack is not None else self.ctx.es
        self._name_seq += 1
        return stk.enter_context(
            self.nc.sbuf_tensor(f"{name}_{self._name_seq}", list(shape), dtype))

    def ld_const(self, key, shape, dtype, stk=None):
        self._name_seq += 1
        dname = f"k_{key}_{self._name_seq}"
        self.feeds[dname] = key
        da = self.din(dname, shape, dtype)
        self.ctx.wait("sync", self.sx, self.sx.n)
        self.ctx.wait("sync", self.sxo, self.sxo.n)
        if len(shape) == 3:
            t = self.sb("c_" + key, [shape[1], shape[0] * shape[2]], dtype, stk)
            ap = da[:, :, :].rearrange("a b c -> b a c")
            dst = t[:, :].rearrange("b (a c) -> b a c", a=shape[0])
        else:
            t = self.sb("c_" + key, shape, dtype, stk)
            ap = da[:, :]
            dst = t[:, :]
        v = self.ctx.inc("sync", partial(i_dma, out=dst, src=ap), self.sd, 16)
        self._const_marks.append((self.sd, v))
        return t

    def wait_consts(self, eng):
        self.ctx.wait(eng, self.sd, self.sd.n)

    def _build(self):
        S, NS, ctx = self.S, self.NS, self.ctx
        self.gd = ctx.sem("gd")
        self.sd = ctx.sem("sd")
        self.sx = ctx.sem("sx")
        self.sxo = ctx.sem("sxo")
        self.ts = ctx.sem("ts")
        self.vs = ctx.sem("vs")
        self.ss = ctx.sem("ss")
        sd = self.sx
        self.pp = PsumPool(ctx)
        self._const_marks = []
        self._cb_pending = True

        xin = self.din("xin", [D, S])
        yout = self.nc.dram_tensor("yout", [D, S], F32, kind="ExternalOutput").ap()
        cvals = [1e-5, 1e-6, LNC_A, 1.0 / 15.0, 15.0, 0.5, 0.1, 2.0 / 3.0, 0.15, 8.0]
        cb = self.sb("cbias", [128, len(cvals)], F32)
        for i, val in enumerate(cvals):
            vv = ctx.inc("vector", partial(i_memset, ap=cb[:, i:i + 1], c=float(val)),
                         self.vs if hasattr(self, 'vs') else None, 1) if False else None
        self._cb = cb
        vlast = 0
        for i, val in enumerate(cvals):
            vlast = ctx.inc("vector", partial(i_memset, ap=cb[:, i:i + 1], c=float(val)),
                            self.vs, 1)
            self.nc.const_aps.register(F32, float(val), cb[:, i:i + 1])
        for e in ("tensor", "scalar", "gpsimd", "sync"):
            ctx.wait(e, self.vs, vlast)
        xres = self.nc.dram_tensor("xres", [D, S], F32, kind="Internal").ap()
        self.xres = xres
        self.xres_w = []

        self.w = {}
        for n, shp in [("a_wqT", [D, D]), ("a_wkT", [D, 128]), ("a_wvT", [D, 128]),
                       ("a_wogT", [D, D]), ("a_woutT", [D, D]), ("a_wifT", [D, 16]),
                       ("b_wqT", [D, D]), ("b_wkT", [D, D]), ("b_wvT", [D, D]),
                       ("b_woT", [D, D]),
                       ("f_wqkv", [DEPTH_IN, D, 3072]), ("f_wo", [DEPTH_IN, D, D]),
                       ("f_w1", [DEPTH_IN, D, 2 * DIP]), ("f_w2", [DEPTH_IN, DIP, D])]:
            self.w[n] = self.din(n, shp)

        self.cst = {}
        for n, shp, dt in [("a_bif", [128, 1], F32), ("a_lnw", [128, 8], F32),
                           ("b_bq", [128, 8], F32), ("b_bk", [128, 8], F32),
                           ("b_bv", [128, 8], F32), ("b_bo", [128, 8], F32),
                           ("f_b1a", [128, DEPTH_IN * NJ], F32),
                           ("f_b1g", [128, DEPTH_IN * NJ], F32),
                           ("f_b2", [128, DEPTH_IN * 8], F32),
                           ("f_g", [128, DEPTH_IN * 8], F32),
                           ("f_bln", [128, DEPTH_IN * 8], F32),
                           ("f_qs", [128, DEPTH_IN], F32),
                           ("f_ks", [128, DEPTH_IN], F32),
                           ("id_bf", [128, 128], BF16), ("ones_1", [128, 128], BF16),
                           ("ones_gn", [128, 128], BF16), ("ones_ln", [128, 128], BF16)]:
            self.cst[n] = self.ld_const(n, shp, dt)

        self.XB = self.sb("XB", [128, NCH * S], BF16)
        self.ws_a = WStream(ctx, self.gd, "ws_a", [128, 1024], slots=4)
        self.ws_b = WStream(ctx, self.gd, "ws_b", [128, 1024], slots=4)
        self.ws_w2 = None
        self.ws_s = WStream(ctx, self.gd, "ws_s", [128, 128], slots=2)

        with ExitStack() as stk:
            Xf = self.sb("Xf", [128, NCH * S], F32, stk)
            v = ctx.inc("sync", partial(
                i_dma, out=Xf[:, :], src=xin.rearrange("(c p) s -> p c s", p=128)), sd, 16)
            ctx.wait("sync", sd, v)
            ctx.inc("sync", partial(
                i_dma, out=xres.rearrange("(c p) s -> p c s", p=128), src=Xf[:, :]), sd, 16)
            self._xspill = sd.n
            self.xres_w = [(sd, sd.n)]
            self.ln_stream(Xf, self.XB, None, 1e-5, wait=[(sd, v)])
            ctx.flush()
        if "attn_a" in self.sublayers:
            self.attn_a(xres)
        ctx.flush()

        if "inner1" in self.sublayers:
            self.seed_Z(xres)
            self.inner_pass()
            self.residual_from_hbm(xres)
        if "attn_b" in self.sublayers:
            self.outer_ln_to_bf(xres)
            self.attn_b(xres)
        if "inner2" in self.sublayers:
            self.seed_Z(xres)
            self.inner_pass()
            self.residual_from_hbm(xres)
        ctx.flush()

        with ExitStack() as stk:
            Of = self.sb("Of", [128, NCH * S], F32, stk)
            ctx.wait_marks("sync", self.xres_w)
            v = ctx.inc("sync", partial(
                i_dma, out=Of[:, :], src=xres.rearrange("(c p) s -> p c s", p=128)), sd, 16)
            ctx.wait("sync", sd, v)
            vo = ctx.inc("sync", partial(
                i_dma, out=yout.rearrange("(c p) s -> p c s", p=128), src=Of[:, :]), sd, 16)
            ctx.wait("sync", sd, vo)
            ctx.flush()
        ctx.es.close()

    def ln_stream(self, Xf, XNout, z_out, eps, wait=()):
        S, NS, ctx, pp = self.S, self.NS, self.ctx, self.pp
        ts, vs, ss = self.ts, self.vs, self.ss
        cst = self.cst
        with ExitStack() as stk:
            xbt = [self.sb(f"lnxb{i}", [128, 512], BF16, stk) for i in range(2)]
            sqb = [self.sb(f"lnsq{i}", [128, 512], BF16, stk) for i in range(2)]
            apl = self.sb("lnapl", [128, S], F32, stk)
            A = self.sb("lnA", [128, S], BF16, stk)
            B = self.sb("lnB", [128, S], BF16, stk)
            sc = self.sb("lnsc", [128, 512], F32, stk)
            self.wait_consts("vector")
            self.wait_consts("tensor")
            prev_chain = 0
            cast_mark = [0, 0]
            sq_mark = [0, 0]
            for st in range(NS):
                sl = slice(st * 512, st * 512 + 512)
                mu = pp.get(); m2 = pp.get()
                tmu = tm2 = None
                for c in range(NCH):
                    par = c % 2
                    ctx.wait_marks("vector", wait)
                    ctx.wait("vector", ts, cast_mark[par])
                    ctx.dr("vector")
                    vc = ctx.inc("vector", partial(
                        i_copy, out=xbt[par][:, :], src=Xf[:, c * S:(c + 1) * S][:, sl]), vs)
                    ctx.dr("vector")
                    vsq = ctx.inc("vector", partial(
                        i_tt, out=sqb[par][:, :], a=xbt[par][:, :], b=xbt[par][:, :],
                        op=OP.mult), vs)
                    ctx.wait("tensor", vs, vsq)
                    fn = partial(i_mm, out=mu.t[:, :], lhsT=cst["ones_ln"][:, :],
                                 rhs=xbt[par][:, :],
                                 start=(c == 0), stop=(c == NCH - 1))
                    tmu = ctx.inc("tensor", fn, ts) if c == NCH - 1 else ctx.emit("tensor", fn)
                    fn = partial(i_mm, out=m2.t[:, :], lhsT=cst["ones_ln"][:, :],
                                 rhs=sqb[par][:, :],
                                 start=(c == 0), stop=(c == NCH - 1))
                    tm2 = ctx.inc("tensor", fn, ts)
                    cast_mark[par] = tm2
                ctx.wait("scalar", ts, tmu)
                ctx.wait("scalar", vs, prev_chain)
                s1 = ctx.inc("scalar", partial(i_act, out=sc[:, :], src=mu.t[:, :],
                                               func=AF.Square), ss)
                ctx.wait("vector", ss, s1)
                ctx.wait("vector", ts, tm2)
                ctx.dr("vector")
                v1 = ctx.inc("vector", partial(i_tt, out=sc[:, :], a=m2.t[:, :], b=sc[:, :],
                                               op=OP.subtract), vs)
                ctx.wait("scalar", vs, v1)
                s2 = ctx.inc("scalar", partial(i_act, out=sc[:, :], src=sc[:, :],
                                               func=AF.Sqrt, bias=float(eps)), ss)
                ctx.wait("vector", ss, s2)
                ctx.inc("vector", partial(i_recip, out=A[:, sl], src=sc[:, :]), vs)
                ctx.dr("vector")
                v3 = ctx.inc("vector", partial(i_stt, out=B[:, sl], a=mu.t[:, :], s=-1.0,
                                               b=A[:, sl], op0=OP.mult, op1=OP.mult), vs)
                prev_chain = v3
                pp.put(mu, vs, v3)
                pp.put(m2, vs, v1)
            ctx.dr("vector")
            for c in range(NCH):
                cs = slice(c * S, (c + 1) * S)
                dsts = []
                if XNout is not None:
                    dsts.append(XNout)
                if z_out is not None:
                    dsts.append(z_out)
                for dst in dsts:
                    ctx.dr("vector")
                    ctx.inc("vector", partial(i_tt, out=apl[:, :], a=Xf[:, cs], b=A[:, :],
                                              op=OP.mult), vs)
                    ctx.dr("vector")
                    ctx.inc("vector", partial(i_tt, out=dst[:, cs], a=apl[:, :], b=B[:, :],
                                              op=OP.add), vs)
            self._lnfin = vs.n
        ctx.flush()

    def seed_Z(self, xres):
        S, ctx, sd = self.S, self.ctx, self.sx
        self.Zstk = ExitStack()
        self.Z = self.sb("Z", [128, NCH * S], F32, self.Zstk)
        with ExitStack() as stk:
            Xf = self.sb("olnx", [128, NCH * S], F32, stk)
            ctx.wait_marks("sync", self.xres_w)
            v = ctx.inc("sync", partial(
                i_dma, out=Xf[:, :], src=xres.rearrange("(c p) s -> p c s", p=128)), sd, 16)
            self.ln_stream(Xf, None, self.Z, 1e-5, wait=[(sd, v)])

    def outer_ln_to_bf(self, xres):
        S, ctx, sd = self.S, self.ctx, self.sx
        with ExitStack() as stk:
            Xf = self.sb("olnx2", [128, NCH * S], F32, stk)
            ctx.wait_marks("sync", self.xres_w)
            v = ctx.inc("sync", partial(
                i_dma, out=Xf[:, :], src=xres.rearrange("(c p) s -> p c s", p=128)), sd, 16)
            ctx.wait("vector", self.ts, self.ts.n)
            self.ln_stream(Xf, self.XB, None, 1e-5, wait=[(sd, v)])

    def residual_from_hbm(self, xres):
        S, ctx, sd, vs = self.S, self.ctx, self.sx, self.vs
        with ExitStack() as stk:
            Xf = self.sb("resx", [128, NCH * S], F32, stk)
            ctx.wait_marks("sync", self.xres_w)
            v = ctx.inc("sync", partial(
                i_dma, out=Xf[:, :], src=xres.rearrange("(c p) s -> p c s", p=128)), sd, 16)
            ctx.wait("vector", sd, v)
            vv = ctx.inc("vector", partial(i_tt, out=Xf[:, :], a=Xf[:, :], b=self.Z[:, :],
                                           op=OP.add), vs)
            ctx.wait("sync", vs, vv)
            vw = ctx.inc("sync", partial(
                i_dma, out=xres.rearrange("(c p) s -> p c s", p=128), src=Xf[:, :]), sd, 16)
            ctx.wait("sync", sd, vw)
            self.xres_w = [(sd, vw)]
            ctx.flush()
        self.Zstk.close()

    def proj_echunk(self, wap, SRC, src_wait, evac, ncol=128, ws=None):
        ctx, pp, ts = self.ctx, self.pp, self.ts
        S, NS = self.S, self.NS
        ws = ws or self.ws_a
        slot = ws.load(wap, ncols=NCH * ncol)
        wt = ws.use(slot)
        for st in range(NS):
            sl = slice(st * 512, st * 512 + 512)
            pb = pp.get()
            ctx.wait("tensor", self.vs, src_wait)
            tv = None
            for c in range(NCH):
                fn = partial(i_mm, out=pb.t[0:ncol, :] if ncol < 128 else pb.t[:, :],
                             lhsT=wt[:, c * ncol:(c + 1) * ncol],
                             rhs=SRC[:, c * S:(c + 1) * S][:, sl],
                             start=(c == 0), stop=(c == NCH - 1))
                tv = ctx.inc("tensor", fn, ts) if c == NCH - 1 else ctx.emit("tensor", fn)
            evac(pb, st, tv)
        ws.done(slot, ts, ts.n)

    def attn_a(self, xres):
        S, NS, NT, ctx = self.S, self.NS, self.NT, self.ctx
        ts, vs, ss, sd = self.ts, self.vs, self.ss, self.sx
        pp, w, cst = self.pp, self.w, self.cst
        XN = self.XB
        with ExitStack() as stk:
            for n, shp, dt in [("mbinA", [128, 128], BF16), ("id_f32", [128, 128], F32),
                               ("selA", [NH_A, 128, 128], BF16),
                               ("selV", [NH_A, 128, 128], BF16)]:
                cst[n] = self.ld_const(n, shp, dt, stk)
            self.wait_consts("tensor")
            self.wait_consts("scalar")
            self.wait_consts("vector")
            KT = self.sb("aKT", [128, S], BF16, stk)
            VT = self.sb("aVT", [128, S], BF16, stk)
            vtok = self.sb("avtok", [128, S], BF16, stk)
            hl = self.sb("ahl", [128, S], BF16, stk)
            Xc = self.sb("aXc", [128, S], F32, stk)
            acol = self.sb("aacol", [128, NT * 8], F32, stk)
            gstk = ExitStack()
            lf = self.sb("alf", [8, S], F32, gstk)
            csum = self.sb("acs", [8, S], F32, gstk)
            cmx = self.sb("acmx", [8, S], F32, gstk)
            arow = self.sb("aarow", [8, S], F32, gstk)
            zrow = self.sb("azrow", [8, S], F32, gstk)
            vmrow = self.sb("avm", [8, S], F32, gstk)
            g16i = self.sb("agi", [8, S], F32, gstk)
            g16f = self.sb("agf", [8, S], F32, gstk)
            hi8 = self.sb("ahi8", [8, S], F32, gstk)
            lo8 = self.sb("alo8", [8, S], F32, gstk)
            ctx.wait("tensor", vs, self._lnfin)
            self.wait_consts("tensor")

            def mk_evac_plain(dst):
                def e(pb, st, tv):
                    ctx.wait("scalar", ts, tv)
                    sv = ctx.inc("scalar", partial(
                        i_act, out=dst[:, st * 512:(st + 1) * 512], src=pb.t[:, :],
                        func=AF.Copy), ss)
                    pp.put(pb, ss, sv)
                return e
            self.proj_echunk(w["a_wkT"].rearrange("(c p) n -> p c n", p=128),
                             XN, self._lnfin, mk_evac_plain(KT))
            self.proj_echunk(w["a_wvT"].rearrange("(c p) n -> p c n", p=128),
                             XN, self._lnfin, mk_evac_plain(VT))

            def evac_g(pb, st, tv):
                ctx.wait("scalar", ts, tv)
                self.wait_consts("scalar")
                ctx.inc("scalar", partial(
                    i_act, out=g16i[:, st * 512:(st + 1) * 512], src=pb.t[0:8, :],
                    func=AF.Tanh, bias=cst["a_bif"][0:8, 0:1], scale=1.0 / 15.0), ss)
                sv = ctx.inc("scalar", partial(
                    i_act, out=g16f[:, st * 512:(st + 1) * 512], src=pb.t[32:40, :],
                    func=AF.Tanh, bias=cst["a_bif"][32:40, 0:1], scale=1.0 / 15.0), ss)
                pp.put(pb, ss, sv)

            slot_g = self.ws_s.load(w["a_wifT"].rearrange("(c p) n -> p c n", p=128),
                                    ncols=NCH * 16)
            wt_g = self.ws_s.use(slot_g)
            for st in range(NS):
                sl = slice(st * 512, st * 512 + 512)
                pb = pp.get()
                ctx.wait("tensor", vs, self._lnfin)
                tv = None
                for c in range(NCH):
                    fni = partial(i_mm, out=pb.t[0:8, :], lhsT=wt_g[:, c * 16:c * 16 + 8],
                                  rhs=XN[:, c * S:(c + 1) * S][:, sl],
                                  start=(c == 0), stop=(c == NCH - 1))
                    ctx.emit("tensor", fni)
                for c in range(NCH):
                    fnf = partial(i_mm, out=pb.t[32:40, :], lhsT=wt_g[:, c * 16 + 8:c * 16 + 16],
                                  rhs=XN[:, c * S:(c + 1) * S][:, sl],
                                  start=(c == 0), stop=(c == NCH - 1))
                    tv = ctx.inc("tensor", fnf, ts) if c == NCH - 1 else ctx.emit("tensor", fnf)
                evac_g(pb, st, tv)
            self.ws_s.done(slot_g, ts, ts.n)

            vz = ctx.inc("vector", partial(i_memset, ap=zrow[:, :], c=0.0), vs)
            vz = ctx.inc("vector", partial(i_memset, ap=hl[:, :], c=0.0), vs)
            ctx.wait("scalar", ss, ss.n)
            ctx.inc("scalar", partial(i_act, out=lf[:, :], src=g16f[:, :],
                                      func=AF.Sigmoid, scale=15.0), ss)
            ctx.dr("scalar")
            s_ln = ctx.inc("scalar", partial(i_act, out=lf[:, :], src=lf[:, :], func=AF.Ln), ss)
            ctx.wait("vector", ss, s_ln)
            ctx.wait("vector", vs, vz)
            ctx.inc("vector", partial(i_scan, out=csum[:, :], d0=lf[:, :], d1=zrow[0:8, :],
                                      init=0.0, op0=OP.add, op1=OP.bypass), vs)
            ctx.dr("vector")
            v_a = ctx.inc("vector", partial(i_stt, out=arow[:, :], a=g16i[:, :], s=15.0,
                                            b=csum[:, :], op0=OP.mult, op1=OP.subtract), vs)
            ctx.dr("vector")
            ctx.inc("vector", partial(i_scan, out=cmx[:, :], d0=arow[:, :], d1=zrow[0:8, :],
                                      init=-1e30, op0=OP.max, op1=OP.bypass), vs)
            ctx.dr("vector")
            ctx.inc("vector", partial(i_ts, out=hi8[:, :], a=cmx[:, :], s1=-1.0, s2=None,
                                      op0=OP.mult, op1=None), vs)
            ctx.dr("vector")
            ctx.inc("vector", partial(i_copy, out=hl[0:8, :], src=hi8[:, :]), vs)
            ctx.inc("vector", partial(i_stt, out=lo8[:, :], a=cmx[:, :], s=-1.0,
                                      b=hi8[:, :], op0=OP.mult, op1=OP.subtract), vs)
            ctx.dr("vector")
            v_bc1 = ctx.inc("vector", partial(i_copy, out=hl[32:40, :], src=lo8[:, :]), vs)
            ctx.inc("vector", partial(i_tt, out=vmrow[:, :], a=csum[:, :], b=cmx[:, :],
                                      op=OP.add), vs)
            ctx.dr("vector")
            ctx.inc("vector", partial(i_ts, out=hi8[:, :], a=vmrow[:, :], s1=-1.0, s2=None,
                                      op0=OP.mult, op1=None), vs)
            ctx.dr("vector")
            ctx.inc("vector", partial(i_copy, out=hl[64:72, :], src=hi8[:, :]), vs)
            ctx.inc("vector", partial(i_stt, out=lo8[:, :], a=vmrow[:, :], s=-1.0,
                                      b=hi8[:, :], op0=OP.mult, op1=OP.subtract), vs)
            ctx.dr("vector")
            v_bc2 = ctx.inc("vector", partial(i_copy, out=hl[96:104, :], src=lo8[:, :]), vs)

            ctx.wait("tensor", vs, v_a)
            for t0 in range(0, NT, 8):
                hi = min(t0 + 8, NT)
                pb = pp.get()
                tv = None
                for tt in range(t0, hi):
                    fn = partial(i_tr, out=pb.t[:, (tt - t0) * 8:(tt - t0) * 8 + 8],
                                 src=arow[:, tt * 128:(tt + 1) * 128],
                                 iden=cst["id_f32"][0:8, 0:8])
                    tv = ctx.inc("tensor", fn, ts) if tt == hi - 1 else ctx.emit("tensor", fn)
                ctx.wait("scalar", ts, tv)
                sv = ctx.inc("scalar", partial(
                    i_act, out=acol[:, t0 * 8:hi * 8], src=pb.t[:, 0:(hi - t0) * 8],
                    func=AF.Copy, bias=LNC_A), ss)
                pp.put(pb, ss, sv)

            gstk.close()
            BC = self.sb("aBC", [128, S], F32, stk)
            EMV = self.sb("aEMV", [128, S], F32, stk)
            QH = self.sb("aQH", [128, S], BF16, stk)
            OGH = self.sb("aOGH", [128, S], BF16, stk)
            hout = self.sb("ahout", [128, NCH * S], BF16, stk)
            hsq = self.sb("ahsq", [128, S], BF16, stk)
            ctb = [self.sb(f"act_{i}", [128, 512], BF16, stk) for i in range(2)]
            exb = [self.sb(f"aex_{i}", [128, 512], F32, stk) for i in range(2)]
            scrD = self.sb("ascrD", [128, 512], F32, stk)
            scr = self.sb("ascr", [128, 512], F32, stk)
            scr2 = self.sb("ascr2", [128, 512], F32, stk)
            ctx.wait("vector", ts, ts.n)
            ctx.wait("scalar", ts, ts.n)
            ctx.wait("scalar", vs, vs.n)
            ctx.wait("tensor", vs, vs.n)
            ctx.wait("tensor", ss, ss.n)
            ctx.wait("vector", ss, ss.n)
            ctx.dr("vector")
            ctx.dr("scalar")
            ctx.wait("tensor", ss, ss.n)
            for j0 in range(0, NT, 8):
                hi = min(j0 + 8, NT)
                pb = pp.get(BF16)
                tv = None
                for j in range(j0, hi):
                    fn = partial(i_tr, out=pb.t[:, (j - j0) * 128:(j - j0 + 1) * 128],
                                 src=VT[:, j * 128:(j + 1) * 128], iden=cst["id_bf"][:, :])
                    tv = ctx.inc("tensor", fn, ts) if j == hi - 1 else ctx.emit("tensor", fn)
                ctx.wait("vector", ts, tv)
                vv = ctx.inc("vector", partial(
                    i_copy, out=vtok[:, j0 * 128:hi * 128], src=pb.t[:, 0:(hi - j0) * 128]), vs)
                pp.put(pb, vs, vv)

            ct_mark = [0, 0]
            ex_mark = [0, 0]
            scrD_mark = 0
            scr_s_mark = 0
            scr_v_mark = 0
            bc_mark = 0
            qh_mark = (0, 0)
            for h in range(NH_A):
                def evac_q(pb, st, tv, dst=QH):
                    ctx.wait("scalar", ts, tv)
                    ctx.wait("scalar", ts, qh_mark[0])
                    ctx.wait("scalar", vs, qh_mark[1])
                    sv = ctx.inc("scalar", partial(
                        i_act, out=dst[:, st * 512:(st + 1) * 512], src=pb.t[:, :],
                        func=AF.Copy), ss)
                    pp.put(pb, ss, sv)
                def evac_og(pb, st, tv, dst=OGH):
                    ctx.wait("scalar", ts, tv)
                    sv = ctx.inc("scalar", partial(
                        i_act, out=dst[:, st * 512:(st + 1) * 512], src=pb.t[:, :],
                        func=AF.Sigmoid), ss)
                    pp.put(pb, ss, sv)
                self.proj_echunk(
                    w["a_wqT"][:, h * 128:(h + 1) * 128].rearrange("(c p) n -> p c n", p=128),
                    XN, self._lnfin, evac_q)
                self.proj_echunk(
                    w["a_wogT"][:, h * 128:(h + 1) * 128].rearrange("(c p) n -> p c n", p=128),
                    XN, self._lnfin, evac_og, ws=self.ws_b)
                sqh = ss.n

                for st in range(NS):
                    sl = slice(st * 512, st * 512 + 512)
                    pb = pp.get()
                    ctx.wait("tensor", vs, v_bc2)
                    tv = ctx.inc("tensor", partial(
                        i_mm, out=pb.t[:, :], lhsT=cst["selA"][:, h * 128:(h + 1) * 128],
                        rhs=hl[:, sl], start=True, stop=True), ts)
                    ctx.wait("scalar", ts, tv)
                    ctx.wait("scalar", vs, bc_mark)
                    sv = ctx.inc("scalar", partial(
                        i_act, out=BC[:, sl], src=pb.t[:, :], func=AF.Copy), ss)
                    pp.put(pb, ss, sv)
                    pb = pp.get()
                    ctx.wait("tensor", vs, v_bc2)
                    tv = ctx.inc("tensor", partial(
                        i_mm, out=pb.t[:, :], lhsT=cst["selV"][:, h * 128:(h + 1) * 128],
                        rhs=hl[:, sl], start=True, stop=True), ts)
                    ctx.wait("scalar", ts, tv)
                    sv = ctx.inc("scalar", partial(
                        i_act, out=EMV[:, sl], src=pb.t[:, :], func=AF.Exp), ss)
                    pp.put(pb, ss, sv)
                ctx.dr("scalar")
                sbc = ss.n

                for sp in range(NS):
                    s0 = sp * 512
                    hacc = pp.get()
                    racc = pp.get()
                    ntau = 4 * (sp + 1)
                    tpv = None
                    for tau in range(ntau):
                        t0 = tau * 128
                        cols0 = max(s0, t0)
                        ncol = s0 + 512 - cols0
                        pad = cols0 - s0
                        diag = t0 >= s0
                        par = tau % 2
                        ctv = ctb[par]
                        exv = exb[par]
                        qk = pp.get()
                        ctx.wait("tensor", ss, sqh)
                        tv = ctx.inc("tensor", partial(
                            i_mm, out=qk.t[:, 0:ncol],
                            lhsT=KT[:, t0:t0 + 128],
                            rhs=QH[:, cols0:cols0 + ncol], start=True, stop=True), ts)
                        if diag:
                            ctx.wait("vector", ss, sbc)
                            ctx.wait("vector", ss, scrD_mark)
                            vv = ctx.inc("vector", partial(
                                i_ts, out=scrD[:, 0:ncol], a=BC[:, cols0:cols0 + ncol],
                                s1=acol[:, tau * 8 + h:tau * 8 + h + 1], s2=0.0,
                                op0=OP.add, op1=OP.min), vs)
                            ctx.wait("scalar", vs, vv)
                            ctx.wait("scalar", vs, ex_mark[par])
                            sv = ctx.inc("scalar", partial(
                                i_act, out=exv[:, 0:ncol], src=scrD[:, 0:ncol],
                                func=AF.Exp), ss)
                            scrD_mark = sv
                            ctx.wait("vector", ss, sv)
                            sv = ctx.inc("vector", partial(
                                i_tt, out=exv[:, 0:128], a=exv[:, 0:128],
                                b=cst["mbinA"][:, :], op=OP.mult), vs)
                            ctx.dr("vector")
                            exp_sem, exp_val = vs, sv
                        else:
                            ctx.wait("scalar", ss, sbc)
                            ctx.wait("scalar", vs, ex_mark[par])
                            sv = ctx.inc("scalar", partial(
                                i_act, out=exv[:, 0:ncol], src=BC[:, cols0:cols0 + ncol],
                                func=AF.Exp,
                                bias=acol[:, tau * 8 + h:tau * 8 + h + 1]), ss)
                            exp_sem, exp_val = ss, sv
                        ctx.wait("vector", ts, tv)
                        ctx.wait("vector", exp_sem, exp_val)
                        ctx.wait("vector", ts, ct_mark[par])
                        vv = ctx.inc("vector", partial(
                            i_tt, out=ctv[:, 0:ncol], a=qk.t[:, 0:ncol], b=exv[:, 0:ncol],
                            op=OP.mult), vs)
                        ex_mark[par] = vv
                        pp.put(qk, vs, vv)
                        ctx.wait("tensor", vs, vv)
                        fn1 = partial(i_mm, out=hacc.t[:, pad:pad + ncol],
                                      lhsT=vtok[:, t0:t0 + 128], rhs=ctv[:, 0:ncol],
                                      start=(tau == 0), stop=(tau == ntau - 1))
                        ctx.emit("tensor", fn1)
                        fn2 = partial(i_mm, out=racc.t[:, pad:pad + ncol],
                                      lhsT=cst["ones_1"][:, :], rhs=ctv[:, 0:ncol],
                                      start=(tau == 0), stop=(tau == ntau - 1))
                        if tau == ntau - 1:
                            tpv = ctx.inc("tensor", fn2, ts)
                        else:
                            ctx.emit("tensor", fn2)
                        ct_mark[par] = ts.n
                    ctx.wait("scalar", ts, tpv)
                    ctx.wait("scalar", vs, scr_v_mark)
                    s_abs = ctx.inc("scalar", partial(i_act, out=scr[:, :], src=racc.t[:, :],
                                                      func=AF.Abs), ss)
                    scr_s_mark = s_abs
                    ctx.wait("vector", ts, tpv)
                    ctx.wait("vector", ss, s_abs)
                    ctx.dr("vector")
                    ctx.inc("vector", partial(i_tt, out=scr[:, :], a=scr[:, :],
                                              b=EMV[:, s0:s0 + 512], op=OP.max), vs)
                    ctx.dr("vector")
                    ctx.inc("vector", partial(i_ts, out=scr[:, :], a=scr[:, :], s1=1e-6,
                                              s2=None, op0=OP.add, op1=None), vs)
                    ctx.dr("vector")
                    ctx.inc("vector", partial(i_recip, out=scr[:, :], src=scr[:, :]), vs)
                    ctx.dr("vector")
                    vv = ctx.inc("vector", partial(
                        i_tt, out=hout[:, h * S + s0:h * S + s0 + 512], a=hacc.t[:, :],
                        b=scr[:, :], op=OP.mult), vs)
                    bc_mark = vv
                    scr_v_mark = vv
                    pp.put(hacc, vs, vv)
                    pp.put(racc, vs, vv)

                hb0 = h * S
                ctx.wait("vector", ts, qh_mark[0])
                ctx.dr("vector")
                vv = ctx.inc("vector", partial(i_tt, out=hsq[:, :], a=hout[:, hb0:hb0 + S],
                                               b=hout[:, hb0:hb0 + S], op=OP.mult), vs)
                gn_prev = bc_mark
                for st in range(NS):
                    sl = slice(st * 512, st * 512 + 512)
                    gl = slice(hb0 + st * 512, hb0 + st * 512 + 512)
                    mu = pp.get(); m2 = pp.get()
                    ctx.wait("tensor", vs, vv)
                    t1 = ctx.inc("tensor", partial(
                        i_mm, out=mu.t[:, :], lhsT=cst["ones_gn"][:, :], rhs=hout[:, gl],
                        start=True, stop=True), ts)
                    t2 = ctx.inc("tensor", partial(
                        i_mm, out=m2.t[:, :], lhsT=cst["ones_gn"][:, :], rhs=hsq[:, sl],
                        start=True, stop=True), ts)
                    ctx.wait("scalar", ts, t1)
                    ctx.wait("scalar", vs, gn_prev)
                    s1 = ctx.inc("scalar", partial(i_act, out=scr[:, :], src=mu.t[:, :],
                                                   func=AF.Square), ss)
                    ctx.wait("vector", ss, s1)
                    ctx.wait("vector", ts, t2)
                    v1 = ctx.inc("vector", partial(i_tt, out=scr[:, :], a=m2.t[:, :],
                                                   b=scr[:, :], op=OP.subtract), vs)
                    ctx.wait("scalar", vs, v1)
                    s2 = ctx.inc("scalar", partial(i_act, out=scr[:, :], src=scr[:, :],
                                                   func=AF.Sqrt, bias=1e-6), ss)
                    scr_s_mark = s2
                    ctx.wait("vector", ss, s2)
                    ctx.inc("vector", partial(i_recip, out=scr[:, :], src=scr[:, :]), vs)
                    v2 = ctx.inc("vector", partial(i_tt, out=scr2[:, :], a=hout[:, gl],
                                                   b=mu.t[:, :], op=OP.subtract), vs)
                    pp.put(mu, vs, v2)
                    pp.put(m2, vs, v1)
                    ctx.dr("vector")
                    ctx.inc("vector", partial(i_tt, out=scr2[:, :], a=scr2[:, :], b=scr[:, :],
                                              op=OP.mult), vs)
                    ctx.dr("vector")
                    gn_prev = ctx.inc("vector", partial(
                        i_stt, out=hout[:, gl], a=scr2[:, :],
                        s=cst["a_lnw"][:, h:h + 1], b=OGH[:, sl],
                        op0=OP.mult, op1=OP.mult), vs)
                    scr_v_mark = gn_prev
                qh_mark = (ts.n, gn_prev)
            vhout = vs.n
            sxo = self.sxo
            for e in range(NCH):
                ctx.wait_marks("sync", self.xres_w)
                ctx.wait("sync", sxo, sxo.n)
                vx = ctx.inc("sync", partial(
                    i_dma, out=Xc[:, :],
                    src=xres.rearrange("(c p) s -> p c s", p=128)[:, e, :]), sd, 16)

                def evac_res(pb, st, tv, e=e, vx=vx):
                    ctx.wait("vector", ts, tv)
                    ctx.wait("vector", sd, vx)
                    ctx.dr("vector")
                    vv = ctx.inc("vector", partial(
                        i_stt, out=Xc[:, st * 512:(st + 1) * 512],
                        a=pb.t[:, :], s=0.0,
                        b=Xc[:, st * 512:(st + 1) * 512],
                        op0=OP.add, op1=OP.add), vs)
                    pp.put(pb, vs, vv)
                self.proj_echunk(
                    w["a_woutT"][:, e * 128:(e + 1) * 128].rearrange("(c p) n -> p c n", p=128),
                    hout, vhout, evac_res, ws=self.ws_b)
                ctx.wait("sync", vs, vs.n)
                vw = ctx.inc("sync", partial(
                    i_dma,
                    out=xres.rearrange("(c p) s -> p c s", p=128)[:, e, :],
                    src=Xc[:, :]), sxo, 16)
                if e == NCH - 1:
                    self.xres_w = [(sxo, vw)]
        ctx.flush()

    def inner_pass(self):
        for i in range(self.n_inner):
            if "iattn" not in DBG_SKIP:
                self.inner_attn(i)
            if "iffn" not in DBG_SKIP:
                self.inner_ffn(i)

    def shift_cast(self, Z, XS):
        S, ctx, vs = self.S, self.ctx, self.vs
        ctx.wait("vector", self.ts, self.ts.n)
        ctx.wait("vector", self.ss, self.ss.n)
        ctx.dr("vector")
        for c in range(NCH):
            if c < 4:
                ctx.inc("vector", partial(i_copy, out=XS[:, c * S:(c + 1) * S],
                                          src=Z[:, c * S:(c + 1) * S]), vs)
            else:
                ctx.inc("vector", partial(i_memset, ap=XS[:, c * S:c * S + 1], c=0.0), vs)
                ctx.inc("vector", partial(i_copy, out=XS[:, c * S + 1:(c + 1) * S],
                                          src=Z[:, c * S:(c + 1) * S - 1]), vs)
        return vs.n

    def inner_attn(self, li):
        S, NS, NT, NW, ctx = self.S, self.NS, self.NT, self.NW, self.ctx
        ts, vs, ss = self.ts, self.vs, self.ss
        pp, w, cst = self.pp, self.w, self.cst
        Z, XS = self.Z, self.XB
        vshift = self.shift_cast(Z, XS)

        with ExitStack() as stk:
            for n, shp in [("rot_cq", [128, S]), ("rot_sq", [128, S]),
                           ("rot_ck", [128, S]), ("rot_sk", [128, S]),
                           ("m0", [128, 512]), ("m1", [128, 512]),
                           ("m0w", [128, 256]), ("m1w", [128, 256]),
                           ("bdiag64", [128, 128])]:
                cst[n] = self.ld_const(n, shp, BF16, stk)
            cst["rotperm"] = self.ld_const("rotperm", [128, 128], F32, stk)
            self.wait_consts("tensor")
            self.wait_consts("scalar")
            self.wait_consts("vector")
            ATT = self.sb("iATT", [128, NCH * S], BF16, stk)
            uq = self.sb("iuq", [128, S], BF16, stk)
            uk = self.sb("iuk", [128, S], BF16, stk)
            ufb = [self.sb(f"iuf{i}", [128, 512], F32, stk) for i in range(2)]
            sqb = [self.sb(f"isq{i}", [128, 512], BF16, stk) for i in range(2)]
            rA = self.sb("irA", [128, 512], F32, stk)
            rB = self.sb("irB", [128, 512], F32, stk)
            tb = self.sb("itb", [128, 512], F32, stk)
            t2b = self.sb("it2b", [128, 512], F32, stk)
            sbf2 = self.sb("isbf2", [128, S], BF16, stk)
            vtok = self.sb("ivtok", [128, S], BF16, stk)
            Pb = [self.sb(f"iP{i}", [128, 512], BF16, stk) for i in range(2)]
            PTs = self.sb("iPTs", [128, 1024], BF16, stk)
            sums = self.sb("isums", [128, 2], F32, stk)
            rc = self.sb("irc", [128, 2], F32, stk)
            qs_col = cst["f_qs"][:, li:li + 1]
            ks_col = cst["f_ks"][:, li:li + 1]

            P_mark = [0, 0]
            att_done = 0
            chunk_ts = 0
            chunk_vs = 0
            uf_mark = [0, 0]
            sqm = [0, 0]
            r_mark = [0, 0]
            for c in range(NCH):
                wqkv = w["f_wqkv"][li]

                def qk_side(col0, dst, scol, Ct, St):
                    slot = self.ws_a.load(
                        wqkv[:, col0:col0 + 128].rearrange("(c2 p) n -> p c2 n", p=128))
                    wt = self.ws_a.use(slot)
                    for st in range(NS):
                        sl = slice(st * 512, st * 512 + 512)
                        par = st % 2
                        uf = ufb[par]
                        sq = sqb[par]
                        pb = pp.get()
                        ctx.wait("tensor", vs, vshift)
                        tv = None
                        for cc in range(NCH):
                            fn = partial(i_mm, out=pb.t[:, :],
                                         lhsT=wt[:, cc * 128:(cc + 1) * 128],
                                         rhs=XS[:, cc * S:(cc + 1) * S][:, sl],
                                         start=(cc == 0), stop=(cc == NCH - 1))
                            tv = ctx.inc("tensor", fn, ts) if cc == NCH - 1 \
                                else ctx.emit("tensor", fn)
                        ctx.wait("scalar", ts, tv)
                        ctx.wait("scalar", vs, uf_mark[par])
                        sv = ctx.inc("scalar", partial(
                            i_act, out=uf[:, :], src=pb.t[:, :], func=AF.Identity,
                            scale=scol), ss)
                        ctx.wait("scalar", ts, tv)
                        ctx.wait("scalar", ts, sqm[par])
                        ctx.dr("scalar")
                        v_sq = ctx.inc("scalar", partial(
                            i_act, out=sq[:, :], src=pb.t[:, :], func=AF.Square), ss)
                        prm = pp.get()
                        ctx.wait("tensor", ss, sv)
                        tpm = ctx.inc("tensor", partial(
                            i_mm, out=prm.t[:, :], lhsT=cst["rotperm"][:, :], rhs=uf[:, :],
                            start=True, stop=True), ts)
                        ssp = pp.get()
                        ctx.wait("tensor", ss, v_sq)
                        tss = ctx.inc("tensor", partial(
                            i_mm, out=ssp.t[:, :], lhsT=cst["bdiag64"][:, :], rhs=sq[:, :],
                            start=True, stop=True), ts)
                        sqm[par] = tss
                        pp.put(pb, ss, v_sq)
                        pb.free.append((ss, sv))
                        ctx.wait("vector", ts, tss)
                        ctx.wait("vector", ss, r_mark[0])
                        v1 = ctx.inc("vector", partial(
                            i_ts, out=rA[:, :], a=ssp.t[:, :], s1=1e-24, s2=None,
                            op0=OP.max, op1=None), vs)
                        pp.put(ssp, vs, v1)
                        ctx.wait("scalar", vs, v1)
                        s_sq = ctx.inc("scalar", partial(
                            i_act, out=rB[:, :], src=rA[:, :], func=AF.Sqrt), ss)
                        r_mark[0] = s_sq
                        ctx.wait("vector", ss, s_sq)
                        ctx.dr("vector")
                        v2 = ctx.inc("vector", partial(i_recip, out=rA[:, :], src=rB[:, :]), vs)
                        ctx.wait("vector", ss, sv)
                        ctx.dr("vector")
                        ctx.inc("vector", partial(
                            i_tt, out=tb[:, :], a=uf[:, :], b=Ct[:, sl], op=OP.mult), vs)
                        ctx.wait("vector", ts, tpm)
                        ctx.inc("vector", partial(
                            i_tt, out=t2b[:, :], a=prm.t[:, :], b=St[:, sl], op=OP.mult), vs)
                        ctx.dr("vector")
                        v3 = ctx.inc("vector", partial(
                            i_tt, out=tb[:, :], a=tb[:, :], b=t2b[:, :], op=OP.add), vs)
                        pp.put(prm, vs, v3)
                        ctx.dr("vector")
                        v4 = ctx.inc("vector", partial(
                            i_tt, out=dst[:, sl], a=tb[:, :], b=rA[:, :], op=OP.mult), vs)
                        uf_mark[par] = v4
                    self.ws_a.done(slot, ts, ts.n)
                    return vs.n

                ctx.wait("scalar", ts, chunk_ts)
                ctx.wait("scalar", vs, chunk_vs)
                vrq = qk_side(c * 128, uq, qs_col, cst["rot_cq"], cst["rot_sq"])
                vrk = qk_side(1024 + c * 128, uk, ks_col, cst["rot_ck"], cst["rot_sk"])

                def evac_v(pb, st, tv):
                    sl = slice(st * 512, st * 512 + 512)
                    ctx.wait("scalar", ts, tv)
                    sv = ctx.inc("scalar", partial(i_act, out=sbf2[:, sl], src=pb.t[:, :],
                                                   func=AF.Copy), ss)
                    pp.put(pb, ss, sv)
                self.proj_echunk(
                    wqkv[:, 2048 + c * 128:2048 + (c + 1) * 128]
                    .rearrange("(c2 p) n -> p c2 n", p=128),
                    XS, vshift, evac_v)
                sv_v = ss.n

                ctx.wait("tensor", ss, sv_v)
                for j0 in range(0, NT, 8):
                    ctx.wait("vector", ts, chunk_ts)
                    hi = min(j0 + 8, NT)
                    pb = pp.get(BF16)
                    tv = None
                    for j in range(j0, hi):
                        fn = partial(i_tr, out=pb.t[:, (j - j0) * 128:(j - j0 + 1) * 128],
                                     src=sbf2[:, j * 128:(j + 1) * 128], iden=cst["id_bf"][:, :])
                        tv = ctx.inc("tensor", fn, ts) if j == hi - 1 else ctx.emit("tensor", fn)
                    ctx.wait("vector", ts, tv)
                    vv = ctx.inc("vector", partial(
                        i_copy, out=vtok[:, j0 * 128:hi * 128], src=pb.t[:, 0:(hi - j0) * 128]), vs)
                    pp.put(pb, vs, vv)
                t_vtok = ts.n

                for hp in range(2):
                    hb = hp * 64
                    for wdw in range(NW):
                        nj = 256 if wdw == 0 else 512
                        jg0 = 0 if wdw == 0 else (wdw - 1) * 256
                        ptp = pp.get(BF16)
                        pvp = pp.get()
                        t_tr = None
                        for it in range(2):
                            i0 = wdw * 256 + it * 128
                            par = it
                            P = Pb[par]
                            scp = pp.get()
                            ctx.wait("tensor", vs, vrk)
                            ctx.emit("tensor", partial(
                                i_mm, out=scp.t[:, 0:nj],
                                lhsT=uq[hb:hb + 64, i0:i0 + 128],
                                rhs=uk[hb:hb + 64, jg0:jg0 + nj], start=True, stop=False))
                            mask = (cst["m0w"] if it == 0 else cst["m1w"]) if wdw == 0 else \
                                   (cst["m0"] if it == 0 else cst["m1"])
                            tv = ctx.inc("tensor", partial(
                                i_mm, out=scp.t[:, 0:nj], lhsT=cst["id_bf"][:, :],
                                rhs=mask[:, 0:nj], start=False, stop=True), ts)
                            ctx.wait("scalar", ts, tv)
                            ctx.wait("scalar", ts, P_mark[par])
                            sv = ctx.inc("scalar", partial(
                                i_act, out=P[:, 0:nj], src=scp.t[:, 0:nj], func=AF.Exp,
                                scale=8.0, accum=sums[:, it:it + 1]), ss)
                            pp.put(scp, ss, sv)
                            ctx.wait("vector", ss, sv)
                            ctx.inc("vector", partial(i_recip, out=rc[:, it:it + 1],
                                                      src=sums[:, it:it + 1]), vs)
                            ctx.dr("vector")
                            vn = ctx.inc("vector", partial(
                                i_ts, out=P[:, 0:nj], a=P[:, 0:nj],
                                s1=rc[:, it:it + 1], s2=None, op0=OP.mult, op1=None), vs)
                            ctx.wait("tensor", vs, vn)
                            for jc in range(nj // 128):
                                fn = partial(i_tr,
                                             out=ptp.t[:, jc * 256 + it * 128:
                                                       jc * 256 + it * 128 + 128],
                                             src=P[:, jc * 128:(jc + 1) * 128],
                                             iden=cst["id_bf"][:, :])
                                t_tr = ctx.inc("tensor", fn, ts) if jc == nj // 128 - 1 \
                                    else ctx.emit("tensor", fn)
                            P_mark[par] = t_tr
                        ctx.wait("vector", ts, t_tr)
                        vpt = ctx.inc("vector", partial(
                            i_copy, out=PTs[:, 0:2 * nj], src=ptp.t[:, 0:2 * nj]), vs)
                        pp.put(ptp, vs, vpt)
                        njc = nj // 128
                        tpv = None
                        ctx.wait("tensor", vs, vpt)
                        for jc in range(njc):
                            jt = jg0 // 128 + jc
                            fn = partial(i_mm, out=pvp.t[hb:hb + 64, 0:256],
                                         lhsT=vtok[:, jt * 128 + hb:jt * 128 + hb + 64],
                                         rhs=PTs[:, jc * 256:jc * 256 + 256],
                                         start=(jc == 0), stop=(jc == njc - 1))
                            tpv = ctx.inc("tensor", fn, ts) if jc == njc - 1 \
                                else ctx.emit("tensor", fn)
                        ctx.wait("scalar", ts, tpv)
                        sv = ctx.inc("scalar", partial(
                            i_act,
                            out=ATT[:, c * S:(c + 1) * S][hb:hb + 64,
                                                          wdw * 256:(wdw + 1) * 256],
                            src=pvp.t[hb:hb + 64, 0:256], func=AF.Copy), ss)
                        pp.put(pvp, ss, sv)
                        att_done = sv
                chunk_ts = ts.n
                chunk_vs = vs.n
            for e in range(NCH):
                def evac_resz(pb, st, tv, e=e):
                    ctx.wait("vector", ts, tv)
                    vv = ctx.inc("vector", partial(
                        i_stt,
                        out=Z[:, e * self.S:(e + 1) * self.S][:, st * 512:(st + 1) * 512],
                        a=pb.t[:, :], s=0.0,
                        b=Z[:, e * self.S:(e + 1) * self.S][:, st * 512:(st + 1) * 512],
                        op0=OP.add, op1=OP.add), vs)
                    pp.put(pb, vs, vv)
                ctx.wait("tensor", ss, att_done)
                self.proj_echunk(
                    w["f_wo"][li][:, e * 128:(e + 1) * 128].rearrange("(c p) n -> p c n", p=128),
                    ATT, 0, evac_resz, ws=self.ws_b)
        ctx.flush()

    def inner_ffn(self, li):
        S, NS, ctx = self.S, self.NS, self.ctx
        ts, vs, ss = self.ts, self.vs, self.ss
        pp, w, cst = self.pp, self.w, self.cst
        Z, XS = self.Z, self.XB
        SH = min(S, 1024)
        nh = S // SH
        vshift = self.shift_cast(Z, XS)

        with ExitStack() as stk:
            ACT = self.sb("fACT", [128, NJ * SH], BF16, stk)
            Xq = self.sb("fXq", [128, NCH * SH], BF16, stk)
            A = self.sb("fA", [128, SH], BF16, stk)
            Bt = self.sb("fB", [128, SH], BF16, stk)
            asb = [self.sb(f"fasb{i}", [128, SH], BF16, stk) for i in range(2)]
            gsb = [self.sb(f"fgsb{i}", [128, SH], BF16, stk) for i in range(2)]
            sc = self.sb("fsc", [128, 512], F32, stk)
            ws_w2 = WStream(ctx, self.gd, f"wsw2_{li}_{ctx.n_stages}", [128, NJ * 128],
                            slots=2, stack=stk)
            g_col = cst["f_g"][:, li * 8:(li + 1) * 8]
            bln_col = cst["f_bln"][:, li * 8:(li + 1) * 8]
            b1a = cst["f_b1a"][:, li * NJ:(li + 1) * NJ]
            b1g = cst["f_b1g"][:, li * NJ:(li + 1) * NJ]
            b2c = cst["f_b2"][:, li * 8:(li + 1) * 8]

            ag_mark = [0, 0]
            half_ts = 0
            sc_s_mark = 0
            for hf in range(nh):
                h0 = hf * SH
                ctx.wait("vector", vs, vshift)
                ctx.wait("vector", ts, half_ts)
                for c in range(NCH):
                    ctx.inc("vector", partial(
                        i_tt, out=Xq[:, c * SH:(c + 1) * SH],
                        a=XS[:, c * S + h0:c * S + h0 + SH],
                        b=XS[:, c * S + h0:c * S + h0 + SH], op=OP.mult), vs)
                vsq = vs.n
                chain_prev = 0
                t_stats = None
                for st in range(SH // 512):
                    sl = slice(st * 512, st * 512 + 512)
                    gl = slice(h0 + st * 512, h0 + st * 512 + 512)
                    mu = pp.get(); m2 = pp.get()
                    ctx.wait("tensor", vs, vsq)
                    t1 = t2 = None
                    for c in range(NCH):
                        fn = partial(i_mm, out=mu.t[:, :], lhsT=cst["ones_ln"][:, :],
                                     rhs=XS[:, c * S:(c + 1) * S][:, gl],
                                     start=(c == 0), stop=(c == NCH - 1))
                        t1 = ctx.inc("tensor", fn, ts) if c == NCH - 1 else ctx.emit("tensor", fn)
                    for c in range(NCH):
                        fn = partial(i_mm, out=m2.t[:, :], lhsT=cst["ones_ln"][:, :],
                                     rhs=Xq[:, c * SH:(c + 1) * SH][:, sl],
                                     start=(c == 0), stop=(c == NCH - 1))
                        t2 = ctx.inc("tensor", fn, ts) if c == NCH - 1 else ctx.emit("tensor", fn)
                    t_stats = t2
                    ctx.wait("scalar", ts, t1)
                    ctx.wait("scalar", vs, chain_prev)
                    s1 = ctx.inc("scalar", partial(i_act, out=sc[:, :], src=mu.t[:, :],
                                                   func=AF.Square), ss)
                    ctx.wait("vector", ss, s1)
                    ctx.wait("vector", ts, t2)
                    v1 = ctx.inc("vector", partial(i_tt, out=sc[:, :], a=m2.t[:, :], b=sc[:, :],
                                                   op=OP.subtract), vs)
                    ctx.wait("scalar", vs, v1)
                    s2 = ctx.inc("scalar", partial(i_act, out=sc[:, :], src=sc[:, :],
                                                   func=AF.Sqrt, bias=1e-5), ss)
                    sc_s_mark = s2
                    ctx.wait("vector", ss, s2)
                    ctx.inc("vector", partial(i_recip, out=A[:, sl], src=sc[:, :]), vs)
                    ctx.dr("vector")
                    v3 = ctx.inc("vector", partial(i_stt, out=Bt[:, sl], a=mu.t[:, :], s=-1.0,
                                                   b=A[:, sl], op0=OP.mult, op1=OP.mult), vs)
                    chain_prev = v3
                    pp.put(mu, vs, v3)
                    pp.put(m2, vs, v1)
                ctx.wait("vector", ts, t_stats)
                ctx.dr("vector")
                for c in range(NCH):
                    gls = slice(c * S + h0, c * S + h0 + SH)
                    ctx.inc("vector", partial(i_tt, out=Xq[:, c * SH:(c + 1) * SH],
                                              a=XS[:, gls], b=A[:, :], op=OP.mult), vs)
                    ctx.dr("vector")
                    ctx.inc("vector", partial(i_tt, out=Xq[:, c * SH:(c + 1) * SH],
                                              a=Xq[:, c * SH:(c + 1) * SH], b=Bt[:, :],
                                              op=OP.add), vs)
                    ctx.dr("vector")
                    ctx.inc("vector", partial(i_ts, out=XS[:, gls], a=Xq[:, c * SH:(c + 1) * SH],
                                              s1=g_col[:, c:c + 1], s2=bln_col[:, c:c + 1],
                                              op0=OP.mult, op1=OP.add), vs)
                vln = vs.n

                for j in range(NJ):
                    par = j % 2
                    sa = self.ws_a.load(
                        w["f_w1"][li][:, j * 128:(j + 1) * 128]
                        .rearrange("(c p) n -> p c n", p=128))
                    sg = self.ws_b.load(
                        w["f_w1"][li][:, DIP + j * 128:DIP + (j + 1) * 128]
                        .rearrange("(c p) n -> p c n", p=128))
                    wta = self.ws_a.use(sa)
                    wtg = self.ws_b.use(sg)
                    for st in range(SH // 512):
                        gl = slice(h0 + st * 512, h0 + st * 512 + 512)
                        pa = pp.get(); pg = pp.get()
                        ctx.wait("tensor", vs, vln)
                        ta = tg = None
                        for cc in range(NCH):
                            fn = partial(i_mm, out=pa.t[:, :],
                                         lhsT=wta[:, cc * 128:(cc + 1) * 128],
                                         rhs=XS[:, cc * S:(cc + 1) * S][:, gl],
                                         start=(cc == 0), stop=(cc == NCH - 1))
                            ta = ctx.inc("tensor", fn, ts) if cc == NCH - 1 \
                                else ctx.emit("tensor", fn)
                        for cc in range(NCH):
                            fn = partial(i_mm, out=pg.t[:, :],
                                         lhsT=wtg[:, cc * 128:(cc + 1) * 128],
                                         rhs=XS[:, cc * S:(cc + 1) * S][:, gl],
                                         start=(cc == 0), stop=(cc == NCH - 1))
                            tg = ctx.inc("tensor", fn, ts) if cc == NCH - 1 \
                                else ctx.emit("tensor", fn)
                        ctx.wait("scalar", ts, ta)
                        ctx.wait("scalar", vs, ag_mark[par])
                        s1 = ctx.inc("scalar", partial(
                            i_act, out=asb[par][:, st * 512:(st + 1) * 512], src=pa.t[:, :],
                            func=AF.Identity, bias=b1a[:, j:j + 1]), ss)
                        pp.put(pa, ss, s1)
                        ctx.wait("scalar", ts, tg)
                        s2 = ctx.inc("scalar", partial(
                            i_act, out=gsb[par][:, st * 512:(st + 1) * 512], src=pg.t[:, :],
                            func=AF.Gelu, bias=b1g[:, j:j + 1]), ss)
                        pp.put(pg, ss, s2)
                        ctx.wait("vector", ss, s2)
                        vv = ctx.inc("vector", partial(
                            i_tt, out=ACT[:, j * SH + st * 512:j * SH + (st + 1) * 512],
                            a=asb[par][:, st * 512:(st + 1) * 512],
                            b=gsb[par][:, st * 512:(st + 1) * 512], op=OP.mult), vs)
                        ag_mark[par] = vv
                    self.ws_a.done(sa, ts, ts.n)
                    self.ws_b.done(sg, ts, ts.n)
                vact = vs.n

                for e in range(NCH):
                    slot = ws_w2.load(
                        w["f_w2"][li][:, e * 128:(e + 1) * 128]
                        .rearrange("(c p) n -> p c n", p=128))
                    wt = ws_w2.use(slot)
                    for st in range(SH // 512):
                        sl = slice(st * 512, st * 512 + 512)
                        pb = pp.get()
                        ctx.wait("tensor", vs, vact)
                        tv = None
                        for j in range(NJ):
                            fn = partial(i_mm, out=pb.t[:, :],
                                         lhsT=wt[:, j * 128:(j + 1) * 128],
                                         rhs=ACT[:, j * SH:(j + 1) * SH][:, sl],
                                         start=(j == 0), stop=(j == NJ - 1))
                            tv = ctx.inc("tensor", fn, ts) if j == NJ - 1 \
                                else ctx.emit("tensor", fn)
                        ctx.wait("vector", ts, tv)
                        ctx.wait("vector", ss, sc_s_mark)
                        ctx.dr("vector")
                        v1 = ctx.inc("vector", partial(
                            i_ts, out=sc[:, :], a=pb.t[:, :], s1=b2c[:, e:e + 1], s2=None,
                            op0=OP.add, op1=None), vs)
                        ctx.dr("vector")
                        ctx.inc("vector", partial(
                            i_tt, out=Z[:, e * S + h0:e * S + h0 + SH][:, sl], a=sc[:, :],
                            b=Z[:, e * S + h0:e * S + h0 + SH][:, sl], op=OP.add), vs)
                        pp.put(pb, vs, v1)
                    ws_w2.done(slot, ts, ts.n)
                half_ts = ts.n
        ctx.flush()

    def attn_b(self, xres):
        S, NS, NT, ctx = self.S, self.NS, self.NT, self.ctx
        ts, vs, ss, sd = self.ts, self.vs, self.ss, self.sx
        sxo = self.sxo
        pp, w, cst = self.pp, self.w, self.cst
        XN = self.XB
        OT = self.XB

        with ExitStack() as stk:
            QT = self.sb("bQT", [128, NCH * S], BF16, stk)
            KT = self.sb("bKT", [128, NCH * S], BF16, stk)
            vtok = self.sb("bvtok", [128, NCH * S], BF16, stk)
            TH = self.sb("bTH", [128, S], F32, stk)
            P = self.sb("bP", [128, S], BF16, stk)
            PT = self.sb("bPT", [128, 4 * S], BF16, stk)
            sc1 = self.sb("bsc1", [128, 512], F32, stk)
            sc2 = self.sb("bsc2", [128, 512], F32, stk)
            vtc = self.sb("bvtc", [128, S], BF16, stk)
            sums = self.sb("bsums", [128, 1], F32, stk)
            rc = self.sb("brc", [128, 1], F32, stk)
            Xc = self.sb("bXc", [128, S], F32, stk)

            ctx.wait("tensor", vs, self._lnfin)

            def mk_evac_b(dst, bcol, e):
                def ev(pb, st, tv):
                    sl = slice(st * 512, st * 512 + 512)
                    ctx.wait("scalar", ts, tv)
                    sv = ctx.inc("scalar", partial(
                        i_act, out=dst[:, e * S:(e + 1) * S][:, sl], src=pb.t[:, :],
                        func=AF.Identity, bias=bcol[:, e:e + 1]), ss)
                    pp.put(pb, ss, sv)
                return ev
            for e in range(NCH):
                self.proj_echunk(
                    w["b_wqT"][:, e * 128:(e + 1) * 128].rearrange("(c p) n -> p c n", p=128),
                    XN, self._lnfin, mk_evac_b(QT, cst["b_bq"], e))
            for e in range(NCH):
                self.proj_echunk(
                    w["b_wkT"][:, e * 128:(e + 1) * 128].rearrange("(c p) n -> p c n", p=128),
                    XN, self._lnfin, mk_evac_b(KT, cst["b_bk"], e))
            for e in range(NCH):
                def evv(pb, st, tv, e=e):
                    sl = slice(st * 512, st * 512 + 512)
                    ctx.wait("scalar", ts, tv)
                    sv = ctx.inc("scalar", partial(
                        i_act, out=vtc[:, sl], src=pb.t[:, :], func=AF.Identity,
                        bias=cst["b_bv"][:, e:e + 1]), ss)
                    pp.put(pb, ss, sv)
                self.proj_echunk(
                    w["b_wvT"][:, e * 128:(e + 1) * 128].rearrange("(c p) n -> p c n", p=128),
                    XN, self._lnfin, evv)
                sve = ss.n
                ctx.wait("tensor", ss, sve)
                for j0 in range(0, NT, 8):
                    hi = min(j0 + 8, NT)
                    pb = pp.get(BF16)
                    tv = None
                    for j in range(j0, hi):
                        fn = partial(i_tr, out=pb.t[:, (j - j0) * 128:(j - j0 + 1) * 128],
                                     src=vtc[:, j * 128:(j + 1) * 128], iden=cst["id_bf"][:, :])
                        tv = ctx.inc("tensor", fn, ts) if j == hi - 1 else ctx.emit("tensor", fn)
                    ctx.wait("vector", ts, tv)
                    vv = None
                    for j in range(j0, hi):
                        vv = ctx.inc("vector", partial(
                            i_copy,
                            out=vtok[:, j * 1024 + e * 128:j * 1024 + (e + 1) * 128],
                            src=pb.t[:, (j - j0) * 128:(j - j0 + 1) * 128]), vs)
                    pp.put(pb, vs, vv)
            v_vtok = vs.n

            th_mark = 0
            p_mark = 0
            pt_mark = 0
            for g0 in range(0, NT, 4):
                ghi = min(g0 + 4, NT)
                for sg in range(g0, ghi):
                    i0 = sg * 128
                    for jt in range(NS):
                        jl = slice(jt * 512, jt * 512 + 512)
                        pb = pp.get()
                        tv = None
                        for c in range(NCH):
                            fn = partial(i_mm, out=pb.t[:, :],
                                         lhsT=QT[:, c * S + i0:c * S + i0 + 128],
                                         rhs=KT[:, c * S:(c + 1) * S][:, jl],
                                         start=(c == 0), stop=(c == NCH - 1))
                            tv = ctx.inc("tensor", fn, ts) if c == NCH - 1 \
                                else ctx.emit("tensor", fn)
                        ctx.wait("scalar", ts, tv)
                        ctx.wait("scalar", vs, th_mark)
                        s1 = ctx.inc("scalar", partial(i_act, out=sc1[:, :], src=pb.t[:, :],
                                                       func=AF.Tanh, scale=0.5), ss)
                        s2 = ctx.inc("scalar", partial(i_act, out=sc2[:, :], src=pb.t[:, :],
                                                       func=AF.Tanh), ss)
                        ctx.wait("vector", ss, s2)
                        v1 = ctx.inc("vector", partial(
                            i_stt, out=sc2[:, :], a=sc1[:, :], s=2.0 / 3.0, b=sc2[:, :],
                            op0=OP.mult, op1=OP.add), vs)
                        ctx.wait("scalar", vs, v1)
                        s3 = ctx.inc("scalar", partial(i_act, out=sc1[:, :], src=pb.t[:, :],
                                                       func=AF.Relu, scale=0.1), ss)
                        pp.put(pb, ss, s3)
                        ctx.wait("vector", ss, s3)
                        ctx.dr("vector")
                        th_mark = ctx.inc("vector", partial(
                            i_stt, out=TH[:, jl], a=sc2[:, :], s=0.15, b=sc1[:, :],
                            op0=OP.mult, op1=OP.add), vs)
                    ctx.wait("scalar", vs, th_mark)
                    ctx.wait("scalar", ts, p_mark)
                    sv = ctx.inc("scalar", partial(i_act, out=P[:, :], src=TH[:, :],
                                                   func=AF.Exp, accum=sums[:, 0:1]), ss)
                    ctx.wait("vector", ss, sv)
                    ctx.inc("vector", partial(i_recip, out=rc[:, 0:1], src=sums[:, 0:1]), vs)
                    ctx.dr("vector")
                    vn = ctx.inc("vector", partial(i_ts, out=P[:, :], a=P[:, :], s1=rc[:, 0:1],
                                                   s2=None, op0=OP.mult, op1=None), vs)
                    ctx.wait("tensor", vs, vn)
                    ctx.wait("tensor", vs, v_vtok)
                    ctx.wait("vector", ts, pt_mark)
                    for j0 in range(0, NT, 8):
                        hi = min(j0 + 8, NT)
                        pb = pp.get(BF16)
                        tv = None
                        for j in range(j0, hi):
                            fn = partial(i_tr, out=pb.t[:, (j - j0) * 128:(j - j0 + 1) * 128],
                                         src=P[:, j * 128:(j + 1) * 128],
                                         iden=cst["id_bf"][:, :])
                            tv = ctx.inc("tensor", fn, ts) if j == hi - 1 \
                                else ctx.emit("tensor", fn)
                        p_mark = tv
                        ctx.wait("vector", ts, tv)
                        vv = None
                        for j in range(j0, hi):
                            vv = ctx.inc("vector", partial(
                                i_copy,
                                out=PT[:, j * 512 + (sg - g0) * 128:
                                       j * 512 + (sg - g0 + 1) * 128],
                                src=pb.t[:, (j - j0) * 128:(j - j0 + 1) * 128]), vs)
                        pp.put(pb, vs, vv)
                vptg = vs.n
                gw = (ghi - g0) * 128
                gi0 = g0 * 128
                for e in range(NCH):
                    pb = pp.get()
                    ctx.wait("tensor", vs, vptg)
                    tv = None
                    for j in range(NT):
                        fn = partial(i_mm, out=pb.t[:, 0:gw],
                                     lhsT=vtok[:, j * 1024 + e * 128:j * 1024 + (e + 1) * 128],
                                     rhs=PT[:, j * 512:j * 512 + gw],
                                     start=(j == 0), stop=(j == NT - 1))
                        tv = ctx.inc("tensor", fn, ts) if j == NT - 1 else ctx.emit("tensor", fn)
                    pt_mark = tv
                    ctx.wait("scalar", ts, tv)
                    sv = ctx.inc("scalar", partial(
                        i_act, out=OT[:, e * S + gi0:e * S + gi0 + gw], src=pb.t[:, 0:gw],
                        func=AF.Copy), ss)
                    pp.put(pb, ss, sv)
            v_ot = ss.n

            for e in range(NCH):
                ctx.wait_marks("sync", self.xres_w)
                ctx.wait("sync", sxo, sxo.n)
                vx = ctx.inc("sync", partial(
                    i_dma, out=Xc[:, :],
                    src=xres.rearrange("(c p) s -> p c s", p=128)[:, e, :]),
                    sd, 16)
                def evr(pb, st, tv, e=e, vx=vx):
                    sl = slice(st * 512, st * 512 + 512)
                    ctx.wait("vector", ts, tv)
                    ctx.wait("vector", sd, vx)
                    v1 = ctx.inc("vector", partial(
                        i_ts, out=sc1[:, :], a=pb.t[:, :], s1=cst["b_bo"][:, e:e + 1],
                        s2=None, op0=OP.add, op1=None), vs)
                    ctx.dr("vector")
                    ctx.inc("vector", partial(i_tt, out=Xc[:, sl], a=sc1[:, :], b=Xc[:, sl],
                                              op=OP.add), vs)
                    pp.put(pb, vs, v1)
                ctx.wait("tensor", ss, v_ot)
                self.proj_echunk(
                    w["b_woT"][:, e * 128:(e + 1) * 128].rearrange("(c p) n -> p c n", p=128),
                    OT, 0, evr, ws=self.ws_b)
                ctx.wait("sync", vs, vs.n)
                vw = ctx.inc("sync", partial(
                    i_dma,
                    out=xres.rearrange("(c p) s -> p c s", p=128)[:, e, :],
                    src=Xc[:, :]), sxo, 16)
                ctx.wait("sync", sxo, vw)
                if e == NCH - 1:
                    self.xres_w = [(sxo, vw)]
        ctx.flush()


_CACHE = {}


def build(S=2048, sublayers=("attn_a", "inner1", "attn_b", "inner2"), n_inner=DEPTH_IN):
    key = (S, tuple(sublayers), n_inner)
    if key not in _CACHE:
        b = Builder(S, tuple(sublayers), n_inner)
        b.nc.finalize()
        _CACHE[key] = b
    return _CACHE[key]


def make_inmaps(b, inputs, S=2048, n_cores=8):
    sh, per_core = host_prepare(inputs, S, n_cores)
    m = {}
    for wname in ("a_wqT", "a_wkT", "a_wvT", "a_wogT", "a_woutT", "a_wifT",
                  "b_wqT", "b_wkT", "b_wvT", "b_woT",
                  "f_wqkv", "f_wo", "f_w1", "f_w2"):
        m[wname] = sh[wname]
    for dname, key in b.feeds.items():
        arr = sh[key]
        if arr.ndim == 3 and key.startswith(("selA", "selV")) is False:
            pass
        m[dname] = arr
    return [dict(m, xin=pc["xin"]) for pc in per_core]


def kernel(**inputs):
    S = 2048
    n_cores = 8
    b = build(S)
    in_maps = make_inmaps(b, inputs, S, n_cores)
    res = run_bass_kernel_spmd(b.nc, in_maps, core_ids=list(range(n_cores)))
    outs = [np.asarray(r["yout"]).T for r in res.results]
    return np.ascontiguousarray(np.stack(outs, 0).astype(np.float32))


# version 06
# speedup vs baseline: 1.0956x; 1.0956x over previous
import math
from functools import partial
import numpy as np
import ml_dtypes
from contextlib import ExitStack

import concourse.bass as bass
import concourse.mybir as mybir
from concourse.alu_op_type import AluOpType as OP
from concourse.bass_utils import run_bass_kernel_spmd

F32 = mybir.dt.float32
BF16 = mybir.dt.bfloat16
AF = mybir.ActivationFunctionType

D = 1024
NCH = 8
NH_A = 8
DH_IN = 64
WIN = 256
DI = 2730
DIP = 2816
NJ = 22
DEPTH_IN = 6
NEG = -1e30
DBG_SKIP = set()
LNC_A = -0.5 * math.log(128.0)

bfd = ml_dtypes.bfloat16
ENGINES = ("tensor", "vector", "scalar", "gpsimd", "sync")


def i_mm(E, out, lhsT, rhs, start, stop):
    return E.matmul(out, lhsT, rhs, start=start, stop=stop)

def i_tr(E, out, src, iden):
    return E.transpose(out, src, iden)

def i_act(E, out, src, func, bias=0.0, scale=1.0, accum=None):
    return E.activation(out, src, func, bias=bias, scale=scale, accum_out=accum)

def i_tt(E, out, a, b, op):
    return E.tensor_tensor(out, a, b, op)

def i_ts(E, out, a, s1, s2, op0, op1):
    if s2 is None and op1 is None:
        return E.tensor_scalar(out, a, s1, None, op0)
    return E.tensor_scalar(out, a, s1, s2, op0, op1)

def i_stt(E, out, a, s, b, op0, op1):
    return E.scalar_tensor_tensor(out, a, s, b, op0, op1)

def i_scan(E, out, d0, d1, init, op0, op1):
    return E.tensor_tensor_scan(out, d0, d1, init, op0, op1)

def i_recip(E, out, src):
    return E.reciprocal(out, src)

def i_copy(E, out, src):
    return E.tensor_copy(out, src)

def i_memset(E, ap, c):
    return E.memset(ap, c)

def i_dma(E, out, src):
    return E.dma_start(out, src)

def i_wait(E, h, v):
    return E.wait_ge(h, v)

def i_drain(E):
    return E.drain()


class Sem:
    def __init__(self, handle, name):
        self.h = handle
        self.name = name
        self.n = 0


class Ctx:
    def __init__(self, nc):
        self.nc = nc
        self.code = {e: [] for e in ENGINES}
        self.last_wait = {}
        self.es = ExitStack()
        self.n_stages = 0
        self.n_instr = 0
        self.own_sem = {}

    def sem(self, name):
        return Sem(self.es.enter_context(self.nc.semaphore(name)), name)

    def emit(self, eng, fn):
        assert isinstance(fn, partial)
        self.n_instr += 1
        self.code[eng].append(fn)

    def wait(self, eng, sem, val):
        if val is None or val <= 0:
            return
        if val > sem.n:
            raise RuntimeError(f"wait {sem.name}>={val} but only {sem.n} emitted")
        key = (eng, sem.name)
        if self.last_wait.get(key, 0) >= val:
            return
        self.last_wait[key] = val
        self.code[eng].append(partial(i_wait, h=sem.h, v=val))

    def wait_marks(self, eng, marks):
        for sem, val in marks:
            self.wait(eng, sem, val)

    def dr(self, eng):
        sem = self.own_sem.get(eng)
        if sem is None:
            self.code[eng].append(partial(i_drain))
        else:
            self.wait(eng, sem, sem.n)

    def inc(self, eng, fn, sem, k=1):
        assert isinstance(fn, partial)
        self.n_instr += 1
        sem.n += k
        v = sem.n

        def run(E, fn=fn, h=sem.h, k=k):
            fn(E).then_inc(h, k)
        self.code[eng].append(run)
        return v

    def flush(self):
        code = self.code
        self.code = {e: [] for e in ENGINES}
        if not any(code.values()):
            return
        self.n_stages += 1
        with self.nc.allow_low_precision(reason="bf16 storage by design"), \
                self.nc.Block() as blk:
            if code["gpsimd"]:
                @blk.gpsimd
                def _(E, ops=code["gpsimd"]):
                    for f in ops:
                        f(E)
            if code["sync"]:
                @blk.sync
                def _(E, ops=code["sync"]):
                    for f in ops:
                        f(E)
            if code["tensor"]:
                @blk.tensor
                def _(E, ops=code["tensor"]):
                    for f in ops:
                        f(E)
            if code["scalar"]:
                @blk.scalar
                def _(E, ops=code["scalar"]):
                    for f in ops:
                        f(E)
            if code["vector"]:
                @blk.vector
                def _(E, ops=code["vector"]):
                    for f in ops:
                        f(E)


class PBuf:
    __slots__ = ("t", "dtype", "free")

    def __init__(self, t, dtype):
        self.t = t
        self.dtype = dtype
        self.free = []


class PsumPool:
    def __init__(self, ctx):
        self.ctx = ctx
        nc = ctx.nc
        self.f32 = [PBuf(ctx.es.enter_context(nc.psum_tensor(f"psf{i}", [128, 512], F32)), F32)
                    for i in range(6)]
        self.b16 = [PBuf(ctx.es.enter_context(nc.psum_tensor(f"psb{i}", [128, 1024], BF16)), BF16)
                    for i in range(2)]

    def get(self, dtype=F32):
        pool = self.f32 if dtype == F32 else self.b16
        if not pool:
            raise RuntimeError("psum pool exhausted")
        b = pool.pop(0)
        self.ctx.wait_marks("tensor", b.free)
        b.free = []
        return b

    def put(self, b, sem, val):
        b.free.append((sem, val))
        (self.f32 if b.dtype == F32 else self.b16).append(b)


class WStream:
    def __init__(self, ctx, gd, name, shape, slots=3, stack=None):
        self.ctx = ctx
        stk = stack if stack is not None else ctx.es
        self.tiles = [stk.enter_context(ctx.nc.sbuf_tensor(f"{name}{i}", list(shape), BF16))
                      for i in range(slots)]
        self.sems = [ctx.sem(f"{name}{i}s") for i in range(slots)]
        self.slots = slots
        self.i = 0
        self.ready = [0] * slots
        self.free = [[] for _ in range(slots)]

    def load(self, dram_ap, ncols=None):
        s = self.i % self.slots
        self.i += 1
        self.ctx.wait_marks("gpsimd", self.free[s])
        dest = self.tiles[s][:, :] if ncols is None else self.tiles[s][:, 0:ncols]
        self.ready[s] = self.ctx.inc("gpsimd", partial(i_dma, out=dest, src=dram_ap),
                                     self.sems[s], 16)
        return s

    def use(self, s):
        self.ctx.wait("tensor", self.sems[s], self.ready[s])
        return self.tiles[s]

    def done(self, s, sem, val):
        self.free[s] = [(sem, val)]


def _cols(v):
    v = np.asarray(v, np.float32)
    k = v.size // 128
    return np.ascontiguousarray(v.reshape(k, 128).T)


def _rotary_tables(S):
    hd = np.arange(0, DH_IN, 2, dtype=np.float64)
    inv_freq = 1.0 / 10000.0 ** (hd / DH_IN)
    t = np.arange(S, dtype=np.float64)
    fh = t[:, None] * inv_freq[None, :]
    freqs = np.concatenate([fh, fh], -1)
    base = (hd + 0.4 * DH_IN) / (1.4 * DH_IN)
    power = (t - S // 2) / (WIN // 2)
    sch = base[None, :] ** power[:, None]
    scale = np.concatenate([sch, sch], -1)
    cos, sin = np.cos(freqs), np.sin(freqs)
    mk = lambda a: np.ascontiguousarray(np.concatenate([a.T, a.T], 0).astype(bfd))
    return (mk(cos * scale), mk(sin * scale),
            mk(cos / scale), mk(sin / scale))


def _masks():
    il = np.arange(128)[:, None]
    jl = np.arange(512)[None, :]
    m0 = np.where((jl >= il) & (jl <= il + 256), 0.0, NEG)
    m1 = np.where((jl >= il + 128) & (jl <= il + 384), 0.0, NEG)
    jl2 = np.arange(256)[None, :]
    m0w = np.where(jl2 <= il, 0.0, NEG)
    m1w = np.where(jl2 <= il + 128, 0.0, NEG)
    tb = np.arange(128)[:, None]
    sb2 = np.arange(128)[None, :]
    mbin = (tb <= sb2).astype(np.float32)
    c = lambda a: np.ascontiguousarray(a.astype(bfd))
    return c(m0), c(m1), c(m0w), c(m1w), c(mbin)


def _tile3(wT, ncol):
    K, M = wT.shape
    nc_ = K // 128
    nj = M // ncol
    t = wT.reshape(nc_, 128, nj, ncol).transpose(2, 1, 0, 3).reshape(nj, 128, nc_ * ncol)
    return np.ascontiguousarray(t)


def host_prepare(inputs, S, n_cores):
    f32 = np.float32
    sh = {}
    x = np.asarray(inputs["L0_attn_a_x"], f32)
    per_core = [{"xin": np.ascontiguousarray(x[c, :S, :].T)} for c in range(n_cores)]
    T = lambda a: np.ascontiguousarray(np.asarray(a, f32).T)

    sh["a_wqT"] = _tile3(T(inputs["L0_attn_a_Wq"]), 128)
    sh["a_wkT"] = _tile3(T(inputs["L0_attn_a_Wk"]), 128)
    sh["a_wvT"] = _tile3(T(inputs["L0_attn_a_Wv"]), 128)
    sh["a_wogT"] = _tile3(T(inputs["L0_attn_a_Wog"]), 128)
    sh["a_woutT"] = _tile3(T(inputs["L0_attn_a_Wout"]), 128)
    sh["a_wifT"] = _tile3(np.ascontiguousarray(
        np.concatenate([T(inputs["L0_attn_a_Wi"]), T(inputs["L0_attn_a_Wf"])], axis=1)), 16)
    bif = np.zeros((128, 1), f32)
    bif[0:8, 0] = np.asarray(inputs["L0_attn_a_bi"], f32) / 15.0
    bif[32:40, 0] = np.asarray(inputs["L0_attn_a_bf"], f32) / 15.0
    sh["a_bif"] = bif
    sh["a_lnw"] = _cols(inputs["L0_attn_a_ln_w"])

    sh["b_wqT"] = _tile3(T(inputs["L0_attn_b_Wq"]), 128)
    sh["b_wkT"] = _tile3(T(inputs["L0_attn_b_Wk"]), 128)
    sh["b_wvT"] = _tile3(T(inputs["L0_attn_b_Wv"]), 128)
    sh["b_woT"] = _tile3(T(inputs["L0_attn_b_Wo"]), 128)
    for nm in ("bq", "bk", "bv", "bo"):
        sh["b_" + nm] = _cols(inputs["L0_attn_b_" + nm])

    sh["f_wqkv"] = np.ascontiguousarray(np.stack(
        [_tile3(np.asarray(inputs["L0_ffn_Wqkv"], f32)[i], 128) for i in range(DEPTH_IN)]))
    sh["f_wo"] = np.ascontiguousarray(np.stack(
        [_tile3(np.asarray(inputs["L0_ffn_Wo"], f32)[i], 128) for i in range(DEPTH_IN)]))
    w1 = np.asarray(inputs["L0_ffn_W1"], f32)
    w1p = np.zeros((DEPTH_IN, D, 2 * DIP), f32)
    w1p[:, :, :DI] = w1[:, :, :DI]
    w1p[:, :, DIP:DIP + DI] = w1[:, :, DI:]
    sh["f_w1"] = np.ascontiguousarray(np.stack(
        [_tile3(w1p[i], 128) for i in range(DEPTH_IN)]))
    w2 = np.asarray(inputs["L0_ffn_W2"], f32)
    w2p = np.zeros((DEPTH_IN, DIP, D), f32)
    w2p[:, :DI, :] = w2
    sh["f_w2"] = np.ascontiguousarray(np.stack(
        [w2p[i].reshape(NJ, 128, 8, 128).transpose(2, 1, 0, 3).reshape(8, 128, NJ * 128)
         for i in range(DEPTH_IN)]))
    b1 = np.asarray(inputs["L0_ffn_b1"], f32)
    b1a = np.zeros((DEPTH_IN, DIP), f32); b1a[:, :DI] = b1[:, :DI]
    b1g = np.zeros((DEPTH_IN, DIP), f32); b1g[:, :DI] = b1[:, DI:]
    hstk = lambda lst: np.ascontiguousarray(np.concatenate(lst, axis=1))
    sh["f_b1a"] = hstk([_cols(b1a[i]) for i in range(DEPTH_IN)])
    sh["f_b1g"] = hstk([_cols(b1g[i]) for i in range(DEPTH_IN)])
    sh["f_b2"] = hstk([_cols(np.asarray(inputs["L0_ffn_b2"], f32)[i]) for i in range(DEPTH_IN)])
    sh["f_g"] = hstk([_cols(np.asarray(inputs["L0_ffn_ln_g"], f32)[i]) for i in range(DEPTH_IN)])
    sh["f_bln"] = hstk([_cols(np.asarray(inputs["L0_ffn_ln_b"], f32)[i]) for i in range(DEPTH_IN)])
    qs = np.asarray(inputs["L0_ffn_q_scale"], f32)
    ks = np.asarray(inputs["L0_ffn_k_scale"], f32)
    sh["f_qs"] = np.ascontiguousarray(np.concatenate([qs, qs], 1).T)
    sh["f_ks"] = np.ascontiguousarray(np.concatenate([ks, ks], 1).T)

    Cq, Sq, Ck, Sk = _rotary_tables(S)
    sh["rot_cq"], sh["rot_sq"], sh["rot_ck"], sh["rot_sk"] = Cq, Sq, Ck, Sk
    sh["m0"], sh["m1"], sh["m0w"], sh["m1w"], sh["mbinA"] = _masks()
    sh["id_bf"] = np.ascontiguousarray(np.eye(128, dtype=bfd))
    sh["id_f32"] = np.ascontiguousarray(np.eye(128, dtype=np.float32))
    sh["ones_1"] = np.full((128, 128), 1.0, bfd)
    sh["ones_gn"] = np.full((128, 128), 1.0 / 128.0, bfd)
    sh["ones_ln"] = np.full((128, 128), 1.0 / 1024.0, bfd)
    bd = np.zeros((128, 128), f32)
    bd[0:64, 0:64] = 1.0; bd[64:128, 64:128] = 1.0
    sh["bdiag64"] = np.ascontiguousarray(bd.astype(bfd))
    pm = np.zeros((128, 128), f32)
    for p in range(128):
        base = (p // 64) * 64
        dl = p % 64
        sgn = -1.0 if dl < 32 else 1.0
        sig = base + (dl + 32) % 64
        pm[sig, p] = sgn
    sh["rotperm"] = np.ascontiguousarray(pm)
    selA = np.zeros((NH_A, 128, 128), f32)
    selV = np.zeros((NH_A, 128, 128), f32)
    for h in range(NH_A):
        selA[h, h, :] = 1; selA[h, 32 + h, :] = 1
        selV[h, 64 + h, :] = 1; selV[h, 96 + h, :] = 1
    sh["selA"] = np.ascontiguousarray(selA.astype(bfd))
    sh["selV"] = np.ascontiguousarray(selV.astype(bfd))
    return sh, per_core


class Builder:
    def __init__(self, S, sublayers=("attn_a", "inner1", "attn_b", "inner2"),
                 n_inner=DEPTH_IN):
        self.S = S
        self.NS = S // 512
        self.NW = S // WIN
        self.NT = S // 128
        self.sublayers = sublayers
        self.n_inner = n_inner
        self.nc = bass.Bass("TRN2", target_bir_lowering=False)
        self.ctx = Ctx(self.nc)
        self._name_seq = 0
        self.feeds = {}
        self._build()

    def din(self, name, shape, dtype=F32):
        return self.nc.dram_tensor(name, list(shape), dtype, kind="ExternalInput").ap()

    def sb(self, name, shape, dtype, stack=None):
        stk = stack if stack is not None else self.ctx.es
        self._name_seq += 1
        return stk.enter_context(
            self.nc.sbuf_tensor(f"{name}_{self._name_seq}", list(shape), dtype))

    def ld_const(self, key, shape, dtype, stk=None):
        self._name_seq += 1
        dname = f"k_{key}_{self._name_seq}"
        self.feeds[dname] = key
        da = self.din(dname, shape, dtype)
        self.ctx.wait("sync", self.sx, self.sx.n)
        self.ctx.wait("sync", self.sxo, self.sxo.n)
        if len(shape) == 3:
            t = self.sb("c_" + key, [shape[1], shape[0] * shape[2]], dtype, stk)
            ap = da[:, :, :].rearrange("a b c -> b a c")
            dst = t[:, :].rearrange("b (a c) -> b a c", a=shape[0])
        else:
            t = self.sb("c_" + key, shape, dtype, stk)
            ap = da[:, :]
            dst = t[:, :]
        v = self.ctx.inc("sync", partial(i_dma, out=dst, src=ap), self.sd, 16)
        self._const_marks.append((self.sd, v))
        return t

    def wait_consts(self, eng):
        self.ctx.wait(eng, self.sd, self.sd.n)

    def _build(self):
        S, NS, ctx = self.S, self.NS, self.ctx
        self.gd = ctx.sem("gd")
        self.sd = ctx.sem("sd")
        self.sx = ctx.sem("sx")
        self.sxo = ctx.sem("sxo")
        self.ts = ctx.sem("ts")
        self.vs = ctx.sem("vs")
        self.ss = ctx.sem("ss")
        ctx.own_sem = {"tensor": self.ts, "vector": self.vs, "scalar": self.ss}
        sd = self.sx
        self.pp = PsumPool(ctx)
        self._const_marks = []
        self._cb_pending = True

        xin = self.din("xin", [D, S])
        yout = self.nc.dram_tensor("yout", [D, S], F32, kind="ExternalOutput").ap()
        cvals = [1e-5, 1e-6, LNC_A, 1.0 / 15.0, 15.0, 0.5, 0.1, 2.0 / 3.0, 0.15, 8.0,
                 -0.5, -1.0]
        cb = self.sb("cbias", [128, len(cvals)], F32)
        for i, val in enumerate(cvals):
            vv = ctx.inc("vector", partial(i_memset, ap=cb[:, i:i + 1], c=float(val)),
                         self.vs if hasattr(self, 'vs') else None, 1) if False else None
        self._cb = cb
        vlast = 0
        for i, val in enumerate(cvals):
            vlast = ctx.inc("vector", partial(i_memset, ap=cb[:, i:i + 1], c=float(val)),
                            self.vs, 1)
            self.nc.const_aps.register(F32, float(val), cb[:, i:i + 1])
        for e in ("tensor", "scalar", "gpsimd", "sync"):
            ctx.wait(e, self.vs, vlast)
        xres = self.nc.dram_tensor("xres", [D, S], F32, kind="Internal").ap()
        self.xres = xres
        self.xres_w = []

        self.w = {}
        for n, shp in [("a_wqT", [8, 128, D]), ("a_wkT", [1, 128, D]),
                       ("a_wvT", [1, 128, D]),
                       ("a_wogT", [8, 128, D]), ("a_woutT", [8, 128, D]),
                       ("a_wifT", [1, 128, 128]),
                       ("b_wqT", [8, 128, D]), ("b_wkT", [8, 128, D]),
                       ("b_wvT", [8, 128, D]), ("b_woT", [8, 128, D]),
                       ("f_wqkv", [DEPTH_IN, 24, 128, D]), ("f_wo", [DEPTH_IN, 8, 128, D]),
                       ("f_w1", [DEPTH_IN, 44, 128, D]),
                       ("f_w2", [DEPTH_IN, 8, 128, NJ * 128])]:
            self.w[n] = self.din(n, shp)

        self.cst = {}
        for n, shp, dt in [("a_bif", [128, 1], F32), ("a_lnw", [128, 8], F32),
                           ("b_bq", [128, 8], F32), ("b_bk", [128, 8], F32),
                           ("b_bv", [128, 8], F32), ("b_bo", [128, 8], F32),
                           ("f_b1a", [128, DEPTH_IN * NJ], F32),
                           ("f_b1g", [128, DEPTH_IN * NJ], F32),
                           ("f_b2", [128, DEPTH_IN * 8], F32),
                           ("f_g", [128, DEPTH_IN * 8], F32),
                           ("f_bln", [128, DEPTH_IN * 8], F32),
                           ("f_qs", [128, DEPTH_IN], F32),
                           ("f_ks", [128, DEPTH_IN], F32),
                           ("id_bf", [128, 128], BF16), ("ones_1", [128, 128], BF16),
                           ("ones_gn", [128, 128], BF16), ("ones_ln", [128, 128], BF16)]:
            self.cst[n] = self.ld_const(n, shp, dt)

        self.XB = self.sb("XB", [128, NCH * S], BF16)
        self.ws_a = WStream(ctx, self.gd, "ws_a", [128, 1024], slots=4)
        self.ws_b = WStream(ctx, self.gd, "ws_b", [128, 1024], slots=4)
        self.ws_w2 = None
        self.ws_s = WStream(ctx, self.gd, "ws_s", [128, 128], slots=2)

        with ExitStack() as stk:
            Xf = self.sb("Xf", [128, NCH * S], F32, stk)
            v = ctx.inc("sync", partial(
                i_dma, out=Xf[:, :], src=xin.rearrange("(c p) s -> p c s", p=128)), sd, 16)
            ctx.wait("sync", sd, v)
            ctx.inc("sync", partial(
                i_dma, out=xres.rearrange("(c p) s -> p c s", p=128), src=Xf[:, :]), sd, 16)
            self._xspill = sd.n
            self.xres_w = [(sd, sd.n)]
            self.ln_stream(Xf, self.XB, None, 1e-5, wait=[(sd, v)])
            ctx.flush()
        if "attn_a" in self.sublayers:
            self.attn_a(xres)
        ctx.flush()

        if "inner1" in self.sublayers:
            self.seed_Z(xres)
            self.inner_pass()
            self.residual_from_hbm(xres)
        if "attn_b" in self.sublayers:
            self.outer_ln_to_bf(xres)
            self.attn_b(xres)
        if "inner2" in self.sublayers:
            self.seed_Z(xres)
            self.inner_pass()
            self.residual_from_hbm(xres)
        ctx.flush()

        with ExitStack() as stk:
            Of = self.sb("Of", [128, NCH * S], F32, stk)
            ctx.wait_marks("sync", self.xres_w)
            v = ctx.inc("sync", partial(
                i_dma, out=Of[:, :], src=xres.rearrange("(c p) s -> p c s", p=128)), sd, 16)
            ctx.wait("sync", sd, v)
            vo = ctx.inc("sync", partial(
                i_dma, out=yout.rearrange("(c p) s -> p c s", p=128), src=Of[:, :]), sd, 16)
            ctx.wait("sync", sd, vo)
            ctx.flush()
        ctx.es.close()

    def ln_stream(self, Xf, XNout, z_out, eps, wait=()):
        S, NS, ctx, pp = self.S, self.NS, self.ctx, self.pp
        ts, vs, ss = self.ts, self.vs, self.ss
        cst = self.cst
        with ExitStack() as stk:
            xbt = [self.sb(f"lnxb{i}", [128, 512], BF16, stk) for i in range(2)]
            sqb = [self.sb(f"lnsq{i}", [128, 512], BF16, stk) for i in range(2)]
            apl = self.sb("lnapl", [128, S], F32, stk)
            A = self.sb("lnA", [128, S], BF16, stk)
            B = self.sb("lnB", [128, S], BF16, stk)
            sc = self.sb("lnsc", [128, 512], F32, stk)
            self.wait_consts("vector")
            self.wait_consts("tensor")
            prev_chain = 0
            cast_mark = [0, 0]
            sq_mark = [0, 0]
            for st in range(NS):
                sl = slice(st * 512, st * 512 + 512)
                mu = pp.get(); m2 = pp.get()
                tmu = tm2 = None
                for c in range(NCH):
                    par = c % 2
                    ctx.wait_marks("vector", wait)
                    ctx.wait("vector", ts, cast_mark[par])
                    ctx.dr("vector")
                    vc = ctx.inc("vector", partial(
                        i_copy, out=xbt[par][:, :], src=Xf[:, c * S:(c + 1) * S][:, sl]), vs)
                    ctx.dr("vector")
                    vsq = ctx.inc("vector", partial(
                        i_tt, out=sqb[par][:, :], a=xbt[par][:, :], b=xbt[par][:, :],
                        op=OP.mult), vs)
                    ctx.wait("tensor", vs, vsq)
                    fn = partial(i_mm, out=mu.t[:, :], lhsT=cst["ones_ln"][:, :],
                                 rhs=xbt[par][:, :],
                                 start=(c == 0), stop=(c == NCH - 1))
                    tmu = ctx.inc("tensor", fn, ts) if c == NCH - 1 else ctx.emit("tensor", fn)
                    fn = partial(i_mm, out=m2.t[:, :], lhsT=cst["ones_ln"][:, :],
                                 rhs=sqb[par][:, :],
                                 start=(c == 0), stop=(c == NCH - 1))
                    tm2 = ctx.inc("tensor", fn, ts)
                    cast_mark[par] = tm2
                ctx.wait("scalar", ts, tmu)
                ctx.wait("scalar", vs, prev_chain)
                s1 = ctx.inc("scalar", partial(i_act, out=sc[:, :], src=mu.t[:, :],
                                               func=AF.Square), ss)
                ctx.wait("vector", ss, s1)
                ctx.wait("vector", ts, tm2)
                ctx.dr("vector")
                v1 = ctx.inc("vector", partial(i_tt, out=sc[:, :], a=m2.t[:, :], b=sc[:, :],
                                               op=OP.subtract), vs)
                ctx.wait("scalar", vs, v1)
                ctx.inc("scalar", partial(i_act, out=sc[:, :], src=sc[:, :],
                                          func=AF.Ln, bias=float(eps)), ss)
                ctx.dr("scalar")
                s2 = ctx.inc("scalar", partial(i_act, out=A[:, sl], src=sc[:, :],
                                               func=AF.Exp, scale=-0.5), ss)
                ctx.wait("vector", ss, s2)
                v3 = ctx.inc("vector", partial(i_stt, out=B[:, sl], a=mu.t[:, :], s=-1.0,
                                               b=A[:, sl], op0=OP.mult, op1=OP.mult), vs)
                prev_chain = v3
                pp.put(mu, vs, v3)
                pp.put(m2, vs, v1)
            ctx.dr("vector")
            for c in range(NCH):
                cs = slice(c * S, (c + 1) * S)
                dsts = []
                if XNout is not None:
                    dsts.append(XNout)
                if z_out is not None:
                    dsts.append(z_out)
                for dst in dsts:
                    ctx.dr("vector")
                    ctx.inc("vector", partial(i_tt, out=apl[:, :], a=Xf[:, cs], b=A[:, :],
                                              op=OP.mult), vs)
                    ctx.dr("vector")
                    ctx.inc("vector", partial(i_tt, out=dst[:, cs], a=apl[:, :], b=B[:, :],
                                              op=OP.add), vs)
            self._lnfin = vs.n
        ctx.flush()

    def seed_Z(self, xres):
        S, ctx, sd = self.S, self.ctx, self.sx
        self.Zstk = ExitStack()
        self.Z = self.sb("Z", [128, NCH * S], F32, self.Zstk)
        with ExitStack() as stk:
            Xf = self.sb("olnx", [128, NCH * S], F32, stk)
            ctx.wait_marks("sync", self.xres_w)
            v = ctx.inc("sync", partial(
                i_dma, out=Xf[:, :], src=xres.rearrange("(c p) s -> p c s", p=128)), sd, 16)
            self.ln_stream(Xf, None, self.Z, 1e-5, wait=[(sd, v)])

    def outer_ln_to_bf(self, xres):
        S, ctx, sd = self.S, self.ctx, self.sx
        with ExitStack() as stk:
            Xf = self.sb("olnx2", [128, NCH * S], F32, stk)
            ctx.wait_marks("sync", self.xres_w)
            v = ctx.inc("sync", partial(
                i_dma, out=Xf[:, :], src=xres.rearrange("(c p) s -> p c s", p=128)), sd, 16)
            ctx.wait("vector", self.ts, getattr(self, "xs_free", 0))
            self.ln_stream(Xf, self.XB, None, 1e-5, wait=[(sd, v)])

    def residual_from_hbm(self, xres):
        S, ctx, sd, vs = self.S, self.ctx, self.sx, self.vs
        with ExitStack() as stk:
            Xf = self.sb("resx", [128, NCH * S], F32, stk)
            ctx.wait_marks("sync", self.xres_w)
            v = ctx.inc("sync", partial(
                i_dma, out=Xf[:, :], src=xres.rearrange("(c p) s -> p c s", p=128)), sd, 16)
            ctx.wait("vector", sd, v)
            vv = ctx.inc("vector", partial(i_tt, out=Xf[:, :], a=Xf[:, :], b=self.Z[:, :],
                                           op=OP.add), vs)
            ctx.wait("sync", vs, vv)
            vw = ctx.inc("sync", partial(
                i_dma, out=xres.rearrange("(c p) s -> p c s", p=128), src=Xf[:, :]), sd, 16)
            ctx.wait("sync", sd, vw)
            self.xres_w = [(sd, vw)]
            ctx.flush()
        self.Zstk.close()

    def proj_echunk(self, wap, SRC, src_wait, evac, ncol=128, ws=None):
        ctx, pp, ts = self.ctx, self.pp, self.ts
        S, NS = self.S, self.NS
        ws = ws or self.ws_a
        slot = ws.load(wap, ncols=NCH * ncol)
        wt = ws.use(slot)
        for st in range(NS):
            sl = slice(st * 512, st * 512 + 512)
            pb = pp.get()
            ctx.wait("tensor", self.vs, src_wait)
            tv = None
            for c in range(NCH):
                fn = partial(i_mm, out=pb.t[0:ncol, :] if ncol < 128 else pb.t[:, :],
                             lhsT=wt[:, c * ncol:(c + 1) * ncol],
                             rhs=SRC[:, c * S:(c + 1) * S][:, sl],
                             start=(c == 0), stop=(c == NCH - 1))
                tv = ctx.inc("tensor", fn, ts) if c == NCH - 1 else ctx.emit("tensor", fn)
            evac(pb, st, tv)
        ws.done(slot, ts, ts.n)

    def attn_a(self, xres):
        S, NS, NT, ctx = self.S, self.NS, self.NT, self.ctx
        ts, vs, ss, sd = self.ts, self.vs, self.ss, self.sx
        pp, w, cst = self.pp, self.w, self.cst
        XN = self.XB
        with ExitStack() as stk:
            for n, shp, dt in [("mbinA", [128, 128], BF16), ("id_f32", [128, 128], F32),
                               ("selA", [NH_A, 128, 128], BF16),
                               ("selV", [NH_A, 128, 128], BF16)]:
                cst[n] = self.ld_const(n, shp, dt, stk)
            self.wait_consts("tensor")
            self.wait_consts("scalar")
            self.wait_consts("vector")
            KT = self.sb("aKT", [128, S], BF16, stk)
            VT = self.sb("aVT", [128, S], BF16, stk)
            vtok = self.sb("avtok", [128, S], BF16, stk)
            hl = self.sb("ahl", [128, S], BF16, stk)
            Xc = self.sb("aXc", [128, S], F32, stk)
            acol = self.sb("aacol", [128, NT * 8], F32, stk)
            gstk = ExitStack()
            lf = self.sb("alf", [8, S], F32, gstk)
            csum = self.sb("acs", [8, S], F32, gstk)
            cmx = self.sb("acmx", [8, S], F32, gstk)
            arow = self.sb("aarow", [8, S], F32, gstk)
            zrow = self.sb("azrow", [8, S], F32, gstk)
            vmrow = self.sb("avm", [8, S], F32, gstk)
            g16i = self.sb("agi", [8, S], F32, gstk)
            g16f = self.sb("agf", [8, S], F32, gstk)
            hi8 = self.sb("ahi8", [8, S], F32, gstk)
            lo8 = self.sb("alo8", [8, S], F32, gstk)
            ctx.wait("tensor", vs, self._lnfin)
            self.wait_consts("tensor")

            def mk_evac_plain(dst):
                def e(pb, st, tv):
                    ctx.wait("scalar", ts, tv)
                    sv = ctx.inc("scalar", partial(
                        i_act, out=dst[:, st * 512:(st + 1) * 512], src=pb.t[:, :],
                        func=AF.Copy), ss)
                    pp.put(pb, ss, sv)
                return e
            self.proj_echunk(w["a_wkT"][0], XN, self._lnfin, mk_evac_plain(KT))
            self.proj_echunk(w["a_wvT"][0], XN, self._lnfin, mk_evac_plain(VT))

            def evac_g(pb, st, tv):
                ctx.wait("scalar", ts, tv)
                self.wait_consts("scalar")
                ctx.inc("scalar", partial(
                    i_act, out=g16i[:, st * 512:(st + 1) * 512], src=pb.t[0:8, :],
                    func=AF.Tanh, bias=cst["a_bif"][0:8, 0:1], scale=1.0 / 15.0), ss)
                sv = ctx.inc("scalar", partial(
                    i_act, out=g16f[:, st * 512:(st + 1) * 512], src=pb.t[32:40, :],
                    func=AF.Tanh, bias=cst["a_bif"][32:40, 0:1], scale=1.0 / 15.0), ss)
                pp.put(pb, ss, sv)

            slot_g = self.ws_s.load(w["a_wifT"][0], ncols=NCH * 16)
            wt_g = self.ws_s.use(slot_g)
            for st in range(NS):
                sl = slice(st * 512, st * 512 + 512)
                pb = pp.get()
                ctx.wait("tensor", vs, self._lnfin)
                tv = None
                for c in range(NCH):
                    fni = partial(i_mm, out=pb.t[0:8, :], lhsT=wt_g[:, c * 16:c * 16 + 8],
                                  rhs=XN[:, c * S:(c + 1) * S][:, sl],
                                  start=(c == 0), stop=(c == NCH - 1))
                    ctx.emit("tensor", fni)
                for c in range(NCH):
                    fnf = partial(i_mm, out=pb.t[32:40, :], lhsT=wt_g[:, c * 16 + 8:c * 16 + 16],
                                  rhs=XN[:, c * S:(c + 1) * S][:, sl],
                                  start=(c == 0), stop=(c == NCH - 1))
                    tv = ctx.inc("tensor", fnf, ts) if c == NCH - 1 else ctx.emit("tensor", fnf)
                evac_g(pb, st, tv)
            self.ws_s.done(slot_g, ts, ts.n)

            vz = ctx.inc("vector", partial(i_memset, ap=zrow[:, :], c=0.0), vs)
            vz = ctx.inc("vector", partial(i_memset, ap=hl[:, :], c=0.0), vs)
            ctx.wait("scalar", ss, ss.n)
            ctx.inc("scalar", partial(i_act, out=lf[:, :], src=g16f[:, :],
                                      func=AF.Sigmoid, scale=15.0), ss)
            ctx.dr("scalar")
            s_ln = ctx.inc("scalar", partial(i_act, out=lf[:, :], src=lf[:, :], func=AF.Ln), ss)
            ctx.wait("vector", ss, s_ln)
            ctx.wait("vector", vs, vz)
            ctx.inc("vector", partial(i_scan, out=csum[:, :], d0=lf[:, :], d1=zrow[0:8, :],
                                      init=0.0, op0=OP.add, op1=OP.bypass), vs)
            ctx.dr("vector")
            v_a = ctx.inc("vector", partial(i_stt, out=arow[:, :], a=g16i[:, :], s=15.0,
                                            b=csum[:, :], op0=OP.mult, op1=OP.subtract), vs)
            ctx.dr("vector")
            ctx.inc("vector", partial(i_scan, out=cmx[:, :], d0=arow[:, :], d1=zrow[0:8, :],
                                      init=-1e30, op0=OP.max, op1=OP.bypass), vs)
            ctx.dr("vector")
            ctx.inc("vector", partial(i_ts, out=hi8[:, :], a=cmx[:, :], s1=-1.0, s2=None,
                                      op0=OP.mult, op1=None), vs)
            ctx.dr("vector")
            ctx.inc("vector", partial(i_copy, out=hl[0:8, :], src=hi8[:, :]), vs)
            ctx.inc("vector", partial(i_stt, out=lo8[:, :], a=cmx[:, :], s=-1.0,
                                      b=hi8[:, :], op0=OP.mult, op1=OP.subtract), vs)
            ctx.dr("vector")
            v_bc1 = ctx.inc("vector", partial(i_copy, out=hl[32:40, :], src=lo8[:, :]), vs)
            ctx.inc("vector", partial(i_tt, out=vmrow[:, :], a=csum[:, :], b=cmx[:, :],
                                      op=OP.add), vs)
            ctx.dr("vector")
            ctx.inc("vector", partial(i_ts, out=hi8[:, :], a=vmrow[:, :], s1=-1.0, s2=None,
                                      op0=OP.mult, op1=None), vs)
            ctx.dr("vector")
            ctx.inc("vector", partial(i_copy, out=hl[64:72, :], src=hi8[:, :]), vs)
            ctx.inc("vector", partial(i_stt, out=lo8[:, :], a=vmrow[:, :], s=-1.0,
                                      b=hi8[:, :], op0=OP.mult, op1=OP.subtract), vs)
            ctx.dr("vector")
            v_bc2 = ctx.inc("vector", partial(i_copy, out=hl[96:104, :], src=lo8[:, :]), vs)

            ctx.wait("tensor", vs, v_a)
            for t0 in range(0, NT, 8):
                hi = min(t0 + 8, NT)
                pb = pp.get()
                tv = None
                for tt in range(t0, hi):
                    fn = partial(i_tr, out=pb.t[:, (tt - t0) * 8:(tt - t0) * 8 + 8],
                                 src=arow[:, tt * 128:(tt + 1) * 128],
                                 iden=cst["id_f32"][0:8, 0:8])
                    tv = ctx.inc("tensor", fn, ts) if tt == hi - 1 else ctx.emit("tensor", fn)
                ctx.wait("scalar", ts, tv)
                sv = ctx.inc("scalar", partial(
                    i_act, out=acol[:, t0 * 8:hi * 8], src=pb.t[:, 0:(hi - t0) * 8],
                    func=AF.Copy, bias=LNC_A), ss)
                pp.put(pb, ss, sv)

            gstk.close()
            BC = self.sb("aBC", [128, S], F32, stk)
            EMV = self.sb("aEMV", [128, S], F32, stk)
            QH = self.sb("aQH", [128, S], BF16, stk)
            OGH = self.sb("aOGH", [128, S], BF16, stk)
            hout = self.sb("ahout", [128, NCH * S], BF16, stk)
            hsq = self.sb("ahsq", [128, S], BF16, stk)
            ctb = [self.sb(f"act_{i}", [128, 512], BF16, stk) for i in range(2)]
            exb = [self.sb(f"aex_{i}", [128, 512], F32, stk) for i in range(2)]
            scrD = self.sb("ascrD", [128, 512], F32, stk)
            scr = self.sb("ascr", [128, 512], F32, stk)
            scr2 = self.sb("ascr2", [128, 512], F32, stk)
            ctx.wait("vector", ts, ts.n)
            ctx.wait("scalar", ts, ts.n)
            ctx.wait("scalar", vs, vs.n)
            ctx.wait("tensor", vs, vs.n)
            ctx.wait("tensor", ss, ss.n)
            ctx.wait("vector", ss, ss.n)
            ctx.dr("vector")
            ctx.dr("scalar")
            ctx.wait("tensor", ss, ss.n)
            for j0 in range(0, NT, 8):
                hi = min(j0 + 8, NT)
                pb = pp.get(BF16)
                tv = None
                for j in range(j0, hi):
                    fn = partial(i_tr, out=pb.t[:, (j - j0) * 128:(j - j0 + 1) * 128],
                                 src=VT[:, j * 128:(j + 1) * 128], iden=cst["id_bf"][:, :])
                    tv = ctx.inc("tensor", fn, ts) if j == hi - 1 else ctx.emit("tensor", fn)
                ctx.wait("vector", ts, tv)
                vv = ctx.inc("vector", partial(
                    i_copy, out=vtok[:, j0 * 128:hi * 128], src=pb.t[:, 0:(hi - j0) * 128]), vs)
                pp.put(pb, vs, vv)

            ct_mark = [0, 0]
            ex_mark = [0, 0]
            scrD_mark = 0
            scr_s_mark = 0
            scr_v_mark = 0
            bc_mark = 0
            qh_mark = (0, 0)
            for h in range(NH_A):
                def evac_q(pb, st, tv, dst=QH):
                    ctx.wait("scalar", ts, tv)
                    ctx.wait("scalar", ts, qh_mark[0])
                    ctx.wait("scalar", vs, qh_mark[1])
                    sv = ctx.inc("scalar", partial(
                        i_act, out=dst[:, st * 512:(st + 1) * 512], src=pb.t[:, :],
                        func=AF.Copy), ss)
                    pp.put(pb, ss, sv)
                def evac_og(pb, st, tv, dst=OGH):
                    ctx.wait("scalar", ts, tv)
                    sv = ctx.inc("scalar", partial(
                        i_act, out=dst[:, st * 512:(st + 1) * 512], src=pb.t[:, :],
                        func=AF.Sigmoid), ss)
                    pp.put(pb, ss, sv)
                self.proj_echunk(w["a_wqT"][h], XN, self._lnfin, evac_q)
                self.proj_echunk(w["a_wogT"][h], XN, self._lnfin, evac_og, ws=self.ws_b)
                sqh = ss.n

                for st in range(NS):
                    sl = slice(st * 512, st * 512 + 512)
                    pb = pp.get()
                    ctx.wait("tensor", vs, v_bc2)
                    tv = ctx.inc("tensor", partial(
                        i_mm, out=pb.t[:, :], lhsT=cst["selA"][:, h * 128:(h + 1) * 128],
                        rhs=hl[:, sl], start=True, stop=True), ts)
                    ctx.wait("scalar", ts, tv)
                    ctx.wait("scalar", vs, bc_mark)
                    sv = ctx.inc("scalar", partial(
                        i_act, out=BC[:, sl], src=pb.t[:, :], func=AF.Copy), ss)
                    pp.put(pb, ss, sv)
                    pb = pp.get()
                    ctx.wait("tensor", vs, v_bc2)
                    tv = ctx.inc("tensor", partial(
                        i_mm, out=pb.t[:, :], lhsT=cst["selV"][:, h * 128:(h + 1) * 128],
                        rhs=hl[:, sl], start=True, stop=True), ts)
                    ctx.wait("scalar", ts, tv)
                    sv = ctx.inc("scalar", partial(
                        i_act, out=EMV[:, sl], src=pb.t[:, :], func=AF.Exp), ss)
                    pp.put(pb, ss, sv)
                ctx.dr("scalar")
                sbc = ss.n

                for sp in range(NS):
                    s0 = sp * 512
                    hacc = pp.get()
                    racc = pp.get()
                    ntau = 4 * (sp + 1)
                    tpv = None
                    for tau in range(ntau):
                        t0 = tau * 128
                        cols0 = max(s0, t0)
                        ncol = s0 + 512 - cols0
                        pad = cols0 - s0
                        diag = t0 >= s0
                        par = tau % 2
                        ctv = ctb[par]
                        exv = exb[par]
                        qk = pp.get()
                        ctx.wait("tensor", ss, sqh)
                        tv = ctx.inc("tensor", partial(
                            i_mm, out=qk.t[:, 0:ncol],
                            lhsT=KT[:, t0:t0 + 128],
                            rhs=QH[:, cols0:cols0 + ncol], start=True, stop=True), ts)
                        if diag:
                            ctx.wait("vector", ss, sbc)
                            ctx.wait("vector", ss, scrD_mark)
                            vv = ctx.inc("vector", partial(
                                i_ts, out=scrD[:, 0:ncol], a=BC[:, cols0:cols0 + ncol],
                                s1=acol[:, tau * 8 + h:tau * 8 + h + 1], s2=0.0,
                                op0=OP.add, op1=OP.min), vs)
                            ctx.wait("scalar", vs, vv)
                            ctx.wait("scalar", vs, ex_mark[par])
                            sv = ctx.inc("scalar", partial(
                                i_act, out=exv[:, 0:ncol], src=scrD[:, 0:ncol],
                                func=AF.Exp), ss)
                            scrD_mark = sv
                            ctx.wait("vector", ss, sv)
                            sv = ctx.inc("vector", partial(
                                i_tt, out=exv[:, 0:128], a=exv[:, 0:128],
                                b=cst["mbinA"][:, :], op=OP.mult), vs)
                            ctx.dr("vector")
                            exp_sem, exp_val = vs, sv
                        else:
                            ctx.wait("scalar", ss, sbc)
                            ctx.wait("scalar", vs, ex_mark[par])
                            sv = ctx.inc("scalar", partial(
                                i_act, out=exv[:, 0:ncol], src=BC[:, cols0:cols0 + ncol],
                                func=AF.Exp,
                                bias=acol[:, tau * 8 + h:tau * 8 + h + 1]), ss)
                            exp_sem, exp_val = ss, sv
                        ctx.wait("vector", ts, tv)
                        ctx.wait("vector", exp_sem, exp_val)
                        ctx.wait("vector", ts, ct_mark[par])
                        vv = ctx.inc("vector", partial(
                            i_tt, out=ctv[:, 0:ncol], a=qk.t[:, 0:ncol], b=exv[:, 0:ncol],
                            op=OP.mult), vs)
                        ex_mark[par] = vv
                        pp.put(qk, vs, vv)
                        ctx.wait("tensor", vs, vv)
                        fn1 = partial(i_mm, out=hacc.t[:, pad:pad + ncol],
                                      lhsT=vtok[:, t0:t0 + 128], rhs=ctv[:, 0:ncol],
                                      start=(tau == 0), stop=(tau == ntau - 1))
                        ctx.emit("tensor", fn1)
                        fn2 = partial(i_mm, out=racc.t[:, pad:pad + ncol],
                                      lhsT=cst["ones_1"][:, :], rhs=ctv[:, 0:ncol],
                                      start=(tau == 0), stop=(tau == ntau - 1))
                        if tau == ntau - 1:
                            tpv = ctx.inc("tensor", fn2, ts)
                        else:
                            ctx.emit("tensor", fn2)
                        ct_mark[par] = ts.n
                    ctx.wait("scalar", ts, tpv)
                    ctx.wait("scalar", vs, scr_v_mark)
                    s_abs = ctx.inc("scalar", partial(i_act, out=scr[:, :], src=racc.t[:, :],
                                                      func=AF.Abs), ss)
                    scr_s_mark = s_abs
                    ctx.wait("vector", ts, tpv)
                    ctx.wait("vector", ss, s_abs)
                    ctx.dr("vector")
                    ctx.inc("vector", partial(i_tt, out=scr[:, :], a=scr[:, :],
                                              b=EMV[:, s0:s0 + 512], op=OP.max), vs)
                    ctx.dr("vector")
                    vmx = ctx.inc("vector", partial(i_ts, out=scr[:, :], a=scr[:, :], s1=1e-6,
                                                    s2=None, op0=OP.add, op1=None), vs)
                    ctx.wait("scalar", vs, vmx)
                    ctx.inc("scalar", partial(i_act, out=scr[:, :], src=scr[:, :],
                                              func=AF.Ln), ss)
                    ctx.dr("scalar")
                    svn = ctx.inc("scalar", partial(i_act, out=scr[:, :], src=scr[:, :],
                                                    func=AF.Exp, scale=-1.0), ss)
                    scr_s_mark = svn
                    ctx.wait("vector", ss, svn)
                    vv = ctx.inc("vector", partial(
                        i_tt, out=hout[:, h * S + s0:h * S + s0 + 512], a=hacc.t[:, :],
                        b=scr[:, :], op=OP.mult), vs)
                    bc_mark = vv
                    scr_v_mark = vv
                    pp.put(hacc, vs, vv)
                    pp.put(racc, vs, vv)

                hb0 = h * S
                ctx.wait("vector", ts, qh_mark[0])
                ctx.dr("vector")
                vv = ctx.inc("vector", partial(i_tt, out=hsq[:, :], a=hout[:, hb0:hb0 + S],
                                               b=hout[:, hb0:hb0 + S], op=OP.mult), vs)
                gn_prev = bc_mark
                for st in range(NS):
                    sl = slice(st * 512, st * 512 + 512)
                    gl = slice(hb0 + st * 512, hb0 + st * 512 + 512)
                    mu = pp.get(); m2 = pp.get()
                    ctx.wait("tensor", vs, vv)
                    t1 = ctx.inc("tensor", partial(
                        i_mm, out=mu.t[:, :], lhsT=cst["ones_gn"][:, :], rhs=hout[:, gl],
                        start=True, stop=True), ts)
                    t2 = ctx.inc("tensor", partial(
                        i_mm, out=m2.t[:, :], lhsT=cst["ones_gn"][:, :], rhs=hsq[:, sl],
                        start=True, stop=True), ts)
                    ctx.wait("scalar", ts, t1)
                    ctx.wait("scalar", vs, gn_prev)
                    s1 = ctx.inc("scalar", partial(i_act, out=scr[:, :], src=mu.t[:, :],
                                                   func=AF.Square), ss)
                    ctx.wait("vector", ss, s1)
                    ctx.wait("vector", ts, t2)
                    v1 = ctx.inc("vector", partial(i_tt, out=scr[:, :], a=m2.t[:, :],
                                                   b=scr[:, :], op=OP.subtract), vs)
                    ctx.wait("scalar", vs, v1)
                    ctx.inc("scalar", partial(i_act, out=scr[:, :], src=scr[:, :],
                                              func=AF.Ln, bias=1e-6), ss)
                    ctx.dr("scalar")
                    s2 = ctx.inc("scalar", partial(i_act, out=scr[:, :], src=scr[:, :],
                                                   func=AF.Exp, scale=-0.5), ss)
                    scr_s_mark = s2
                    ctx.wait("vector", ss, s2)
                    v2 = ctx.inc("vector", partial(i_tt, out=scr2[:, :], a=hout[:, gl],
                                                   b=mu.t[:, :], op=OP.subtract), vs)
                    pp.put(mu, vs, v2)
                    pp.put(m2, vs, v1)
                    ctx.dr("vector")
                    ctx.inc("vector", partial(i_tt, out=scr2[:, :], a=scr2[:, :], b=scr[:, :],
                                              op=OP.mult), vs)
                    ctx.dr("vector")
                    gn_prev = ctx.inc("vector", partial(
                        i_stt, out=hout[:, gl], a=scr2[:, :],
                        s=cst["a_lnw"][:, h:h + 1], b=OGH[:, sl],
                        op0=OP.mult, op1=OP.mult), vs)
                    scr_v_mark = gn_prev
                qh_mark = (ts.n, gn_prev)
            vhout = vs.n
            sxo = self.sxo
            for e in range(NCH):
                ctx.wait_marks("sync", self.xres_w)
                ctx.wait("sync", sxo, sxo.n)
                vx = ctx.inc("sync", partial(
                    i_dma, out=Xc[:, :],
                    src=xres.rearrange("(c p) s -> p c s", p=128)[:, e, :]), sd, 16)

                def evac_res(pb, st, tv, e=e, vx=vx):
                    ctx.wait("vector", ts, tv)
                    ctx.wait("vector", sd, vx)
                    ctx.dr("vector")
                    vv = ctx.inc("vector", partial(
                        i_stt, out=Xc[:, st * 512:(st + 1) * 512],
                        a=pb.t[:, :], s=0.0,
                        b=Xc[:, st * 512:(st + 1) * 512],
                        op0=OP.add, op1=OP.add), vs)
                    pp.put(pb, vs, vv)
                self.proj_echunk(w["a_woutT"][e], hout, vhout, evac_res, ws=self.ws_b)
                ctx.wait("sync", vs, vs.n)
                vw = ctx.inc("sync", partial(
                    i_dma,
                    out=xres.rearrange("(c p) s -> p c s", p=128)[:, e, :],
                    src=Xc[:, :]), sxo, 16)
                if e == NCH - 1:
                    self.xres_w = [(sxo, vw)]
        ctx.flush()

    def inner_pass(self):
        for i in range(self.n_inner):
            if "iattn" not in DBG_SKIP:
                self.inner_attn(i)
            if "iffn" not in DBG_SKIP:
                self.inner_ffn(i)

    def shift_cast(self, Z, XS):
        S, ctx, vs = self.S, self.ctx, self.vs
        ctx.wait("vector", self.ts, getattr(self, "xs_free", 0))
        ctx.dr("vector")
        for c in range(NCH):
            if c < 4:
                ctx.inc("vector", partial(i_copy, out=XS[:, c * S:(c + 1) * S],
                                          src=Z[:, c * S:(c + 1) * S]), vs)
            else:
                ctx.inc("vector", partial(i_memset, ap=XS[:, c * S:c * S + 1], c=0.0), vs)
                ctx.inc("vector", partial(i_copy, out=XS[:, c * S + 1:(c + 1) * S],
                                          src=Z[:, c * S:(c + 1) * S - 1]), vs)
        return vs.n

    def inner_attn(self, li):
        S, NS, NT, NW, ctx = self.S, self.NS, self.NT, self.NW, self.ctx
        ts, vs, ss = self.ts, self.vs, self.ss
        pp, w, cst = self.pp, self.w, self.cst
        Z, XS = self.Z, self.XB
        vshift = self.shift_cast(Z, XS)

        with ExitStack() as stk:
            for n, shp in [("rot_cq", [128, S]), ("rot_sq", [128, S]),
                           ("rot_ck", [128, S]), ("rot_sk", [128, S]),
                           ("m0", [128, 512]), ("m1", [128, 512]),
                           ("m0w", [128, 256]), ("m1w", [128, 256]),
                           ("bdiag64", [128, 128])]:
                cst[n] = self.ld_const(n, shp, BF16, stk)
            cst["rotperm"] = self.ld_const("rotperm", [128, 128], F32, stk)
            self.wait_consts("tensor")
            self.wait_consts("scalar")
            self.wait_consts("vector")
            ATT = self.sb("iATT", [128, NCH * S], BF16, stk)
            uq = self.sb("iuq", [128, S], BF16, stk)
            uk = self.sb("iuk", [128, S], BF16, stk)
            ufb = [self.sb(f"iuf{i}", [128, 512], F32, stk) for i in range(2)]
            sqb = [self.sb(f"isq{i}", [128, 512], BF16, stk) for i in range(2)]
            rA = self.sb("irA", [128, 512], F32, stk)
            rB = self.sb("irB", [128, 512], F32, stk)
            tb = self.sb("itb", [128, 512], F32, stk)
            t2b = self.sb("it2b", [128, 512], F32, stk)
            sbf2 = self.sb("isbf2", [128, S], BF16, stk)
            vtok = self.sb("ivtok", [128, S], BF16, stk)
            Pb = [self.sb(f"iP{i}", [128, 512], BF16, stk) for i in range(2)]
            PTs = self.sb("iPTs", [128, 1024], BF16, stk)
            sums = self.sb("isums", [128, 2], F32, stk)
            rc = self.sb("irc", [128, 2], F32, stk)
            qs_col = cst["f_qs"][:, li:li + 1]
            ks_col = cst["f_ks"][:, li:li + 1]

            P_mark = [0, 0]
            att_done = 0
            chunk_ts = 0
            chunk_vs = 0
            uf_mark = [0, 0]
            sqm = [0, 0]
            r_mark = [0, 0]
            for c in range(NCH):
                wqkv = w["f_wqkv"][li]

                def qk_side(tile_idx, dst, scol, Ct, St):
                    slot = self.ws_a.load(wqkv[tile_idx])
                    wt = self.ws_a.use(slot)
                    for st in range(NS):
                        sl = slice(st * 512, st * 512 + 512)
                        par = st % 2
                        uf = ufb[par]
                        sq = sqb[par]
                        pb = pp.get()
                        ctx.wait("tensor", vs, vshift)
                        tv = None
                        for cc in range(NCH):
                            fn = partial(i_mm, out=pb.t[:, :],
                                         lhsT=wt[:, cc * 128:(cc + 1) * 128],
                                         rhs=XS[:, cc * S:(cc + 1) * S][:, sl],
                                         start=(cc == 0), stop=(cc == NCH - 1))
                            tv = ctx.inc("tensor", fn, ts) if cc == NCH - 1 \
                                else ctx.emit("tensor", fn)
                        ctx.wait("scalar", ts, tv)
                        ctx.wait("scalar", vs, uf_mark[par])
                        sv = ctx.inc("scalar", partial(
                            i_act, out=uf[:, :], src=pb.t[:, :], func=AF.Identity,
                            scale=scol), ss)
                        ctx.wait("scalar", ts, tv)
                        ctx.wait("scalar", ts, sqm[par])
                        ctx.dr("scalar")
                        v_sq = ctx.inc("scalar", partial(
                            i_act, out=sq[:, :], src=pb.t[:, :], func=AF.Square), ss)
                        prm = pp.get()
                        ctx.wait("tensor", ss, sv)
                        tpm = ctx.inc("tensor", partial(
                            i_mm, out=prm.t[:, :], lhsT=cst["rotperm"][:, :], rhs=uf[:, :],
                            start=True, stop=True), ts)
                        ssp = pp.get()
                        ctx.wait("tensor", ss, v_sq)
                        tss = ctx.inc("tensor", partial(
                            i_mm, out=ssp.t[:, :], lhsT=cst["bdiag64"][:, :], rhs=sq[:, :],
                            start=True, stop=True), ts)
                        sqm[par] = tss
                        pp.put(pb, ss, v_sq)
                        pb.free.append((ss, sv))
                        ctx.wait("vector", ts, tss)
                        ctx.wait("vector", ss, r_mark[0])
                        v1 = ctx.inc("vector", partial(
                            i_ts, out=rA[:, :], a=ssp.t[:, :], s1=1e-24, s2=None,
                            op0=OP.max, op1=None), vs)
                        pp.put(ssp, vs, v1)
                        ctx.wait("scalar", vs, v1)
                        ctx.wait("scalar", vs, uf_mark[0])
                        ctx.wait("scalar", vs, uf_mark[1])
                        ctx.inc("scalar", partial(
                            i_act, out=rB[:, :], src=rA[:, :], func=AF.Ln), ss)
                        ctx.dr("scalar")
                        s_sq = ctx.inc("scalar", partial(
                            i_act, out=rA[:, :], src=rB[:, :], func=AF.Exp, scale=-0.5), ss)
                        r_mark[0] = s_sq
                        ctx.wait("vector", ss, sv)
                        ctx.dr("vector")
                        ctx.inc("vector", partial(
                            i_tt, out=tb[:, :], a=uf[:, :], b=Ct[:, sl], op=OP.mult), vs)
                        ctx.wait("vector", ts, tpm)
                        ctx.inc("vector", partial(
                            i_tt, out=t2b[:, :], a=prm.t[:, :], b=St[:, sl], op=OP.mult), vs)
                        ctx.dr("vector")
                        v3 = ctx.inc("vector", partial(
                            i_tt, out=tb[:, :], a=tb[:, :], b=t2b[:, :], op=OP.add), vs)
                        pp.put(prm, vs, v3)
                        ctx.wait("vector", ss, r_mark[0])
                        ctx.dr("vector")
                        v4 = ctx.inc("vector", partial(
                            i_tt, out=dst[:, sl], a=tb[:, :], b=rA[:, :], op=OP.mult), vs)
                        uf_mark[par] = v4
                    self.ws_a.done(slot, ts, ts.n)
                    return vs.n

                ctx.wait("scalar", ts, chunk_ts)
                ctx.wait("scalar", vs, chunk_vs)
                vrq = qk_side(c, uq, qs_col, cst["rot_cq"], cst["rot_sq"])
                vrk = qk_side(8 + c, uk, ks_col, cst["rot_ck"], cst["rot_sk"])

                def evac_v(pb, st, tv):
                    sl = slice(st * 512, st * 512 + 512)
                    ctx.wait("scalar", ts, tv)
                    sv = ctx.inc("scalar", partial(i_act, out=sbf2[:, sl], src=pb.t[:, :],
                                                   func=AF.Copy), ss)
                    pp.put(pb, ss, sv)
                self.proj_echunk(wqkv[16 + c], XS, vshift, evac_v)
                sv_v = ss.n

                ctx.wait("tensor", ss, sv_v)
                for j0 in range(0, NT, 8):
                    ctx.wait("vector", ts, chunk_ts)
                    hi = min(j0 + 8, NT)
                    pb = pp.get(BF16)
                    tv = None
                    for j in range(j0, hi):
                        fn = partial(i_tr, out=pb.t[:, (j - j0) * 128:(j - j0 + 1) * 128],
                                     src=sbf2[:, j * 128:(j + 1) * 128], iden=cst["id_bf"][:, :])
                        tv = ctx.inc("tensor", fn, ts) if j == hi - 1 else ctx.emit("tensor", fn)
                    ctx.wait("vector", ts, tv)
                    vv = ctx.inc("vector", partial(
                        i_copy, out=vtok[:, j0 * 128:hi * 128], src=pb.t[:, 0:(hi - j0) * 128]), vs)
                    pp.put(pb, vs, vv)
                t_vtok = ts.n

                for hp in range(2):
                    hb = hp * 64
                    for wdw in range(NW):
                        nj = 256 if wdw == 0 else 512
                        jg0 = 0 if wdw == 0 else (wdw - 1) * 256
                        ptp = pp.get(BF16)
                        pvp = pp.get()
                        t_tr = None
                        for it in range(2):
                            i0 = wdw * 256 + it * 128
                            par = it
                            P = Pb[par]
                            scp = pp.get()
                            ctx.wait("tensor", vs, vrk)
                            ctx.emit("tensor", partial(
                                i_mm, out=scp.t[:, 0:nj],
                                lhsT=uq[hb:hb + 64, i0:i0 + 128],
                                rhs=uk[hb:hb + 64, jg0:jg0 + nj], start=True, stop=False))
                            mask = (cst["m0w"] if it == 0 else cst["m1w"]) if wdw == 0 else \
                                   (cst["m0"] if it == 0 else cst["m1"])
                            tv = ctx.inc("tensor", partial(
                                i_mm, out=scp.t[:, 0:nj], lhsT=cst["id_bf"][:, :],
                                rhs=mask[:, 0:nj], start=False, stop=True), ts)
                            ctx.wait("scalar", ts, tv)
                            ctx.wait("scalar", ts, P_mark[par])
                            sv = ctx.inc("scalar", partial(
                                i_act, out=P[:, 0:nj], src=scp.t[:, 0:nj], func=AF.Exp,
                                scale=8.0, accum=sums[:, it:it + 1]), ss)
                            pp.put(scp, ss, sv)
                            ctx.wait("vector", ss, sv)
                            ctx.inc("vector", partial(i_recip, out=rc[:, it:it + 1],
                                                      src=sums[:, it:it + 1]), vs)
                            ctx.dr("vector")
                            vn = ctx.inc("vector", partial(
                                i_ts, out=P[:, 0:nj], a=P[:, 0:nj],
                                s1=rc[:, it:it + 1], s2=None, op0=OP.mult, op1=None), vs)
                            ctx.wait("tensor", vs, vn)
                            for jc in range(nj // 128):
                                fn = partial(i_tr,
                                             out=ptp.t[:, jc * 256 + it * 128:
                                                       jc * 256 + it * 128 + 128],
                                             src=P[:, jc * 128:(jc + 1) * 128],
                                             iden=cst["id_bf"][:, :])
                                t_tr = ctx.inc("tensor", fn, ts) if jc == nj // 128 - 1 \
                                    else ctx.emit("tensor", fn)
                            P_mark[par] = t_tr
                        ctx.wait("vector", ts, t_tr)
                        vpt = ctx.inc("vector", partial(
                            i_copy, out=PTs[:, 0:2 * nj], src=ptp.t[:, 0:2 * nj]), vs)
                        pp.put(ptp, vs, vpt)
                        njc = nj // 128
                        tpv = None
                        ctx.wait("tensor", vs, vpt)
                        for jc in range(njc):
                            jt = jg0 // 128 + jc
                            fn = partial(i_mm, out=pvp.t[hb:hb + 64, 0:256],
                                         lhsT=vtok[:, jt * 128 + hb:jt * 128 + hb + 64],
                                         rhs=PTs[:, jc * 256:jc * 256 + 256],
                                         start=(jc == 0), stop=(jc == njc - 1))
                            tpv = ctx.inc("tensor", fn, ts) if jc == njc - 1 \
                                else ctx.emit("tensor", fn)
                        ctx.wait("scalar", ts, tpv)
                        sv = ctx.inc("scalar", partial(
                            i_act,
                            out=ATT[:, c * S:(c + 1) * S][hb:hb + 64,
                                                          wdw * 256:(wdw + 1) * 256],
                            src=pvp.t[hb:hb + 64, 0:256], func=AF.Copy), ss)
                        pp.put(pvp, ss, sv)
                        att_done = sv
                chunk_ts = ts.n
                chunk_vs = vs.n
                if c == NCH - 1:
                    self.xs_free = ts.n
            for e in range(NCH):
                def evac_resz(pb, st, tv, e=e):
                    ctx.wait("vector", ts, tv)
                    vv = ctx.inc("vector", partial(
                        i_stt,
                        out=Z[:, e * self.S:(e + 1) * self.S][:, st * 512:(st + 1) * 512],
                        a=pb.t[:, :], s=0.0,
                        b=Z[:, e * self.S:(e + 1) * self.S][:, st * 512:(st + 1) * 512],
                        op0=OP.add, op1=OP.add), vs)
                    pp.put(pb, vs, vv)
                ctx.wait("tensor", ss, att_done)
                self.proj_echunk(w["f_wo"][li, e], ATT, 0, evac_resz, ws=self.ws_b)
        ctx.flush()

    def inner_ffn(self, li):
        S, NS, ctx = self.S, self.NS, self.ctx
        ts, vs, ss = self.ts, self.vs, self.ss
        pp, w, cst = self.pp, self.w, self.cst
        Z, XS = self.Z, self.XB
        SH = min(S, 1024)
        nh = S // SH
        vshift = self.shift_cast(Z, XS)

        with ExitStack() as stk:
            ACT = self.sb("fACT", [128, NJ * SH], BF16, stk)
            Xq = self.sb("fXq", [128, NCH * SH], BF16, stk)
            A = self.sb("fA", [128, SH], BF16, stk)
            Bt = self.sb("fB", [128, SH], BF16, stk)
            asb = [self.sb(f"fasb{i}", [128, SH], BF16, stk) for i in range(2)]
            gsb = [self.sb(f"fgsb{i}", [128, SH], BF16, stk) for i in range(2)]
            sc = self.sb("fsc", [128, 512], F32, stk)
            ws_w2 = WStream(ctx, self.gd, f"wsw2_{li}_{ctx.n_stages}", [128, NJ * 128],
                            slots=2, stack=stk)
            g_col = cst["f_g"][:, li * 8:(li + 1) * 8]
            bln_col = cst["f_bln"][:, li * 8:(li + 1) * 8]
            b1a = cst["f_b1a"][:, li * NJ:(li + 1) * NJ]
            b1g = cst["f_b1g"][:, li * NJ:(li + 1) * NJ]
            b2c = cst["f_b2"][:, li * 8:(li + 1) * 8]

            ag_mark = [0, 0]
            half_ts = 0
            sc_s_mark = 0
            for hf in range(nh):
                h0 = hf * SH
                ctx.wait("vector", vs, vshift)
                ctx.wait("vector", ts, half_ts)
                for c in range(NCH):
                    ctx.inc("vector", partial(
                        i_tt, out=Xq[:, c * SH:(c + 1) * SH],
                        a=XS[:, c * S + h0:c * S + h0 + SH],
                        b=XS[:, c * S + h0:c * S + h0 + SH], op=OP.mult), vs)
                vsq = vs.n
                chain_prev = 0
                t_stats = None
                for st in range(SH // 512):
                    sl = slice(st * 512, st * 512 + 512)
                    gl = slice(h0 + st * 512, h0 + st * 512 + 512)
                    mu = pp.get(); m2 = pp.get()
                    ctx.wait("tensor", vs, vsq)
                    t1 = t2 = None
                    for c in range(NCH):
                        fn = partial(i_mm, out=mu.t[:, :], lhsT=cst["ones_ln"][:, :],
                                     rhs=XS[:, c * S:(c + 1) * S][:, gl],
                                     start=(c == 0), stop=(c == NCH - 1))
                        t1 = ctx.inc("tensor", fn, ts) if c == NCH - 1 else ctx.emit("tensor", fn)
                    for c in range(NCH):
                        fn = partial(i_mm, out=m2.t[:, :], lhsT=cst["ones_ln"][:, :],
                                     rhs=Xq[:, c * SH:(c + 1) * SH][:, sl],
                                     start=(c == 0), stop=(c == NCH - 1))
                        t2 = ctx.inc("tensor", fn, ts) if c == NCH - 1 else ctx.emit("tensor", fn)
                    t_stats = t2
                    ctx.wait("scalar", ts, t1)
                    ctx.wait("scalar", vs, chain_prev)
                    s1 = ctx.inc("scalar", partial(i_act, out=sc[:, :], src=mu.t[:, :],
                                                   func=AF.Square), ss)
                    ctx.wait("vector", ss, s1)
                    ctx.wait("vector", ts, t2)
                    v1 = ctx.inc("vector", partial(i_tt, out=sc[:, :], a=m2.t[:, :], b=sc[:, :],
                                                   op=OP.subtract), vs)
                    ctx.wait("scalar", vs, v1)
                    ctx.inc("scalar", partial(i_act, out=sc[:, :], src=sc[:, :],
                                              func=AF.Ln, bias=1e-5), ss)
                    ctx.dr("scalar")
                    s2 = ctx.inc("scalar", partial(i_act, out=A[:, sl], src=sc[:, :],
                                                   func=AF.Exp, scale=-0.5), ss)
                    sc_s_mark = s2
                    ctx.wait("vector", ss, s2)
                    v3 = ctx.inc("vector", partial(i_stt, out=Bt[:, sl], a=mu.t[:, :], s=-1.0,
                                                   b=A[:, sl], op0=OP.mult, op1=OP.mult), vs)
                    chain_prev = v3
                    pp.put(mu, vs, v3)
                    pp.put(m2, vs, v1)
                ctx.wait("vector", ts, t_stats)
                ctx.dr("vector")
                for c in range(NCH):
                    gls = slice(c * S + h0, c * S + h0 + SH)
                    ctx.inc("vector", partial(i_tt, out=Xq[:, c * SH:(c + 1) * SH],
                                              a=XS[:, gls], b=A[:, :], op=OP.mult), vs)
                    ctx.dr("vector")
                    ctx.inc("vector", partial(i_tt, out=Xq[:, c * SH:(c + 1) * SH],
                                              a=Xq[:, c * SH:(c + 1) * SH], b=Bt[:, :],
                                              op=OP.add), vs)
                    ctx.dr("vector")
                    ctx.inc("vector", partial(i_ts, out=XS[:, gls], a=Xq[:, c * SH:(c + 1) * SH],
                                              s1=g_col[:, c:c + 1], s2=bln_col[:, c:c + 1],
                                              op0=OP.mult, op1=OP.add), vs)
                vln = vs.n

                for j in range(NJ):
                    par = j % 2
                    sa = self.ws_a.load(w["f_w1"][li, j])
                    sg = self.ws_b.load(w["f_w1"][li, NJ + j])
                    wta = self.ws_a.use(sa)
                    wtg = self.ws_b.use(sg)
                    for st in range(SH // 512):
                        gl = slice(h0 + st * 512, h0 + st * 512 + 512)
                        pa = pp.get(); pg = pp.get()
                        ctx.wait("tensor", vs, vln)
                        ta = tg = None
                        for cc in range(NCH):
                            fn = partial(i_mm, out=pa.t[:, :],
                                         lhsT=wta[:, cc * 128:(cc + 1) * 128],
                                         rhs=XS[:, cc * S:(cc + 1) * S][:, gl],
                                         start=(cc == 0), stop=(cc == NCH - 1))
                            ta = ctx.inc("tensor", fn, ts) if cc == NCH - 1 \
                                else ctx.emit("tensor", fn)
                        for cc in range(NCH):
                            fn = partial(i_mm, out=pg.t[:, :],
                                         lhsT=wtg[:, cc * 128:(cc + 1) * 128],
                                         rhs=XS[:, cc * S:(cc + 1) * S][:, gl],
                                         start=(cc == 0), stop=(cc == NCH - 1))
                            tg = ctx.inc("tensor", fn, ts) if cc == NCH - 1 \
                                else ctx.emit("tensor", fn)
                        ctx.wait("scalar", ts, ta)
                        ctx.wait("scalar", vs, ag_mark[par])
                        s1 = ctx.inc("scalar", partial(
                            i_act, out=asb[par][:, st * 512:(st + 1) * 512], src=pa.t[:, :],
                            func=AF.Identity, bias=b1a[:, j:j + 1]), ss)
                        pp.put(pa, ss, s1)
                        ctx.wait("scalar", ts, tg)
                        s2 = ctx.inc("scalar", partial(
                            i_act, out=gsb[par][:, st * 512:(st + 1) * 512], src=pg.t[:, :],
                            func=AF.Gelu, bias=b1g[:, j:j + 1]), ss)
                        pp.put(pg, ss, s2)
                        ctx.wait("vector", ss, s2)
                        vv = ctx.inc("vector", partial(
                            i_tt, out=ACT[:, j * SH + st * 512:j * SH + (st + 1) * 512],
                            a=asb[par][:, st * 512:(st + 1) * 512],
                            b=gsb[par][:, st * 512:(st + 1) * 512], op=OP.mult), vs)
                        ag_mark[par] = vv
                    self.ws_a.done(sa, ts, ts.n)
                    self.ws_b.done(sg, ts, ts.n)
                vact = vs.n
                self.xs_free = ts.n

                for e in range(NCH):
                    slot = ws_w2.load(w["f_w2"][li, e])
                    wt = ws_w2.use(slot)
                    for st in range(SH // 512):
                        sl = slice(st * 512, st * 512 + 512)
                        pb = pp.get()
                        ctx.wait("tensor", vs, vact)
                        tv = None
                        for j in range(NJ):
                            fn = partial(i_mm, out=pb.t[:, :],
                                         lhsT=wt[:, j * 128:(j + 1) * 128],
                                         rhs=ACT[:, j * SH:(j + 1) * SH][:, sl],
                                         start=(j == 0), stop=(j == NJ - 1))
                            tv = ctx.inc("tensor", fn, ts) if j == NJ - 1 \
                                else ctx.emit("tensor", fn)
                        ctx.wait("vector", ts, tv)
                        ctx.wait("vector", ss, sc_s_mark)
                        ctx.dr("vector")
                        v1 = ctx.inc("vector", partial(
                            i_ts, out=sc[:, :], a=pb.t[:, :], s1=b2c[:, e:e + 1], s2=None,
                            op0=OP.add, op1=None), vs)
                        ctx.dr("vector")
                        ctx.inc("vector", partial(
                            i_tt, out=Z[:, e * S + h0:e * S + h0 + SH][:, sl], a=sc[:, :],
                            b=Z[:, e * S + h0:e * S + h0 + SH][:, sl], op=OP.add), vs)
                        pp.put(pb, vs, v1)
                    ws_w2.done(slot, ts, ts.n)
                half_ts = ts.n
        ctx.flush()

    def attn_b(self, xres):
        S, NS, NT, ctx = self.S, self.NS, self.NT, self.ctx
        ts, vs, ss, sd = self.ts, self.vs, self.ss, self.sx
        sxo = self.sxo
        pp, w, cst = self.pp, self.w, self.cst
        XN = self.XB
        OT = self.XB

        with ExitStack() as stk:
            QT = self.sb("bQT", [128, NCH * S], BF16, stk)
            KT = self.sb("bKT", [128, NCH * S], BF16, stk)
            vtok = self.sb("bvtok", [128, NCH * S], BF16, stk)
            TH = self.sb("bTH", [128, S], F32, stk)
            P = self.sb("bP", [128, S], BF16, stk)
            PT = self.sb("bPT", [128, 4 * S], BF16, stk)
            sc1 = self.sb("bsc1", [128, 512], F32, stk)
            sc2 = self.sb("bsc2", [128, 512], F32, stk)
            vtc = self.sb("bvtc", [128, S], BF16, stk)
            sums = self.sb("bsums", [128, 1], F32, stk)
            rc = self.sb("brc", [128, 1], F32, stk)
            Xc = self.sb("bXc", [128, S], F32, stk)

            ctx.wait("tensor", vs, self._lnfin)

            def mk_evac_b(dst, bcol, e):
                def ev(pb, st, tv):
                    sl = slice(st * 512, st * 512 + 512)
                    ctx.wait("scalar", ts, tv)
                    sv = ctx.inc("scalar", partial(
                        i_act, out=dst[:, e * S:(e + 1) * S][:, sl], src=pb.t[:, :],
                        func=AF.Identity, bias=bcol[:, e:e + 1]), ss)
                    pp.put(pb, ss, sv)
                return ev
            for e in range(NCH):
                self.proj_echunk(w["b_wqT"][e], XN, self._lnfin, mk_evac_b(QT, cst["b_bq"], e))
            for e in range(NCH):
                self.proj_echunk(w["b_wkT"][e], XN, self._lnfin, mk_evac_b(KT, cst["b_bk"], e))
            for e in range(NCH):
                def evv(pb, st, tv, e=e):
                    sl = slice(st * 512, st * 512 + 512)
                    ctx.wait("scalar", ts, tv)
                    sv = ctx.inc("scalar", partial(
                        i_act, out=vtc[:, sl], src=pb.t[:, :], func=AF.Identity,
                        bias=cst["b_bv"][:, e:e + 1]), ss)
                    pp.put(pb, ss, sv)
                self.proj_echunk(w["b_wvT"][e], XN, self._lnfin, evv)
                sve = ss.n
                ctx.wait("tensor", ss, sve)
                for j0 in range(0, NT, 8):
                    hi = min(j0 + 8, NT)
                    pb = pp.get(BF16)
                    tv = None
                    for j in range(j0, hi):
                        fn = partial(i_tr, out=pb.t[:, (j - j0) * 128:(j - j0 + 1) * 128],
                                     src=vtc[:, j * 128:(j + 1) * 128], iden=cst["id_bf"][:, :])
                        tv = ctx.inc("tensor", fn, ts) if j == hi - 1 else ctx.emit("tensor", fn)
                    ctx.wait("vector", ts, tv)
                    vv = None
                    for j in range(j0, hi):
                        vv = ctx.inc("vector", partial(
                            i_copy,
                            out=vtok[:, j * 1024 + e * 128:j * 1024 + (e + 1) * 128],
                            src=pb.t[:, (j - j0) * 128:(j - j0 + 1) * 128]), vs)
                    pp.put(pb, vs, vv)
            v_vtok = vs.n

            th_mark = 0
            p_mark = 0
            pt_mark = 0
            for g0 in range(0, NT, 4):
                ghi = min(g0 + 4, NT)
                for sg in range(g0, ghi):
                    i0 = sg * 128
                    for jt in range(NS):
                        jl = slice(jt * 512, jt * 512 + 512)
                        pb = pp.get()
                        tv = None
                        for c in range(NCH):
                            fn = partial(i_mm, out=pb.t[:, :],
                                         lhsT=QT[:, c * S + i0:c * S + i0 + 128],
                                         rhs=KT[:, c * S:(c + 1) * S][:, jl],
                                         start=(c == 0), stop=(c == NCH - 1))
                            tv = ctx.inc("tensor", fn, ts) if c == NCH - 1 \
                                else ctx.emit("tensor", fn)
                        ctx.wait("scalar", ts, tv)
                        ctx.wait("scalar", vs, th_mark)
                        s1 = ctx.inc("scalar", partial(i_act, out=sc1[:, :], src=pb.t[:, :],
                                                       func=AF.Tanh, scale=0.5), ss)
                        s2 = ctx.inc("scalar", partial(i_act, out=sc2[:, :], src=pb.t[:, :],
                                                       func=AF.Tanh), ss)
                        ctx.wait("vector", ss, s2)
                        v1 = ctx.inc("vector", partial(
                            i_stt, out=sc2[:, :], a=sc1[:, :], s=2.0 / 3.0, b=sc2[:, :],
                            op0=OP.mult, op1=OP.add), vs)
                        ctx.wait("scalar", vs, v1)
                        s3 = ctx.inc("scalar", partial(i_act, out=sc1[:, :], src=pb.t[:, :],
                                                       func=AF.Relu, scale=0.1), ss)
                        pp.put(pb, ss, s3)
                        ctx.wait("vector", ss, s3)
                        ctx.dr("vector")
                        th_mark = ctx.inc("vector", partial(
                            i_stt, out=TH[:, jl], a=sc2[:, :], s=0.15, b=sc1[:, :],
                            op0=OP.mult, op1=OP.add), vs)
                    ctx.wait("scalar", vs, th_mark)
                    ctx.wait("scalar", ts, p_mark)
                    sv = ctx.inc("scalar", partial(i_act, out=P[:, :], src=TH[:, :],
                                                   func=AF.Exp, accum=sums[:, 0:1]), ss)
                    ctx.wait("vector", ss, sv)
                    ctx.inc("vector", partial(i_recip, out=rc[:, 0:1], src=sums[:, 0:1]), vs)
                    ctx.dr("vector")
                    vn = ctx.inc("vector", partial(i_ts, out=P[:, :], a=P[:, :], s1=rc[:, 0:1],
                                                   s2=None, op0=OP.mult, op1=None), vs)
                    ctx.wait("tensor", vs, vn)
                    ctx.wait("tensor", vs, v_vtok)
                    ctx.wait("vector", ts, pt_mark)
                    for j0 in range(0, NT, 8):
                        hi = min(j0 + 8, NT)
                        pb = pp.get(BF16)
                        tv = None
                        for j in range(j0, hi):
                            fn = partial(i_tr, out=pb.t[:, (j - j0) * 128:(j - j0 + 1) * 128],
                                         src=P[:, j * 128:(j + 1) * 128],
                                         iden=cst["id_bf"][:, :])
                            tv = ctx.inc("tensor", fn, ts) if j == hi - 1 \
                                else ctx.emit("tensor", fn)
                        p_mark = tv
                        ctx.wait("vector", ts, tv)
                        vv = None
                        for j in range(j0, hi):
                            vv = ctx.inc("vector", partial(
                                i_copy,
                                out=PT[:, j * 512 + (sg - g0) * 128:
                                       j * 512 + (sg - g0 + 1) * 128],
                                src=pb.t[:, (j - j0) * 128:(j - j0 + 1) * 128]), vs)
                        pp.put(pb, vs, vv)
                vptg = vs.n
                gw = (ghi - g0) * 128
                gi0 = g0 * 128
                for e in range(NCH):
                    pb = pp.get()
                    ctx.wait("tensor", vs, vptg)
                    tv = None
                    for j in range(NT):
                        fn = partial(i_mm, out=pb.t[:, 0:gw],
                                     lhsT=vtok[:, j * 1024 + e * 128:j * 1024 + (e + 1) * 128],
                                     rhs=PT[:, j * 512:j * 512 + gw],
                                     start=(j == 0), stop=(j == NT - 1))
                        tv = ctx.inc("tensor", fn, ts) if j == NT - 1 else ctx.emit("tensor", fn)
                    pt_mark = tv
                    ctx.wait("scalar", ts, tv)
                    sv = ctx.inc("scalar", partial(
                        i_act, out=OT[:, e * S + gi0:e * S + gi0 + gw], src=pb.t[:, 0:gw],
                        func=AF.Copy), ss)
                    pp.put(pb, ss, sv)
            v_ot = ss.n

            for e in range(NCH):
                ctx.wait_marks("sync", self.xres_w)
                ctx.wait("sync", sxo, sxo.n)
                vx = ctx.inc("sync", partial(
                    i_dma, out=Xc[:, :],
                    src=xres.rearrange("(c p) s -> p c s", p=128)[:, e, :]),
                    sd, 16)
                def evr(pb, st, tv, e=e, vx=vx):
                    sl = slice(st * 512, st * 512 + 512)
                    ctx.wait("vector", ts, tv)
                    ctx.wait("vector", sd, vx)
                    v1 = ctx.inc("vector", partial(
                        i_ts, out=sc1[:, :], a=pb.t[:, :], s1=cst["b_bo"][:, e:e + 1],
                        s2=None, op0=OP.add, op1=None), vs)
                    ctx.dr("vector")
                    ctx.inc("vector", partial(i_tt, out=Xc[:, sl], a=sc1[:, :], b=Xc[:, sl],
                                              op=OP.add), vs)
                    pp.put(pb, vs, v1)
                ctx.wait("tensor", ss, v_ot)
                self.proj_echunk(w["b_woT"][e], OT, 0, evr, ws=self.ws_b)
                ctx.wait("sync", vs, vs.n)
                vw = ctx.inc("sync", partial(
                    i_dma,
                    out=xres.rearrange("(c p) s -> p c s", p=128)[:, e, :],
                    src=Xc[:, :]), sxo, 16)
                ctx.wait("sync", sxo, vw)
                if e == NCH - 1:
                    self.xres_w = [(sxo, vw)]
        ctx.flush()


_CACHE = {}


def build(S=2048, sublayers=("attn_a", "inner1", "attn_b", "inner2"), n_inner=DEPTH_IN):
    key = (S, tuple(sublayers), n_inner)
    if key not in _CACHE:
        b = Builder(S, tuple(sublayers), n_inner)
        b.nc.finalize()
        _CACHE[key] = b
    return _CACHE[key]


def make_inmaps(b, inputs, S=2048, n_cores=8):
    sh, per_core = host_prepare(inputs, S, n_cores)
    m = {}
    for wname in ("a_wqT", "a_wkT", "a_wvT", "a_wogT", "a_woutT", "a_wifT",
                  "b_wqT", "b_wkT", "b_wvT", "b_woT",
                  "f_wqkv", "f_wo", "f_w1", "f_w2"):
        m[wname] = sh[wname]
    for dname, key in b.feeds.items():
        arr = sh[key]
        if arr.ndim == 3 and key.startswith(("selA", "selV")) is False:
            pass
        m[dname] = arr
    return [dict(m, xin=pc["xin"]) for pc in per_core]


def kernel(**inputs):
    S = 2048
    n_cores = 8
    b = build(S)
    in_maps = make_inmaps(b, inputs, S, n_cores)
    res = run_bass_kernel_spmd(b.nc, in_maps, core_ids=list(range(n_cores)))
    outs = [np.asarray(r["yout"]).T for r in res.results]
    return np.ascontiguousarray(np.stack(outs, 0).astype(np.float32))


# version 07
# speedup vs baseline: 1.3435x; 1.2263x over previous
import math
from functools import partial
import numpy as np
import ml_dtypes
from contextlib import ExitStack

import concourse.bass as bass
import concourse.mybir as mybir
from concourse.alu_op_type import AluOpType as OP
from concourse.bass_utils import run_bass_kernel_spmd

F32 = mybir.dt.float32
BF16 = mybir.dt.bfloat16
AF = mybir.ActivationFunctionType

D = 1024
NCH = 8
NH_A = 8
DH_IN = 64
WIN = 256
DI = 2730
DIP = 2816
NJ = 22
DEPTH_IN = 6
NEG = -1e30
DBG_SKIP = set()
LNC_A = -0.5 * math.log(128.0)

bfd = ml_dtypes.bfloat16
ENGINES = ("tensor", "vector", "scalar", "gpsimd", "sync")


def i_mm(E, out, lhsT, rhs, start, stop):
    return E.matmul(out, lhsT, rhs, start=start, stop=stop)

def i_tr(E, out, src, iden):
    return E.transpose(out, src, iden)

def i_act(E, out, src, func, bias=0.0, scale=1.0, accum=None):
    return E.activation(out, src, func, bias=bias, scale=scale, accum_out=accum)

def i_tt(E, out, a, b, op):
    return E.tensor_tensor(out, a, b, op)

def i_ts(E, out, a, s1, s2, op0, op1):
    if s2 is None and op1 is None:
        return E.tensor_scalar(out, a, s1, None, op0)
    return E.tensor_scalar(out, a, s1, s2, op0, op1)

def i_stt(E, out, a, s, b, op0, op1):
    return E.scalar_tensor_tensor(out, a, s, b, op0, op1)

def i_scan(E, out, d0, d1, init, op0, op1):
    return E.tensor_tensor_scan(out, d0, d1, init, op0, op1)

def i_recip(E, out, src):
    return E.reciprocal(out, src)

def i_copy(E, out, src):
    return E.tensor_copy(out, src)

def i_memset(E, ap, c):
    return E.memset(ap, c)

def i_dma(E, out, src):
    return E.dma_start(out, src)

def i_wait(E, h, v):
    return E.wait_ge(h, v)

def i_drain(E):
    return E.drain()


class Sem:
    def __init__(self, handle, name):
        self.h = handle
        self.name = name
        self.n = 0


class Ctx:
    def __init__(self, nc):
        self.nc = nc
        self.code = {e: [] for e in ENGINES}
        self.last_wait = {}
        self.es = ExitStack()
        self.n_stages = 0
        self.n_instr = 0
        self.own_sem = {}

    def sem(self, name):
        return Sem(self.es.enter_context(self.nc.semaphore(name)), name)

    def emit(self, eng, fn):
        assert isinstance(fn, partial)
        self.n_instr += 1
        self.code[eng].append(fn)

    def wait(self, eng, sem, val):
        if val is None or val <= 0:
            return
        if val > sem.n:
            raise RuntimeError(f"wait {sem.name}>={val} but only {sem.n} emitted")
        key = (eng, sem.name)
        if self.last_wait.get(key, 0) >= val:
            return
        self.last_wait[key] = val
        self.code[eng].append(partial(i_wait, h=sem.h, v=val))

    def wait_marks(self, eng, marks):
        for sem, val in marks:
            self.wait(eng, sem, val)

    def dr(self, eng):
        sem = self.own_sem.get(eng)
        if sem is None:
            self.code[eng].append(partial(i_drain))
        else:
            self.wait(eng, sem, sem.n)

    def inc(self, eng, fn, sem, k=1):
        assert isinstance(fn, partial)
        self.n_instr += 1
        sem.n += k
        v = sem.n

        def run(E, fn=fn, h=sem.h, k=k):
            fn(E).then_inc(h, k)
        self.code[eng].append(run)
        return v

    def flush(self, final=False):
        if not final:
            return
        code = self.code
        self.code = {e: [] for e in ENGINES}
        if not any(code.values()):
            return
        self.n_stages += 1
        with self.nc.allow_low_precision(reason="bf16 storage by design"), \
                self.nc.Block() as blk:
            if code["gpsimd"]:
                @blk.gpsimd
                def _(E, ops=code["gpsimd"]):
                    for f in ops:
                        f(E)
            if code["sync"]:
                @blk.sync
                def _(E, ops=code["sync"]):
                    for f in ops:
                        f(E)
            if code["tensor"]:
                @blk.tensor
                def _(E, ops=code["tensor"]):
                    for f in ops:
                        f(E)
            if code["scalar"]:
                @blk.scalar
                def _(E, ops=code["scalar"]):
                    for f in ops:
                        f(E)
            if code["vector"]:
                @blk.vector
                def _(E, ops=code["vector"]):
                    for f in ops:
                        f(E)


class PBuf:
    __slots__ = ("t", "dtype", "free")

    def __init__(self, t, dtype):
        self.t = t
        self.dtype = dtype
        self.free = []


class PsumPool:
    def __init__(self, ctx):
        self.ctx = ctx
        nc = ctx.nc
        self.f32 = [PBuf(ctx.es.enter_context(nc.psum_tensor(f"psf{i}", [128, 512], F32)), F32)
                    for i in range(6)]
        self.b16 = [PBuf(ctx.es.enter_context(nc.psum_tensor(f"psb{i}", [128, 1024], BF16)), BF16)
                    for i in range(2)]

    def get(self, dtype=F32):
        pool = self.f32 if dtype == F32 else self.b16
        if not pool:
            raise RuntimeError("psum pool exhausted")
        b = pool.pop(0)
        self.ctx.wait_marks("tensor", b.free)
        b.free = []
        return b

    def put(self, b, sem, val):
        b.free.append((sem, val))
        (self.f32 if b.dtype == F32 else self.b16).append(b)


class WStream:
    def __init__(self, ctx, gd, name, shape, slots=3, stack=None):
        self.ctx = ctx
        stk = stack if stack is not None else ctx.es
        self.tiles = [stk.enter_context(ctx.nc.sbuf_tensor(f"{name}{i}", list(shape), BF16))
                      for i in range(slots)]
        self.sems = [ctx.sem(f"{name}{i}s") for i in range(slots)]
        self.slots = slots
        self.i = 0
        self.ready = [0] * slots
        self.free = [[] for _ in range(slots)]

    def load(self, dram_ap, ncols=None):
        s = self.i % self.slots
        self.i += 1
        self.ctx.wait_marks("gpsimd", self.free[s])
        dest = self.tiles[s][:, :] if ncols is None else self.tiles[s][:, 0:ncols]
        self.ready[s] = self.ctx.inc("gpsimd", partial(i_dma, out=dest, src=dram_ap),
                                     self.sems[s], 16)
        return s

    def use(self, s):
        self.ctx.wait("tensor", self.sems[s], self.ready[s])
        return self.tiles[s]

    def done(self, s, sem, val):
        self.free[s] = [(sem, val)]


def _cols(v):
    v = np.asarray(v, np.float32)
    k = v.size // 128
    return np.ascontiguousarray(v.reshape(k, 128).T)


def _rotary_tables(S):
    hd = np.arange(0, DH_IN, 2, dtype=np.float64)
    inv_freq = 1.0 / 10000.0 ** (hd / DH_IN)
    t = np.arange(S, dtype=np.float64)
    fh = t[:, None] * inv_freq[None, :]
    freqs = np.concatenate([fh, fh], -1)
    base = (hd + 0.4 * DH_IN) / (1.4 * DH_IN)
    power = (t - S // 2) / (WIN // 2)
    sch = base[None, :] ** power[:, None]
    scale = np.concatenate([sch, sch], -1)
    cos, sin = np.cos(freqs), np.sin(freqs)
    mk = lambda a: np.ascontiguousarray(np.concatenate([a.T, a.T], 0).astype(bfd))
    return (mk(cos * scale), mk(sin * scale),
            mk(cos / scale), mk(sin / scale))


def _masks():
    il = np.arange(128)[:, None]
    jl = np.arange(512)[None, :]
    m0 = np.where((jl >= il) & (jl <= il + 256), 0.0, NEG)
    m1 = np.where((jl >= il + 128) & (jl <= il + 384), 0.0, NEG)
    jl2 = np.arange(256)[None, :]
    m0w = np.where(jl2 <= il, 0.0, NEG)
    m1w = np.where(jl2 <= il + 128, 0.0, NEG)
    tb = np.arange(128)[:, None]
    sb2 = np.arange(128)[None, :]
    mbin = (tb <= sb2).astype(np.float32)
    c = lambda a: np.ascontiguousarray(a.astype(bfd))
    return c(m0), c(m1), c(m0w), c(m1w), c(mbin)


def _tile3(wT, ncol):
    K, M = wT.shape
    nc_ = K // 128
    nj = M // ncol
    t = wT.reshape(nc_, 128, nj, ncol).transpose(2, 1, 0, 3).reshape(nj, 128, nc_ * ncol)
    return np.ascontiguousarray(t)


def host_prepare(inputs, S, n_cores):
    f32 = np.float32
    sh = {}
    x = np.asarray(inputs["L0_attn_a_x"], f32)
    per_core = [{"xin": np.ascontiguousarray(x[c, :S, :].T)} for c in range(n_cores)]
    T = lambda a: np.ascontiguousarray(np.asarray(a, f32).T)

    sh["a_wqT"] = _tile3(T(inputs["L0_attn_a_Wq"]), 128)
    sh["a_wkT"] = _tile3(T(inputs["L0_attn_a_Wk"]), 128)
    sh["a_wvT"] = _tile3(T(inputs["L0_attn_a_Wv"]), 128)
    sh["a_wogT"] = _tile3(T(inputs["L0_attn_a_Wog"]), 128)
    sh["a_woutT"] = _tile3(T(inputs["L0_attn_a_Wout"]), 128)
    sh["a_wifT"] = _tile3(np.ascontiguousarray(
        np.concatenate([T(inputs["L0_attn_a_Wi"]), T(inputs["L0_attn_a_Wf"])], axis=1)), 16)
    bif = np.zeros((128, 1), f32)
    bif[0:8, 0] = np.asarray(inputs["L0_attn_a_bi"], f32) / 15.0
    bif[32:40, 0] = np.asarray(inputs["L0_attn_a_bf"], f32) / 15.0
    sh["a_bif"] = bif
    sh["a_lnw"] = _cols(inputs["L0_attn_a_ln_w"])

    sh["b_wqT"] = _tile3(T(inputs["L0_attn_b_Wq"]), 128)
    sh["b_wkT"] = _tile3(T(inputs["L0_attn_b_Wk"]), 128)
    sh["b_wvT"] = _tile3(T(inputs["L0_attn_b_Wv"]), 128)
    sh["b_woT"] = _tile3(T(inputs["L0_attn_b_Wo"]), 128)
    for nm in ("bq", "bk", "bv", "bo"):
        sh["b_" + nm] = _cols(inputs["L0_attn_b_" + nm])

    sh["f_wqkv"] = np.ascontiguousarray(np.stack(
        [_tile3(np.asarray(inputs["L0_ffn_Wqkv"], f32)[i], 128) for i in range(DEPTH_IN)]))
    sh["f_wo"] = np.ascontiguousarray(np.stack(
        [_tile3(np.asarray(inputs["L0_ffn_Wo"], f32)[i], 128) for i in range(DEPTH_IN)]))
    w1 = np.asarray(inputs["L0_ffn_W1"], f32)
    w1p = np.zeros((DEPTH_IN, D, 2 * DIP), f32)
    w1p[:, :, :DI] = w1[:, :, :DI]
    w1p[:, :, DIP:DIP + DI] = w1[:, :, DI:]
    sh["f_w1"] = np.ascontiguousarray(np.stack(
        [_tile3(w1p[i], 128) for i in range(DEPTH_IN)]))
    w2 = np.asarray(inputs["L0_ffn_W2"], f32)
    w2p = np.zeros((DEPTH_IN, DIP, D), f32)
    w2p[:, :DI, :] = w2
    sh["f_w2"] = np.ascontiguousarray(np.stack(
        [w2p[i].reshape(NJ, 128, 8, 128).transpose(2, 1, 0, 3).reshape(8, 128, NJ * 128)
         for i in range(DEPTH_IN)]))
    b1 = np.asarray(inputs["L0_ffn_b1"], f32)
    b1a = np.zeros((DEPTH_IN, DIP), f32); b1a[:, :DI] = b1[:, :DI]
    b1g = np.zeros((DEPTH_IN, DIP), f32); b1g[:, :DI] = b1[:, DI:]
    hstk = lambda lst: np.ascontiguousarray(np.concatenate(lst, axis=1))
    sh["f_b1a"] = hstk([_cols(b1a[i]) for i in range(DEPTH_IN)])
    sh["f_b1g"] = hstk([_cols(b1g[i]) for i in range(DEPTH_IN)])
    sh["f_b2"] = hstk([_cols(np.asarray(inputs["L0_ffn_b2"], f32)[i]) for i in range(DEPTH_IN)])
    sh["f_g"] = hstk([_cols(np.asarray(inputs["L0_ffn_ln_g"], f32)[i]) for i in range(DEPTH_IN)])
    sh["f_bln"] = hstk([_cols(np.asarray(inputs["L0_ffn_ln_b"], f32)[i]) for i in range(DEPTH_IN)])
    qs = np.asarray(inputs["L0_ffn_q_scale"], f32)
    ks = np.asarray(inputs["L0_ffn_k_scale"], f32)
    sh["f_qs"] = np.ascontiguousarray(np.concatenate([qs, qs], 1).T)
    sh["f_ks"] = np.ascontiguousarray(np.concatenate([ks, ks], 1).T)

    Cq, Sq, Ck, Sk = _rotary_tables(S)
    sh["rot_cq"], sh["rot_sq"], sh["rot_ck"], sh["rot_sk"] = Cq, Sq, Ck, Sk
    sh["m0"], sh["m1"], sh["m0w"], sh["m1w"], sh["mbinA"] = _masks()
    sh["id_bf"] = np.ascontiguousarray(np.eye(128, dtype=bfd))
    sh["id_f32"] = np.ascontiguousarray(np.eye(128, dtype=np.float32))
    sh["ones_1"] = np.full((128, 128), 1.0, bfd)
    sh["ones_gn"] = np.full((128, 128), 1.0 / 128.0, bfd)
    sh["ones_ln"] = np.full((128, 128), 1.0 / 1024.0, bfd)
    bd = np.zeros((128, 128), f32)
    bd[0:64, 0:64] = 1.0; bd[64:128, 64:128] = 1.0
    sh["bdiag64"] = np.ascontiguousarray(bd.astype(bfd))
    pm = np.zeros((128, 128), f32)
    for p in range(128):
        base = (p // 64) * 64
        dl = p % 64
        sgn = -1.0 if dl < 32 else 1.0
        sig = base + (dl + 32) % 64
        pm[sig, p] = sgn
    sh["rotperm"] = np.ascontiguousarray(pm)
    selA = np.zeros((NH_A, 128, 128), f32)
    selV = np.zeros((NH_A, 128, 128), f32)
    for h in range(NH_A):
        selA[h, h, :] = 1; selA[h, 32 + h, :] = 1
        selV[h, 64 + h, :] = 1; selV[h, 96 + h, :] = 1
    sh["selA"] = np.ascontiguousarray(selA.astype(bfd))
    sh["selV"] = np.ascontiguousarray(selV.astype(bfd))
    return sh, per_core


class Builder:
    def __init__(self, S, sublayers=("attn_a", "inner1", "attn_b", "inner2"),
                 n_inner=DEPTH_IN):
        self.S = S
        self.NS = S // 512
        self.NW = S // WIN
        self.NT = S // 128
        self.sublayers = sublayers
        self.n_inner = n_inner
        self.nc = bass.Bass("TRN2", target_bir_lowering=False)
        self.ctx = Ctx(self.nc)
        self._name_seq = 0
        self.feeds = {}
        self._build()

    def din(self, name, shape, dtype=F32):
        return self.nc.dram_tensor(name, list(shape), dtype, kind="ExternalInput").ap()

    def sb(self, name, shape, dtype, stack=None):
        stk = stack if stack is not None else self.ctx.es
        self._name_seq += 1
        return stk.enter_context(
            self.nc.sbuf_tensor(f"{name}_{self._name_seq}", list(shape), dtype))

    def ld_const(self, key, shape, dtype, stk=None):
        self._name_seq += 1
        dname = f"k_{key}_{self._name_seq}"
        self.feeds[dname] = key
        da = self.din(dname, shape, dtype)
        self.ctx.wait("sync", self.sx, self.sx.n)
        self.ctx.wait("sync", self.sxo, self.sxo.n)
        self.ctx.wait("sync", self.ts, self.ts.n)
        self.ctx.wait("sync", self.vs, self.vs.n)
        self.ctx.wait("sync", self.ss, self.ss.n)
        if len(shape) == 3:
            t = self.sb("c_" + key, [shape[1], shape[0] * shape[2]], dtype, stk)
            ap = da[:, :, :].rearrange("a b c -> b a c")
            dst = t[:, :].rearrange("b (a c) -> b a c", a=shape[0])
        else:
            t = self.sb("c_" + key, shape, dtype, stk)
            ap = da[:, :]
            dst = t[:, :]
        v = self.ctx.inc("sync", partial(i_dma, out=dst, src=ap), self.sd, 16)
        self._const_marks.append((self.sd, v))
        return t

    def wait_consts(self, eng):
        self.ctx.wait(eng, self.sd, self.sd.n)

    def next_xl(self):
        s = self.xl[self._xli % 4]
        self._xli += 1
        return s

    def sync_fence(self):
        for sem in (self.sx, self.sxo, self.ts, self.vs, self.ss):
            self.ctx.wait("sync", sem, sem.n)

    def _build(self):
        S, NS, ctx = self.S, self.NS, self.ctx
        self.gd = ctx.sem("gd")
        self.sd = ctx.sem("sd")
        self.sx = ctx.sem("sx")
        self.sxo = ctx.sem("sxo")
        self.ts = ctx.sem("ts")
        self.vs = ctx.sem("vs")
        self.ss = ctx.sem("ss")
        ctx.own_sem = {"tensor": self.ts, "vector": self.vs, "scalar": self.ss}
        self.xl = [ctx.sem(f"xl{i}") for i in range(4)]
        self._xli = 0
        sd = self.sx
        self.pp = PsumPool(ctx)
        self._const_marks = []
        self._cb_pending = True

        xin = self.din("xin", [D, S])
        yout = self.nc.dram_tensor("yout", [D, S], F32, kind="ExternalOutput").ap()
        cvals = [1e-5, 1e-6, LNC_A, 1.0 / 15.0, 15.0, 0.5, 0.1, 2.0 / 3.0, 0.15, 8.0,
                 -0.5, -1.0]
        cb = self.sb("cbias", [128, len(cvals)], F32)
        for i, val in enumerate(cvals):
            vv = ctx.inc("vector", partial(i_memset, ap=cb[:, i:i + 1], c=float(val)),
                         self.vs if hasattr(self, 'vs') else None, 1) if False else None
        self._cb = cb
        vlast = 0
        for i, val in enumerate(cvals):
            vlast = ctx.inc("vector", partial(i_memset, ap=cb[:, i:i + 1], c=float(val)),
                            self.vs, 1)
            self.nc.const_aps.register(F32, float(val), cb[:, i:i + 1])
        for e in ("tensor", "scalar", "gpsimd", "sync"):
            ctx.wait(e, self.vs, vlast)
        xres = self.nc.dram_tensor("xres", [D, S], F32, kind="Internal").ap()
        self.xres = xres
        self.xres_w = []

        self.w = {}
        for n, shp in [("a_wqT", [8, 128, D]), ("a_wkT", [1, 128, D]),
                       ("a_wvT", [1, 128, D]),
                       ("a_wogT", [8, 128, D]), ("a_woutT", [8, 128, D]),
                       ("a_wifT", [1, 128, 128]),
                       ("b_wqT", [8, 128, D]), ("b_wkT", [8, 128, D]),
                       ("b_wvT", [8, 128, D]), ("b_woT", [8, 128, D]),
                       ("f_wqkv", [DEPTH_IN, 24, 128, D]), ("f_wo", [DEPTH_IN, 8, 128, D]),
                       ("f_w1", [DEPTH_IN, 44, 128, D]),
                       ("f_w2", [DEPTH_IN, 8, 128, NJ * 128])]:
            self.w[n] = self.din(n, shp)

        self.cst = {}
        for n, shp, dt in [("a_bif", [128, 1], F32), ("a_lnw", [128, 8], F32),
                           ("b_bq", [128, 8], F32), ("b_bk", [128, 8], F32),
                           ("b_bv", [128, 8], F32), ("b_bo", [128, 8], F32),
                           ("f_b1a", [128, DEPTH_IN * NJ], F32),
                           ("f_b1g", [128, DEPTH_IN * NJ], F32),
                           ("f_b2", [128, DEPTH_IN * 8], F32),
                           ("f_g", [128, DEPTH_IN * 8], F32),
                           ("f_bln", [128, DEPTH_IN * 8], F32),
                           ("f_qs", [128, DEPTH_IN], F32),
                           ("f_ks", [128, DEPTH_IN], F32),
                           ("id_bf", [128, 128], BF16), ("ones_1", [128, 128], BF16),
                           ("ones_gn", [128, 128], BF16), ("ones_ln", [128, 128], BF16)]:
            self.cst[n] = self.ld_const(n, shp, dt)

        self.XB = self.sb("XB", [128, NCH * S], BF16)
        self.ws_a = WStream(ctx, self.gd, "ws_a", [128, 1024], slots=4)
        self.ws_b = WStream(ctx, self.gd, "ws_b", [128, 1024], slots=4)
        self.ws_w2 = None
        self.ws_s = WStream(ctx, self.gd, "ws_s", [128, 128], slots=2)

        with ExitStack() as stk:
            Xf = self.sb("Xf", [128, NCH * S], F32, stk)
            xsem = self.next_xl()
            v = ctx.inc("sync", partial(
                i_dma, out=Xf[:, :], src=xin.rearrange("(c p) s -> p c s", p=128)), xsem, 16)
            ctx.wait("sync", xsem, v)
            ctx.inc("sync", partial(
                i_dma, out=xres.rearrange("(c p) s -> p c s", p=128), src=Xf[:, :]),
                self.sxo, 16)
            self._xspill = self.sxo.n
            self.xres_w = [(self.sxo, self.sxo.n)]
            self.ln_stream(Xf, self.XB, None, 1e-5, wait=[(xsem, v)])
            ctx.flush()
        if "attn_a" in self.sublayers:
            self.attn_a(xres)
        ctx.flush()

        if "inner1" in self.sublayers:
            self.seed_Z(xres)
            self.inner_pass()
            self.residual_from_hbm(xres)
        if "attn_b" in self.sublayers:
            self.outer_ln_to_bf(xres)
            self.attn_b(xres)
        if "inner2" in self.sublayers:
            self.seed_Z(xres)
            self.inner_pass()
            self.residual_from_hbm(xres)
        ctx.flush()

        with ExitStack() as stk:
            Of = self.sb("Of", [128, NCH * S], F32, stk)
            ctx.wait_marks("sync", self.xres_w)
            self.sync_fence()
            xsem = self.next_xl()
            v = ctx.inc("sync", partial(
                i_dma, out=Of[:, :], src=xres.rearrange("(c p) s -> p c s", p=128)), xsem, 16)
            ctx.wait("sync", xsem, v)
            vo = ctx.inc("sync", partial(
                i_dma, out=yout.rearrange("(c p) s -> p c s", p=128), src=Of[:, :]), sd, 16)
            ctx.wait("sync", sd, vo)
            ctx.flush(final=True)
        ctx.es.close()

    def ln_stream(self, Xf, XNout, z_out, eps, wait=()):
        S, NS, ctx, pp = self.S, self.NS, self.ctx, self.pp
        ts, vs, ss = self.ts, self.vs, self.ss
        cst = self.cst
        with ExitStack() as stk:
            xbt = [self.sb(f"lnxb{i}", [128, 512], BF16, stk) for i in range(2)]
            sqb = [self.sb(f"lnsq{i}", [128, 512], BF16, stk) for i in range(2)]
            apl = self.sb("lnapl", [128, S], F32, stk)
            A = self.sb("lnA", [128, S], BF16, stk)
            B = self.sb("lnB", [128, S], BF16, stk)
            sc = self.sb("lnsc", [128, 512], F32, stk)
            self.wait_consts("vector")
            self.wait_consts("tensor")
            prev_chain = 0
            cast_mark = [0, 0]
            sq_mark = [0, 0]
            for st in range(NS):
                sl = slice(st * 512, st * 512 + 512)
                mu = pp.get(); m2 = pp.get()
                tmu = tm2 = None
                for c in range(NCH):
                    par = c % 2
                    ctx.wait_marks("vector", wait)
                    ctx.wait("vector", ts, cast_mark[par])
                    ctx.dr("vector")
                    vc = ctx.inc("vector", partial(
                        i_copy, out=xbt[par][:, :], src=Xf[:, c * S:(c + 1) * S][:, sl]), vs)
                    ctx.dr("vector")
                    vsq = ctx.inc("vector", partial(
                        i_tt, out=sqb[par][:, :], a=xbt[par][:, :], b=xbt[par][:, :],
                        op=OP.mult), vs)
                    ctx.wait("tensor", vs, vsq)
                    fn = partial(i_mm, out=mu.t[:, :], lhsT=cst["ones_ln"][:, :],
                                 rhs=xbt[par][:, :],
                                 start=(c == 0), stop=(c == NCH - 1))
                    tmu = ctx.inc("tensor", fn, ts) if c == NCH - 1 else ctx.emit("tensor", fn)
                    fn = partial(i_mm, out=m2.t[:, :], lhsT=cst["ones_ln"][:, :],
                                 rhs=sqb[par][:, :],
                                 start=(c == 0), stop=(c == NCH - 1))
                    tm2 = ctx.inc("tensor", fn, ts)
                    cast_mark[par] = tm2
                ctx.wait("scalar", ts, tmu)
                ctx.wait("scalar", vs, prev_chain)
                s1 = ctx.inc("scalar", partial(i_act, out=sc[:, :], src=mu.t[:, :],
                                               func=AF.Square), ss)
                ctx.wait("vector", ss, s1)
                ctx.wait("vector", ts, tm2)
                ctx.dr("vector")
                v1 = ctx.inc("vector", partial(i_tt, out=sc[:, :], a=m2.t[:, :], b=sc[:, :],
                                               op=OP.subtract), vs)
                ctx.wait("scalar", vs, v1)
                ctx.inc("scalar", partial(i_act, out=sc[:, :], src=sc[:, :],
                                          func=AF.Ln, bias=float(eps)), ss)
                ctx.dr("scalar")
                s2 = ctx.inc("scalar", partial(i_act, out=A[:, sl], src=sc[:, :],
                                               func=AF.Exp, scale=-0.5), ss)
                ctx.wait("vector", ss, s2)
                v3 = ctx.inc("vector", partial(i_stt, out=B[:, sl], a=mu.t[:, :], s=-1.0,
                                               b=A[:, sl], op0=OP.mult, op1=OP.mult), vs)
                prev_chain = v3
                pp.put(mu, vs, v3)
                pp.put(m2, vs, v1)
            ctx.dr("vector")
            for c in range(NCH):
                cs = slice(c * S, (c + 1) * S)
                dsts = []
                if XNout is not None:
                    dsts.append(XNout)
                if z_out is not None:
                    dsts.append(z_out)
                for dst in dsts:
                    ctx.dr("vector")
                    ctx.inc("vector", partial(i_tt, out=apl[:, :], a=Xf[:, cs], b=A[:, :],
                                              op=OP.mult), vs)
                    ctx.dr("vector")
                    ctx.inc("vector", partial(i_tt, out=dst[:, cs], a=apl[:, :], b=B[:, :],
                                              op=OP.add), vs)
            self._lnfin = vs.n
        ctx.flush()

    def seed_Z(self, xres):
        S, ctx, sd = self.S, self.ctx, self.sx
        self.Zstk = ExitStack()
        self.Z = self.sb("Z", [128, NCH * S], F32, self.Zstk)
        with ExitStack() as stk:
            Xf = self.sb("olnx", [128, NCH * S], F32, stk)
            ctx.wait_marks("sync", self.xres_w)
            self.sync_fence()
            xsem = self.next_xl()
            v = ctx.inc("sync", partial(
                i_dma, out=Xf[:, :], src=xres.rearrange("(c p) s -> p c s", p=128)), xsem, 16)
            self.ln_stream(Xf, None, self.Z, 1e-5, wait=[(xsem, v)])

    def outer_ln_to_bf(self, xres):
        S, ctx, sd = self.S, self.ctx, self.sx
        with ExitStack() as stk:
            Xf = self.sb("olnx2", [128, NCH * S], F32, stk)
            ctx.wait_marks("sync", self.xres_w)
            self.sync_fence()
            xsem = self.next_xl()
            v = ctx.inc("sync", partial(
                i_dma, out=Xf[:, :], src=xres.rearrange("(c p) s -> p c s", p=128)), xsem, 16)
            ctx.wait("vector", self.ts, getattr(self, "xs_free", 0))
            self.ln_stream(Xf, self.XB, None, 1e-5, wait=[(xsem, v)])

    def residual_from_hbm(self, xres):
        S, ctx, sd, vs = self.S, self.ctx, self.sx, self.vs
        with ExitStack() as stk:
            Xf = self.sb("resx", [128, NCH * S], F32, stk)
            ctx.wait_marks("sync", self.xres_w)
            self.sync_fence()
            xsem = self.next_xl()
            v = ctx.inc("sync", partial(
                i_dma, out=Xf[:, :], src=xres.rearrange("(c p) s -> p c s", p=128)), xsem, 16)
            ctx.wait("vector", xsem, v)
            vv = ctx.inc("vector", partial(i_tt, out=Xf[:, :], a=Xf[:, :], b=self.Z[:, :],
                                           op=OP.add), vs)
            ctx.wait("sync", vs, vv)
            vw = ctx.inc("sync", partial(
                i_dma, out=xres.rearrange("(c p) s -> p c s", p=128), src=Xf[:, :]),
                self.sxo, 16)
            ctx.wait("sync", self.sxo, vw)
            self.xres_w = [(self.sxo, vw)]
            ctx.flush()
        self.Zstk.close()

    def proj_echunk(self, wap, SRC, src_wait, evac, ncol=128, ws=None):
        ctx, pp, ts = self.ctx, self.pp, self.ts
        S, NS = self.S, self.NS
        ws = ws or self.ws_a
        slot = ws.load(wap, ncols=NCH * ncol)
        wt = ws.use(slot)
        for st in range(NS):
            sl = slice(st * 512, st * 512 + 512)
            pb = pp.get()
            ctx.wait("tensor", self.vs, src_wait)
            tv = None
            for c in range(NCH):
                fn = partial(i_mm, out=pb.t[0:ncol, :] if ncol < 128 else pb.t[:, :],
                             lhsT=wt[:, c * ncol:(c + 1) * ncol],
                             rhs=SRC[:, c * S:(c + 1) * S][:, sl],
                             start=(c == 0), stop=(c == NCH - 1))
                tv = ctx.inc("tensor", fn, ts) if c == NCH - 1 else ctx.emit("tensor", fn)
            evac(pb, st, tv)
        ws.done(slot, ts, ts.n)

    def attn_a(self, xres):
        S, NS, NT, ctx = self.S, self.NS, self.NT, self.ctx
        ts, vs, ss, sd = self.ts, self.vs, self.ss, self.sx
        pp, w, cst = self.pp, self.w, self.cst
        XN = self.XB
        with ExitStack() as stk:
            for n, shp, dt in [("mbinA", [128, 128], BF16), ("id_f32", [128, 128], F32),
                               ("selA", [NH_A, 128, 128], BF16),
                               ("selV", [NH_A, 128, 128], BF16)]:
                cst[n] = self.ld_const(n, shp, dt, stk)
            self.wait_consts("tensor")
            self.wait_consts("scalar")
            self.wait_consts("vector")
            KT = self.sb("aKT", [128, S], BF16, stk)
            VT = self.sb("aVT", [128, S], BF16, stk)
            vtok = self.sb("avtok", [128, S], BF16, stk)
            hl = self.sb("ahl", [128, S], BF16, stk)
            Xc = self.sb("aXc", [128, S], F32, stk)
            acol = self.sb("aacol", [128, NT * 8], F32, stk)
            gstk = ExitStack()
            lf = self.sb("alf", [8, S], F32, gstk)
            csum = self.sb("acs", [8, S], F32, gstk)
            cmx = self.sb("acmx", [8, S], F32, gstk)
            arow = self.sb("aarow", [8, S], F32, gstk)
            zrow = self.sb("azrow", [8, S], F32, gstk)
            vmrow = self.sb("avm", [8, S], F32, gstk)
            g16i = self.sb("agi", [8, S], F32, gstk)
            g16f = self.sb("agf", [8, S], F32, gstk)
            hi8 = self.sb("ahi8", [8, S], F32, gstk)
            lo8 = self.sb("alo8", [8, S], F32, gstk)
            ctx.wait("tensor", vs, self._lnfin)
            self.wait_consts("tensor")

            def mk_evac_plain(dst):
                def e(pb, st, tv):
                    ctx.wait("scalar", ts, tv)
                    sv = ctx.inc("scalar", partial(
                        i_act, out=dst[:, st * 512:(st + 1) * 512], src=pb.t[:, :],
                        func=AF.Copy), ss)
                    pp.put(pb, ss, sv)
                return e
            self.proj_echunk(w["a_wkT"][0], XN, self._lnfin, mk_evac_plain(KT))
            self.proj_echunk(w["a_wvT"][0], XN, self._lnfin, mk_evac_plain(VT))

            def evac_g(pb, st, tv):
                ctx.wait("scalar", ts, tv)
                self.wait_consts("scalar")
                ctx.inc("scalar", partial(
                    i_act, out=g16i[:, st * 512:(st + 1) * 512], src=pb.t[0:8, :],
                    func=AF.Tanh, bias=cst["a_bif"][0:8, 0:1], scale=1.0 / 15.0), ss)
                sv = ctx.inc("scalar", partial(
                    i_act, out=g16f[:, st * 512:(st + 1) * 512], src=pb.t[32:40, :],
                    func=AF.Tanh, bias=cst["a_bif"][32:40, 0:1], scale=1.0 / 15.0), ss)
                pp.put(pb, ss, sv)

            slot_g = self.ws_s.load(w["a_wifT"][0], ncols=NCH * 16)
            wt_g = self.ws_s.use(slot_g)
            for st in range(NS):
                sl = slice(st * 512, st * 512 + 512)
                pb = pp.get()
                ctx.wait("tensor", vs, self._lnfin)
                tv = None
                for c in range(NCH):
                    fni = partial(i_mm, out=pb.t[0:8, :], lhsT=wt_g[:, c * 16:c * 16 + 8],
                                  rhs=XN[:, c * S:(c + 1) * S][:, sl],
                                  start=(c == 0), stop=(c == NCH - 1))
                    ctx.emit("tensor", fni)
                for c in range(NCH):
                    fnf = partial(i_mm, out=pb.t[32:40, :], lhsT=wt_g[:, c * 16 + 8:c * 16 + 16],
                                  rhs=XN[:, c * S:(c + 1) * S][:, sl],
                                  start=(c == 0), stop=(c == NCH - 1))
                    tv = ctx.inc("tensor", fnf, ts) if c == NCH - 1 else ctx.emit("tensor", fnf)
                evac_g(pb, st, tv)
            self.ws_s.done(slot_g, ts, ts.n)

            vz = ctx.inc("vector", partial(i_memset, ap=zrow[:, :], c=0.0), vs)
            vz = ctx.inc("vector", partial(i_memset, ap=hl[:, :], c=0.0), vs)
            ctx.wait("scalar", ss, ss.n)
            ctx.inc("scalar", partial(i_act, out=lf[:, :], src=g16f[:, :],
                                      func=AF.Sigmoid, scale=15.0), ss)
            ctx.dr("scalar")
            s_ln = ctx.inc("scalar", partial(i_act, out=lf[:, :], src=lf[:, :], func=AF.Ln), ss)
            ctx.wait("vector", ss, s_ln)
            ctx.wait("vector", vs, vz)
            ctx.inc("vector", partial(i_scan, out=csum[:, :], d0=lf[:, :], d1=zrow[0:8, :],
                                      init=0.0, op0=OP.add, op1=OP.bypass), vs)
            ctx.dr("vector")
            v_a = ctx.inc("vector", partial(i_stt, out=arow[:, :], a=g16i[:, :], s=15.0,
                                            b=csum[:, :], op0=OP.mult, op1=OP.subtract), vs)
            ctx.dr("vector")
            ctx.inc("vector", partial(i_scan, out=cmx[:, :], d0=arow[:, :], d1=zrow[0:8, :],
                                      init=-1e30, op0=OP.max, op1=OP.bypass), vs)
            ctx.dr("vector")
            ctx.inc("vector", partial(i_ts, out=hi8[:, :], a=cmx[:, :], s1=-1.0, s2=None,
                                      op0=OP.mult, op1=None), vs)
            ctx.dr("vector")
            ctx.inc("vector", partial(i_copy, out=hl[0:8, :], src=hi8[:, :]), vs)
            ctx.inc("vector", partial(i_stt, out=lo8[:, :], a=cmx[:, :], s=-1.0,
                                      b=hi8[:, :], op0=OP.mult, op1=OP.subtract), vs)
            ctx.dr("vector")
            v_bc1 = ctx.inc("vector", partial(i_copy, out=hl[32:40, :], src=lo8[:, :]), vs)
            ctx.inc("vector", partial(i_tt, out=vmrow[:, :], a=csum[:, :], b=cmx[:, :],
                                      op=OP.add), vs)
            ctx.dr("vector")
            ctx.inc("vector", partial(i_ts, out=hi8[:, :], a=vmrow[:, :], s1=-1.0, s2=None,
                                      op0=OP.mult, op1=None), vs)
            ctx.dr("vector")
            ctx.inc("vector", partial(i_copy, out=hl[64:72, :], src=hi8[:, :]), vs)
            ctx.inc("vector", partial(i_stt, out=lo8[:, :], a=vmrow[:, :], s=-1.0,
                                      b=hi8[:, :], op0=OP.mult, op1=OP.subtract), vs)
            ctx.dr("vector")
            v_bc2 = ctx.inc("vector", partial(i_copy, out=hl[96:104, :], src=lo8[:, :]), vs)

            ctx.wait("tensor", vs, v_a)
            for t0 in range(0, NT, 8):
                hi = min(t0 + 8, NT)
                pb = pp.get()
                tv = None
                for tt in range(t0, hi):
                    fn = partial(i_tr, out=pb.t[:, (tt - t0) * 8:(tt - t0) * 8 + 8],
                                 src=arow[:, tt * 128:(tt + 1) * 128],
                                 iden=cst["id_f32"][0:8, 0:8])
                    tv = ctx.inc("tensor", fn, ts) if tt == hi - 1 else ctx.emit("tensor", fn)
                ctx.wait("scalar", ts, tv)
                sv = ctx.inc("scalar", partial(
                    i_act, out=acol[:, t0 * 8:hi * 8], src=pb.t[:, 0:(hi - t0) * 8],
                    func=AF.Copy, bias=LNC_A), ss)
                pp.put(pb, ss, sv)

            gstk.close()
            BC = self.sb("aBC", [128, S], F32, stk)
            EMV = self.sb("aEMV", [128, S], F32, stk)
            QH = self.sb("aQH", [128, S], BF16, stk)
            OGH = self.sb("aOGH", [128, S], BF16, stk)
            hout = self.sb("ahout", [128, NCH * S], BF16, stk)
            hsq = self.sb("ahsq", [128, S], BF16, stk)
            ctb = [self.sb(f"act_{i}", [128, 512], BF16, stk) for i in range(2)]
            exb = [self.sb(f"aex_{i}", [128, 512], F32, stk) for i in range(2)]
            scrD = self.sb("ascrD", [128, 512], F32, stk)
            scr = self.sb("ascr", [128, 512], F32, stk)
            scr2 = self.sb("ascr2", [128, 512], F32, stk)
            ctx.wait("vector", ts, ts.n)
            ctx.wait("scalar", ts, ts.n)
            ctx.wait("scalar", vs, vs.n)
            ctx.wait("tensor", vs, vs.n)
            ctx.wait("tensor", ss, ss.n)
            ctx.wait("vector", ss, ss.n)
            ctx.dr("vector")
            ctx.dr("scalar")
            ctx.wait("tensor", ss, ss.n)
            for j0 in range(0, NT, 8):
                hi = min(j0 + 8, NT)
                pb = pp.get(BF16)
                tv = None
                for j in range(j0, hi):
                    fn = partial(i_tr, out=pb.t[:, (j - j0) * 128:(j - j0 + 1) * 128],
                                 src=VT[:, j * 128:(j + 1) * 128], iden=cst["id_bf"][:, :])
                    tv = ctx.inc("tensor", fn, ts) if j == hi - 1 else ctx.emit("tensor", fn)
                ctx.wait("vector", ts, tv)
                vv = ctx.inc("vector", partial(
                    i_copy, out=vtok[:, j0 * 128:hi * 128], src=pb.t[:, 0:(hi - j0) * 128]), vs)
                pp.put(pb, vs, vv)

            ct_mark = [0, 0]
            ex_mark = [0, 0]
            scrD_mark = 0
            scr_s_mark = 0
            scr_v_mark = 0
            bc_mark = 0
            qh_mark = (0, 0)
            for h in range(NH_A):
                def evac_q(pb, st, tv, dst=QH):
                    ctx.wait("scalar", ts, tv)
                    ctx.wait("scalar", ts, qh_mark[0])
                    ctx.wait("scalar", vs, qh_mark[1])
                    sv = ctx.inc("scalar", partial(
                        i_act, out=dst[:, st * 512:(st + 1) * 512], src=pb.t[:, :],
                        func=AF.Copy), ss)
                    pp.put(pb, ss, sv)
                def evac_og(pb, st, tv, dst=OGH):
                    ctx.wait("scalar", ts, tv)
                    sv = ctx.inc("scalar", partial(
                        i_act, out=dst[:, st * 512:(st + 1) * 512], src=pb.t[:, :],
                        func=AF.Sigmoid), ss)
                    pp.put(pb, ss, sv)
                self.proj_echunk(w["a_wqT"][h], XN, self._lnfin, evac_q)
                self.proj_echunk(w["a_wogT"][h], XN, self._lnfin, evac_og, ws=self.ws_b)
                sqh = ss.n

                for st in range(NS):
                    sl = slice(st * 512, st * 512 + 512)
                    pb = pp.get()
                    ctx.wait("tensor", vs, v_bc2)
                    tv = ctx.inc("tensor", partial(
                        i_mm, out=pb.t[:, :], lhsT=cst["selA"][:, h * 128:(h + 1) * 128],
                        rhs=hl[:, sl], start=True, stop=True), ts)
                    ctx.wait("scalar", ts, tv)
                    ctx.wait("scalar", vs, bc_mark)
                    sv = ctx.inc("scalar", partial(
                        i_act, out=BC[:, sl], src=pb.t[:, :], func=AF.Copy), ss)
                    pp.put(pb, ss, sv)
                    pb = pp.get()
                    ctx.wait("tensor", vs, v_bc2)
                    tv = ctx.inc("tensor", partial(
                        i_mm, out=pb.t[:, :], lhsT=cst["selV"][:, h * 128:(h + 1) * 128],
                        rhs=hl[:, sl], start=True, stop=True), ts)
                    ctx.wait("scalar", ts, tv)
                    sv = ctx.inc("scalar", partial(
                        i_act, out=EMV[:, sl], src=pb.t[:, :], func=AF.Exp), ss)
                    pp.put(pb, ss, sv)
                ctx.dr("scalar")
                sbc = ss.n

                for sp in range(NS):
                    s0 = sp * 512
                    hacc = pp.get()
                    racc = pp.get()
                    ntau = 4 * (sp + 1)
                    tpv = None
                    pend = None

                    def emit_pv(pend, last):
                        tau, ctv, vv, pad, ncol = pend
                        ctx.wait("tensor", vs, vv)
                        fn1 = partial(i_mm, out=hacc.t[:, pad:pad + ncol],
                                      lhsT=vtok[:, tau * 128:(tau + 1) * 128],
                                      rhs=ctv[:, 0:ncol],
                                      start=(tau == 0), stop=last)
                        ctx.emit("tensor", fn1)
                        fn2 = partial(i_mm, out=racc.t[:, pad:pad + ncol],
                                      lhsT=cst["ones_1"][:, :], rhs=ctv[:, 0:ncol],
                                      start=(tau == 0), stop=last)
                        if last:
                            return ctx.inc("tensor", fn2, ts)
                        ctx.emit("tensor", fn2)
                        return ts.n

                    for tau in range(ntau):
                        t0 = tau * 128
                        cols0 = max(s0, t0)
                        ncol = s0 + 512 - cols0
                        pad = cols0 - s0
                        diag = t0 >= s0
                        par = tau % 2
                        ctv = ctb[par]
                        exv = exb[par]
                        qk = pp.get()
                        ctx.wait("tensor", ss, sqh)
                        tv = ctx.inc("tensor", partial(
                            i_mm, out=qk.t[:, 0:ncol],
                            lhsT=KT[:, t0:t0 + 128],
                            rhs=QH[:, cols0:cols0 + ncol], start=True, stop=True), ts)
                        if diag:
                            ctx.wait("vector", ss, sbc)
                            ctx.wait("vector", ss, scrD_mark)
                            vv = ctx.inc("vector", partial(
                                i_ts, out=scrD[:, 0:ncol], a=BC[:, cols0:cols0 + ncol],
                                s1=acol[:, tau * 8 + h:tau * 8 + h + 1], s2=0.0,
                                op0=OP.add, op1=OP.min), vs)
                            ctx.wait("scalar", vs, vv)
                            ctx.wait("scalar", vs, ex_mark[par])
                            sv = ctx.inc("scalar", partial(
                                i_act, out=exv[:, 0:ncol], src=scrD[:, 0:ncol],
                                func=AF.Exp), ss)
                            scrD_mark = sv
                            ctx.wait("vector", ss, sv)
                            sv = ctx.inc("vector", partial(
                                i_tt, out=exv[:, 0:128], a=exv[:, 0:128],
                                b=cst["mbinA"][:, :], op=OP.mult), vs)
                            ctx.dr("vector")
                            exp_sem, exp_val = vs, sv
                        else:
                            ctx.wait("scalar", ss, sbc)
                            ctx.wait("scalar", vs, ex_mark[par])
                            sv = ctx.inc("scalar", partial(
                                i_act, out=exv[:, 0:ncol], src=BC[:, cols0:cols0 + ncol],
                                func=AF.Exp,
                                bias=acol[:, tau * 8 + h:tau * 8 + h + 1]), ss)
                            exp_sem, exp_val = ss, sv
                        ctx.wait("vector", ts, tv)
                        ctx.wait("vector", exp_sem, exp_val)
                        ctx.wait("vector", ts, ct_mark[par])
                        vv = ctx.inc("vector", partial(
                            i_tt, out=ctv[:, 0:ncol], a=qk.t[:, 0:ncol], b=exv[:, 0:ncol],
                            op=OP.mult), vs)
                        ex_mark[par] = vv
                        pp.put(qk, vs, vv)
                        if pend is not None:
                            emit_pv(pend, False)
                            ct_mark[pend[0] % 2] = ts.n
                        pend = (tau, ctv, vv, pad, ncol)
                    tpv = emit_pv(pend, True)
                    ct_mark[pend[0] % 2] = tpv
                    ctx.wait("scalar", ts, tpv)
                    ctx.wait("scalar", vs, scr_v_mark)
                    s_abs = ctx.inc("scalar", partial(i_act, out=scr[:, :], src=racc.t[:, :],
                                                      func=AF.Abs), ss)
                    scr_s_mark = s_abs
                    ctx.wait("vector", ts, tpv)
                    ctx.wait("vector", ss, s_abs)
                    ctx.dr("vector")
                    ctx.inc("vector", partial(i_tt, out=scr[:, :], a=scr[:, :],
                                              b=EMV[:, s0:s0 + 512], op=OP.max), vs)
                    ctx.dr("vector")
                    vmx = ctx.inc("vector", partial(i_ts, out=scr[:, :], a=scr[:, :], s1=1e-6,
                                                    s2=None, op0=OP.add, op1=None), vs)
                    ctx.wait("scalar", vs, vmx)
                    ctx.inc("scalar", partial(i_act, out=scr[:, :], src=scr[:, :],
                                              func=AF.Ln), ss)
                    ctx.dr("scalar")
                    svn = ctx.inc("scalar", partial(i_act, out=scr[:, :], src=scr[:, :],
                                                    func=AF.Exp, scale=-1.0), ss)
                    scr_s_mark = svn
                    ctx.wait("vector", ss, svn)
                    vv = ctx.inc("vector", partial(
                        i_tt, out=hout[:, h * S + s0:h * S + s0 + 512], a=hacc.t[:, :],
                        b=scr[:, :], op=OP.mult), vs)
                    bc_mark = vv
                    scr_v_mark = vv
                    pp.put(hacc, vs, vv)
                    pp.put(racc, vs, vv)

                hb0 = h * S
                ctx.wait("vector", ts, qh_mark[0])
                ctx.dr("vector")
                vv = ctx.inc("vector", partial(i_tt, out=hsq[:, :], a=hout[:, hb0:hb0 + S],
                                               b=hout[:, hb0:hb0 + S], op=OP.mult), vs)
                gn_prev = bc_mark
                for st in range(NS):
                    sl = slice(st * 512, st * 512 + 512)
                    gl = slice(hb0 + st * 512, hb0 + st * 512 + 512)
                    mu = pp.get(); m2 = pp.get()
                    ctx.wait("tensor", vs, vv)
                    t1 = ctx.inc("tensor", partial(
                        i_mm, out=mu.t[:, :], lhsT=cst["ones_gn"][:, :], rhs=hout[:, gl],
                        start=True, stop=True), ts)
                    t2 = ctx.inc("tensor", partial(
                        i_mm, out=m2.t[:, :], lhsT=cst["ones_gn"][:, :], rhs=hsq[:, sl],
                        start=True, stop=True), ts)
                    ctx.wait("scalar", ts, t1)
                    ctx.wait("scalar", vs, gn_prev)
                    s1 = ctx.inc("scalar", partial(i_act, out=scr[:, :], src=mu.t[:, :],
                                                   func=AF.Square), ss)
                    ctx.wait("vector", ss, s1)
                    ctx.wait("vector", ts, t2)
                    v1 = ctx.inc("vector", partial(i_tt, out=scr[:, :], a=m2.t[:, :],
                                                   b=scr[:, :], op=OP.subtract), vs)
                    ctx.wait("scalar", vs, v1)
                    ctx.inc("scalar", partial(i_act, out=scr[:, :], src=scr[:, :],
                                              func=AF.Ln, bias=1e-6), ss)
                    ctx.dr("scalar")
                    s2 = ctx.inc("scalar", partial(i_act, out=scr[:, :], src=scr[:, :],
                                                   func=AF.Exp, scale=-0.5), ss)
                    scr_s_mark = s2
                    ctx.wait("vector", ss, s2)
                    v2 = ctx.inc("vector", partial(i_tt, out=scr2[:, :], a=hout[:, gl],
                                                   b=mu.t[:, :], op=OP.subtract), vs)
                    pp.put(mu, vs, v2)
                    pp.put(m2, vs, v1)
                    ctx.dr("vector")
                    ctx.inc("vector", partial(i_tt, out=scr2[:, :], a=scr2[:, :], b=scr[:, :],
                                              op=OP.mult), vs)
                    ctx.dr("vector")
                    gn_prev = ctx.inc("vector", partial(
                        i_stt, out=hout[:, gl], a=scr2[:, :],
                        s=cst["a_lnw"][:, h:h + 1], b=OGH[:, sl],
                        op0=OP.mult, op1=OP.mult), vs)
                    scr_v_mark = gn_prev
                qh_mark = (ts.n, gn_prev)
            vhout = vs.n
            sxo = self.sxo
            ctx.wait("vector", self.sxo, self._xspill)
            for e in range(NCH):
                ctx.wait_marks("sync", self.xres_w)
                ctx.wait("sync", sxo, sxo.n)
                xsem = self.next_xl()
                vx = ctx.inc("sync", partial(
                    i_dma, out=Xc[:, :],
                    src=xres.rearrange("(c p) s -> p c s", p=128)[:, e, :]), xsem, 16)

                def evac_res(pb, st, tv, e=e, vx=vx, xsem=xsem):
                    ctx.wait("vector", ts, tv)
                    ctx.wait("vector", xsem, vx)
                    ctx.dr("vector")
                    vv = ctx.inc("vector", partial(
                        i_stt, out=Xc[:, st * 512:(st + 1) * 512],
                        a=pb.t[:, :], s=0.0,
                        b=Xc[:, st * 512:(st + 1) * 512],
                        op0=OP.add, op1=OP.add), vs)
                    pp.put(pb, vs, vv)
                self.proj_echunk(w["a_woutT"][e], hout, vhout, evac_res, ws=self.ws_b)
                ctx.wait("sync", vs, vs.n)
                vw = ctx.inc("sync", partial(
                    i_dma,
                    out=xres.rearrange("(c p) s -> p c s", p=128)[:, e, :],
                    src=Xc[:, :]), sxo, 16)
                if e == NCH - 1:
                    self.xres_w = [(sxo, vw)]
        ctx.flush()

    def inner_pass(self):
        for i in range(self.n_inner):
            if "iattn" not in DBG_SKIP:
                self.inner_attn(i)
            if "iffn" not in DBG_SKIP:
                self.inner_ffn(i)

    def shift_cast(self, Z, XS):
        S, ctx, vs = self.S, self.ctx, self.vs
        ctx.wait("vector", self.ts, getattr(self, "xs_free", 0))
        ctx.dr("vector")
        for c in range(NCH):
            if c < 4:
                ctx.inc("vector", partial(i_copy, out=XS[:, c * S:(c + 1) * S],
                                          src=Z[:, c * S:(c + 1) * S]), vs)
            else:
                ctx.inc("vector", partial(i_memset, ap=XS[:, c * S:c * S + 1], c=0.0), vs)
                ctx.inc("vector", partial(i_copy, out=XS[:, c * S + 1:(c + 1) * S],
                                          src=Z[:, c * S:(c + 1) * S - 1]), vs)
        return vs.n

    def inner_attn(self, li):
        S, NS, NT, NW, ctx = self.S, self.NS, self.NT, self.NW, self.ctx
        ts, vs, ss = self.ts, self.vs, self.ss
        pp, w, cst = self.pp, self.w, self.cst
        Z, XS = self.Z, self.XB
        vshift = self.shift_cast(Z, XS)

        with ExitStack() as stk:
            for n, shp in [("rot_cq", [128, S]), ("rot_sq", [128, S]),
                           ("rot_ck", [128, S]), ("rot_sk", [128, S]),
                           ("m0", [128, 512]), ("m1", [128, 512]),
                           ("m0w", [128, 256]), ("m1w", [128, 256]),
                           ("bdiag64", [128, 128])]:
                cst[n] = self.ld_const(n, shp, BF16, stk)
            cst["rotperm"] = self.ld_const("rotperm", [128, 128], F32, stk)
            self.wait_consts("tensor")
            self.wait_consts("scalar")
            self.wait_consts("vector")
            ATT = self.sb("iATT", [128, NCH * S], BF16, stk)
            uq = self.sb("iuq", [128, S], BF16, stk)
            uk = self.sb("iuk", [128, S], BF16, stk)
            ufb = [self.sb(f"iuf{i}", [128, 512], F32, stk) for i in range(2)]
            sqb = [self.sb(f"isq{i}", [128, 512], BF16, stk) for i in range(2)]
            rA = self.sb("irA", [128, 512], F32, stk)
            rB = self.sb("irB", [128, 512], F32, stk)
            tb = self.sb("itb", [128, 512], F32, stk)
            t2b = self.sb("it2b", [128, 512], F32, stk)
            sbf2 = self.sb("isbf2", [128, S], BF16, stk)
            vtok = self.sb("ivtok", [128, S], BF16, stk)
            Pb = [self.sb(f"iP{i}", [128, 1024], BF16, stk) for i in range(2)]
            PTs = [self.sb(f"iPTs{i}", [128, 1024], BF16, stk) for i in range(2)]
            sums = self.sb("isums", [128, 4], F32, stk)
            rc = self.sb("irc", [128, 4], F32, stk)
            qs_col = cst["f_qs"][:, li:li + 1]
            ks_col = cst["f_ks"][:, li:li + 1]

            P_mark = [0, 0]
            att_done = 0
            chunk_ts = 0
            chunk_vs = 0
            uf_mark = [0, 0]
            sqm = [0, 0]
            r_mark = [0, 0]
            for c in range(NCH):
                wqkv = w["f_wqkv"][li]

                def qk_side(tile_idx, dst, scol, Ct, St):
                    slot = self.ws_a.load(wqkv[tile_idx])
                    wt = self.ws_a.use(slot)
                    for st in range(NS):
                        sl = slice(st * 512, st * 512 + 512)
                        par = st % 2
                        uf = ufb[par]
                        sq = sqb[par]
                        pb = pp.get()
                        ctx.wait("tensor", vs, vshift)
                        tv = None
                        for cc in range(NCH):
                            fn = partial(i_mm, out=pb.t[:, :],
                                         lhsT=wt[:, cc * 128:(cc + 1) * 128],
                                         rhs=XS[:, cc * S:(cc + 1) * S][:, sl],
                                         start=(cc == 0), stop=(cc == NCH - 1))
                            tv = ctx.inc("tensor", fn, ts) if cc == NCH - 1 \
                                else ctx.emit("tensor", fn)
                        ctx.wait("scalar", ts, tv)
                        ctx.wait("scalar", vs, uf_mark[par])
                        sv = ctx.inc("scalar", partial(
                            i_act, out=uf[:, :], src=pb.t[:, :], func=AF.Identity,
                            scale=scol), ss)
                        ctx.wait("scalar", ts, tv)
                        ctx.wait("scalar", ts, sqm[par])
                        ctx.dr("scalar")
                        v_sq = ctx.inc("scalar", partial(
                            i_act, out=sq[:, :], src=pb.t[:, :], func=AF.Square), ss)
                        prm = pp.get()
                        ctx.wait("tensor", ss, sv)
                        tpm = ctx.inc("tensor", partial(
                            i_mm, out=prm.t[:, :], lhsT=cst["rotperm"][:, :], rhs=uf[:, :],
                            start=True, stop=True), ts)
                        ssp = pp.get()
                        ctx.wait("tensor", ss, v_sq)
                        tss = ctx.inc("tensor", partial(
                            i_mm, out=ssp.t[:, :], lhsT=cst["bdiag64"][:, :], rhs=sq[:, :],
                            start=True, stop=True), ts)
                        sqm[par] = tss
                        pp.put(pb, ss, v_sq)
                        pb.free.append((ss, sv))
                        ctx.wait("vector", ts, tss)
                        ctx.wait("vector", ss, r_mark[0])
                        v1 = ctx.inc("vector", partial(
                            i_ts, out=rA[:, :], a=ssp.t[:, :], s1=1e-24, s2=None,
                            op0=OP.max, op1=None), vs)
                        pp.put(ssp, vs, v1)
                        ctx.wait("scalar", vs, v1)
                        ctx.wait("scalar", vs, uf_mark[0])
                        ctx.wait("scalar", vs, uf_mark[1])
                        ctx.inc("scalar", partial(
                            i_act, out=rB[:, :], src=rA[:, :], func=AF.Ln), ss)
                        ctx.dr("scalar")
                        s_sq = ctx.inc("scalar", partial(
                            i_act, out=rA[:, :], src=rB[:, :], func=AF.Exp, scale=-0.5), ss)
                        r_mark[0] = s_sq
                        ctx.wait("vector", ss, sv)
                        ctx.dr("vector")
                        ctx.inc("vector", partial(
                            i_tt, out=tb[:, :], a=uf[:, :], b=Ct[:, sl], op=OP.mult), vs)
                        ctx.wait("vector", ts, tpm)
                        ctx.inc("vector", partial(
                            i_tt, out=t2b[:, :], a=prm.t[:, :], b=St[:, sl], op=OP.mult), vs)
                        ctx.dr("vector")
                        v3 = ctx.inc("vector", partial(
                            i_tt, out=tb[:, :], a=tb[:, :], b=t2b[:, :], op=OP.add), vs)
                        pp.put(prm, vs, v3)
                        ctx.wait("vector", ss, r_mark[0])
                        ctx.dr("vector")
                        v4 = ctx.inc("vector", partial(
                            i_tt, out=dst[:, sl], a=tb[:, :], b=rA[:, :], op=OP.mult), vs)
                        uf_mark[par] = v4
                    self.ws_a.done(slot, ts, ts.n)
                    return vs.n

                ctx.wait("scalar", ts, chunk_ts)
                ctx.wait("scalar", vs, chunk_vs)
                vrq = qk_side(c, uq, qs_col, cst["rot_cq"], cst["rot_sq"])
                vrk = qk_side(8 + c, uk, ks_col, cst["rot_ck"], cst["rot_sk"])

                def evac_v(pb, st, tv):
                    sl = slice(st * 512, st * 512 + 512)
                    ctx.wait("scalar", ts, tv)
                    sv = ctx.inc("scalar", partial(i_act, out=sbf2[:, sl], src=pb.t[:, :],
                                                   func=AF.Copy), ss)
                    pp.put(pb, ss, sv)
                self.proj_echunk(wqkv[16 + c], XS, vshift, evac_v)
                sv_v = ss.n

                ctx.wait("tensor", ss, sv_v)
                for j0 in range(0, NT, 8):
                    ctx.wait("vector", ts, chunk_ts)
                    hi = min(j0 + 8, NT)
                    pb = pp.get(BF16)
                    tv = None
                    for j in range(j0, hi):
                        fn = partial(i_tr, out=pb.t[:, (j - j0) * 128:(j - j0 + 1) * 128],
                                     src=sbf2[:, j * 128:(j + 1) * 128], iden=cst["id_bf"][:, :])
                        tv = ctx.inc("tensor", fn, ts) if j == hi - 1 else ctx.emit("tensor", fn)
                    ctx.wait("vector", ts, tv)
                    vv = ctx.inc("vector", partial(
                        i_copy, out=vtok[:, j0 * 128:hi * 128], src=pb.t[:, 0:(hi - j0) * 128]), vs)
                    pp.put(pb, vs, vv)
                t_vtok = ts.n

                jobs = [(hp, wdw) for hp in range(2) for wdw in range(NW)]
                pts_mark = [0, 0]
                sum_mark = [0, 0, 0, 0]

                def emit_tr_pv(job):
                    k, hp, wdw, nj, jg0, vnorm = job
                    par = k % 2
                    hb = hp * 64
                    P = Pb[par]
                    ptp = pp.get(BF16)
                    ctx.wait("tensor", vs, vnorm)
                    t_tr = None
                    for it in range(2):
                        for jc in range(nj // 128):
                            fn = partial(i_tr,
                                         out=ptp.t[:, jc * 256 + it * 128:
                                                   jc * 256 + it * 128 + 128],
                                         src=P[:, it * 512 + jc * 128:
                                               it * 512 + (jc + 1) * 128],
                                         iden=cst["id_bf"][:, :])
                            t_tr = ctx.inc("tensor", fn, ts) if (it == 1 and
                                                                 jc == nj // 128 - 1) \
                                else ctx.emit("tensor", fn)
                    P_mark[par] = t_tr
                    ctx.wait("vector", ts, t_tr)
                    ctx.wait("vector", ts, pts_mark[par])
                    vpt = ctx.inc("vector", partial(
                        i_copy, out=PTs[par][:, 0:2 * nj], src=ptp.t[:, 0:2 * nj]), vs)
                    pp.put(ptp, vs, vpt)
                    pvp = pp.get()
                    njc = nj // 128
                    tpv = None
                    ctx.wait("tensor", vs, vpt)
                    for jc in range(njc):
                        jt = jg0 // 128 + jc
                        fn = partial(i_mm, out=pvp.t[hb:hb + 64, 0:256],
                                     lhsT=vtok[:, jt * 128 + hb:jt * 128 + hb + 64],
                                     rhs=PTs[par][:, jc * 256:jc * 256 + 256],
                                     start=(jc == 0), stop=(jc == njc - 1))
                        tpv = ctx.inc("tensor", fn, ts) if jc == njc - 1 \
                            else ctx.emit("tensor", fn)
                    pts_mark[par] = tpv
                    ctx.wait("scalar", ts, tpv)
                    sv = ctx.inc("scalar", partial(
                        i_act,
                        out=ATT[:, c * S:(c + 1) * S][hb:hb + 64,
                                                      wdw * 256:(wdw + 1) * 256],
                        src=pvp.t[hb:hb + 64, 0:256], func=AF.Copy), ss)
                    pp.put(pvp, ss, sv)
                    return sv

                pend = None
                for k, (hp, wdw) in enumerate(jobs):
                    nj = 256 if wdw == 0 else 512
                    jg0 = 0 if wdw == 0 else (wdw - 1) * 256
                    par = k % 2
                    hb = hp * 64
                    P = Pb[par]
                    for it in range(2):
                        i0 = wdw * 256 + it * 128
                        scol = par * 2 + it
                        scp = pp.get()
                        ctx.wait("tensor", vs, vrk)
                        ctx.emit("tensor", partial(
                            i_mm, out=scp.t[:, 0:nj],
                            lhsT=uq[hb:hb + 64, i0:i0 + 128],
                            rhs=uk[hb:hb + 64, jg0:jg0 + nj], start=True, stop=False))
                        mask = (cst["m0w"] if it == 0 else cst["m1w"]) if wdw == 0 else \
                               (cst["m0"] if it == 0 else cst["m1"])
                        tv = ctx.inc("tensor", partial(
                            i_mm, out=scp.t[:, 0:nj], lhsT=cst["id_bf"][:, :],
                            rhs=mask[:, 0:nj], start=False, stop=True), ts)
                        ctx.wait("scalar", ts, tv)
                        ctx.wait("scalar", ts, P_mark[par])
                        ctx.wait("scalar", vs, sum_mark[scol])
                        sv = ctx.inc("scalar", partial(
                            i_act, out=P[:, it * 512:it * 512 + nj], src=scp.t[:, 0:nj],
                            func=AF.Exp, scale=8.0,
                            accum=sums[:, scol:scol + 1]), ss)
                        pp.put(scp, ss, sv)
                        ctx.wait("vector", ss, sv)
                        vr = ctx.inc("vector", partial(i_recip, out=rc[:, scol:scol + 1],
                                                       src=sums[:, scol:scol + 1]), vs)
                        sum_mark[scol] = vr
                        ctx.dr("vector")
                        ctx.inc("vector", partial(
                            i_ts, out=P[:, it * 512:it * 512 + nj],
                            a=P[:, it * 512:it * 512 + nj],
                            s1=rc[:, scol:scol + 1], s2=None, op0=OP.mult, op1=None), vs)
                    vnorm = vs.n
                    if pend is not None:
                        att_done = emit_tr_pv(pend)
                    pend = (k, hp, wdw, nj, jg0, vnorm)
                att_done = emit_tr_pv(pend)
                chunk_ts = ts.n
                chunk_vs = vs.n
                if c == NCH - 1:
                    self.xs_free = ts.n
            for e in range(NCH):
                def evac_resz(pb, st, tv, e=e):
                    ctx.wait("vector", ts, tv)
                    vv = ctx.inc("vector", partial(
                        i_stt,
                        out=Z[:, e * self.S:(e + 1) * self.S][:, st * 512:(st + 1) * 512],
                        a=pb.t[:, :], s=0.0,
                        b=Z[:, e * self.S:(e + 1) * self.S][:, st * 512:(st + 1) * 512],
                        op0=OP.add, op1=OP.add), vs)
                    pp.put(pb, vs, vv)
                ctx.wait("tensor", ss, att_done)
                self.proj_echunk(w["f_wo"][li, e], ATT, 0, evac_resz, ws=self.ws_b)
        ctx.flush()

    def inner_ffn(self, li):
        S, NS, ctx = self.S, self.NS, self.ctx
        ts, vs, ss = self.ts, self.vs, self.ss
        pp, w, cst = self.pp, self.w, self.cst
        Z, XS = self.Z, self.XB
        SH = min(S, 1024)
        nh = S // SH
        vshift = self.shift_cast(Z, XS)

        with ExitStack() as stk:
            ACT = self.sb("fACT", [128, NJ * SH], BF16, stk)
            Xq = self.sb("fXq", [128, NCH * SH], BF16, stk)
            A = self.sb("fA", [128, SH], BF16, stk)
            Bt = self.sb("fB", [128, SH], BF16, stk)
            asb = [self.sb(f"fasb{i}", [128, SH], BF16, stk) for i in range(2)]
            gsb = [self.sb(f"fgsb{i}", [128, SH], BF16, stk) for i in range(2)]
            sc = self.sb("fsc", [128, 512], F32, stk)
            self._name_seq += 1
            ws_w2 = WStream(ctx, self.gd, f"wsw2_{self._name_seq}", [128, NJ * 128],
                            slots=2, stack=stk)
            g_col = cst["f_g"][:, li * 8:(li + 1) * 8]
            bln_col = cst["f_bln"][:, li * 8:(li + 1) * 8]
            b1a = cst["f_b1a"][:, li * NJ:(li + 1) * NJ]
            b1g = cst["f_b1g"][:, li * NJ:(li + 1) * NJ]
            b2c = cst["f_b2"][:, li * 8:(li + 1) * 8]

            ag_mark = [0, 0]
            half_ts = 0
            sc_s_mark = 0
            for hf in range(nh):
                h0 = hf * SH
                ctx.wait("vector", vs, vshift)
                ctx.wait("vector", ts, half_ts)
                for c in range(NCH):
                    ctx.inc("vector", partial(
                        i_tt, out=Xq[:, c * SH:(c + 1) * SH],
                        a=XS[:, c * S + h0:c * S + h0 + SH],
                        b=XS[:, c * S + h0:c * S + h0 + SH], op=OP.mult), vs)
                vsq = vs.n
                chain_prev = 0
                t_stats = None
                for st in range(SH // 512):
                    sl = slice(st * 512, st * 512 + 512)
                    gl = slice(h0 + st * 512, h0 + st * 512 + 512)
                    mu = pp.get(); m2 = pp.get()
                    ctx.wait("tensor", vs, vsq)
                    t1 = t2 = None
                    for c in range(NCH):
                        fn = partial(i_mm, out=mu.t[:, :], lhsT=cst["ones_ln"][:, :],
                                     rhs=XS[:, c * S:(c + 1) * S][:, gl],
                                     start=(c == 0), stop=(c == NCH - 1))
                        t1 = ctx.inc("tensor", fn, ts) if c == NCH - 1 else ctx.emit("tensor", fn)
                    for c in range(NCH):
                        fn = partial(i_mm, out=m2.t[:, :], lhsT=cst["ones_ln"][:, :],
                                     rhs=Xq[:, c * SH:(c + 1) * SH][:, sl],
                                     start=(c == 0), stop=(c == NCH - 1))
                        t2 = ctx.inc("tensor", fn, ts) if c == NCH - 1 else ctx.emit("tensor", fn)
                    t_stats = t2
                    ctx.wait("scalar", ts, t1)
                    ctx.wait("scalar", vs, chain_prev)
                    s1 = ctx.inc("scalar", partial(i_act, out=sc[:, :], src=mu.t[:, :],
                                                   func=AF.Square), ss)
                    ctx.wait("vector", ss, s1)
                    ctx.wait("vector", ts, t2)
                    v1 = ctx.inc("vector", partial(i_tt, out=sc[:, :], a=m2.t[:, :], b=sc[:, :],
                                                   op=OP.subtract), vs)
                    ctx.wait("scalar", vs, v1)
                    ctx.inc("scalar", partial(i_act, out=sc[:, :], src=sc[:, :],
                                              func=AF.Ln, bias=1e-5), ss)
                    ctx.dr("scalar")
                    s2 = ctx.inc("scalar", partial(i_act, out=A[:, sl], src=sc[:, :],
                                                   func=AF.Exp, scale=-0.5), ss)
                    sc_s_mark = s2
                    ctx.wait("vector", ss, s2)
                    v3 = ctx.inc("vector", partial(i_stt, out=Bt[:, sl], a=mu.t[:, :], s=-1.0,
                                                   b=A[:, sl], op0=OP.mult, op1=OP.mult), vs)
                    chain_prev = v3
                    pp.put(mu, vs, v3)
                    pp.put(m2, vs, v1)
                ctx.wait("vector", ts, t_stats)
                ctx.dr("vector")
                for c in range(NCH):
                    gls = slice(c * S + h0, c * S + h0 + SH)
                    ctx.inc("vector", partial(i_tt, out=Xq[:, c * SH:(c + 1) * SH],
                                              a=XS[:, gls], b=A[:, :], op=OP.mult), vs)
                    ctx.dr("vector")
                    ctx.inc("vector", partial(i_tt, out=Xq[:, c * SH:(c + 1) * SH],
                                              a=Xq[:, c * SH:(c + 1) * SH], b=Bt[:, :],
                                              op=OP.add), vs)
                    ctx.dr("vector")
                    ctx.inc("vector", partial(i_ts, out=XS[:, gls], a=Xq[:, c * SH:(c + 1) * SH],
                                              s1=g_col[:, c:c + 1], s2=bln_col[:, c:c + 1],
                                              op0=OP.mult, op1=OP.add), vs)
                vln = vs.n

                for j in range(NJ):
                    par = j % 2
                    sa = self.ws_a.load(w["f_w1"][li, j])
                    sg = self.ws_b.load(w["f_w1"][li, NJ + j])
                    wta = self.ws_a.use(sa)
                    wtg = self.ws_b.use(sg)
                    for st in range(SH // 512):
                        gl = slice(h0 + st * 512, h0 + st * 512 + 512)
                        pa = pp.get(); pg = pp.get()
                        ctx.wait("tensor", vs, vln)
                        ta = tg = None
                        for cc in range(NCH):
                            fn = partial(i_mm, out=pa.t[:, :],
                                         lhsT=wta[:, cc * 128:(cc + 1) * 128],
                                         rhs=XS[:, cc * S:(cc + 1) * S][:, gl],
                                         start=(cc == 0), stop=(cc == NCH - 1))
                            ta = ctx.inc("tensor", fn, ts) if cc == NCH - 1 \
                                else ctx.emit("tensor", fn)
                        for cc in range(NCH):
                            fn = partial(i_mm, out=pg.t[:, :],
                                         lhsT=wtg[:, cc * 128:(cc + 1) * 128],
                                         rhs=XS[:, cc * S:(cc + 1) * S][:, gl],
                                         start=(cc == 0), stop=(cc == NCH - 1))
                            tg = ctx.inc("tensor", fn, ts) if cc == NCH - 1 \
                                else ctx.emit("tensor", fn)
                        ctx.wait("scalar", ts, ta)
                        ctx.wait("scalar", vs, ag_mark[par])
                        s1 = ctx.inc("scalar", partial(
                            i_act, out=asb[par][:, st * 512:(st + 1) * 512], src=pa.t[:, :],
                            func=AF.Identity, bias=b1a[:, j:j + 1]), ss)
                        pp.put(pa, ss, s1)
                        ctx.wait("scalar", ts, tg)
                        s2 = ctx.inc("scalar", partial(
                            i_act, out=gsb[par][:, st * 512:(st + 1) * 512], src=pg.t[:, :],
                            func=AF.Gelu, bias=b1g[:, j:j + 1]), ss)
                        pp.put(pg, ss, s2)
                        ctx.wait("vector", ss, s2)
                        vv = ctx.inc("vector", partial(
                            i_tt, out=ACT[:, j * SH + st * 512:j * SH + (st + 1) * 512],
                            a=asb[par][:, st * 512:(st + 1) * 512],
                            b=gsb[par][:, st * 512:(st + 1) * 512], op=OP.mult), vs)
                        ag_mark[par] = vv
                    self.ws_a.done(sa, ts, ts.n)
                    self.ws_b.done(sg, ts, ts.n)
                vact = vs.n
                self.xs_free = ts.n

                for e in range(NCH):
                    slot = ws_w2.load(w["f_w2"][li, e])
                    wt = ws_w2.use(slot)
                    for st in range(SH // 512):
                        sl = slice(st * 512, st * 512 + 512)
                        pb = pp.get()
                        ctx.wait("tensor", vs, vact)
                        tv = None
                        for j in range(NJ):
                            fn = partial(i_mm, out=pb.t[:, :],
                                         lhsT=wt[:, j * 128:(j + 1) * 128],
                                         rhs=ACT[:, j * SH:(j + 1) * SH][:, sl],
                                         start=(j == 0), stop=(j == NJ - 1))
                            tv = ctx.inc("tensor", fn, ts) if j == NJ - 1 \
                                else ctx.emit("tensor", fn)
                        ctx.wait("vector", ts, tv)
                        ctx.wait("vector", ss, sc_s_mark)
                        ctx.dr("vector")
                        v1 = ctx.inc("vector", partial(
                            i_ts, out=sc[:, :], a=pb.t[:, :], s1=b2c[:, e:e + 1], s2=None,
                            op0=OP.add, op1=None), vs)
                        ctx.dr("vector")
                        ctx.inc("vector", partial(
                            i_tt, out=Z[:, e * S + h0:e * S + h0 + SH][:, sl], a=sc[:, :],
                            b=Z[:, e * S + h0:e * S + h0 + SH][:, sl], op=OP.add), vs)
                        pp.put(pb, vs, v1)
                    ws_w2.done(slot, ts, ts.n)
                half_ts = ts.n
        ctx.flush()

    def attn_b(self, xres):
        S, NS, NT, ctx = self.S, self.NS, self.NT, self.ctx
        ts, vs, ss, sd = self.ts, self.vs, self.ss, self.sx
        sxo = self.sxo
        pp, w, cst = self.pp, self.w, self.cst
        XN = self.XB
        OT = self.XB

        with ExitStack() as stk:
            QT = self.sb("bQT", [128, NCH * S], BF16, stk)
            KT = self.sb("bKT", [128, NCH * S], BF16, stk)
            vtok = self.sb("bvtok", [128, NCH * S], BF16, stk)
            TH = self.sb("bTH", [128, S], F32, stk)
            P = self.sb("bP", [128, S], BF16, stk)
            PT = self.sb("bPT", [128, 4 * S], BF16, stk)
            sc1 = self.sb("bsc1", [128, 512], F32, stk)
            sc2 = self.sb("bsc2", [128, 512], F32, stk)
            vtc = self.sb("bvtc", [128, S], BF16, stk)
            sums = self.sb("bsums", [128, 2], F32, stk)
            rc = self.sb("brc", [128, 2], F32, stk)
            Xc = self.sb("bXc", [128, S], F32, stk)

            ctx.wait("tensor", vs, self._lnfin)

            def mk_evac_b(dst, bcol, e):
                def ev(pb, st, tv):
                    sl = slice(st * 512, st * 512 + 512)
                    ctx.wait("scalar", ts, tv)
                    sv = ctx.inc("scalar", partial(
                        i_act, out=dst[:, e * S:(e + 1) * S][:, sl], src=pb.t[:, :],
                        func=AF.Identity, bias=bcol[:, e:e + 1]), ss)
                    pp.put(pb, ss, sv)
                return ev
            for e in range(NCH):
                self.proj_echunk(w["b_wqT"][e], XN, self._lnfin, mk_evac_b(QT, cst["b_bq"], e))
            for e in range(NCH):
                self.proj_echunk(w["b_wkT"][e], XN, self._lnfin, mk_evac_b(KT, cst["b_bk"], e))
            for e in range(NCH):
                def evv(pb, st, tv, e=e):
                    sl = slice(st * 512, st * 512 + 512)
                    ctx.wait("scalar", ts, tv)
                    sv = ctx.inc("scalar", partial(
                        i_act, out=vtc[:, sl], src=pb.t[:, :], func=AF.Identity,
                        bias=cst["b_bv"][:, e:e + 1]), ss)
                    pp.put(pb, ss, sv)
                self.proj_echunk(w["b_wvT"][e], XN, self._lnfin, evv)
                sve = ss.n
                ctx.wait("tensor", ss, sve)
                for j0 in range(0, NT, 8):
                    hi = min(j0 + 8, NT)
                    pb = pp.get(BF16)
                    tv = None
                    for j in range(j0, hi):
                        fn = partial(i_tr, out=pb.t[:, (j - j0) * 128:(j - j0 + 1) * 128],
                                     src=vtc[:, j * 128:(j + 1) * 128], iden=cst["id_bf"][:, :])
                        tv = ctx.inc("tensor", fn, ts) if j == hi - 1 else ctx.emit("tensor", fn)
                    ctx.wait("vector", ts, tv)
                    vv = None
                    for j in range(j0, hi):
                        vv = ctx.inc("vector", partial(
                            i_copy,
                            out=vtok[:, j * 1024 + e * 128:j * 1024 + (e + 1) * 128],
                            src=pb.t[:, (j - j0) * 128:(j - j0 + 1) * 128]), vs)
                    pp.put(pb, vs, vv)
            v_vtok = vs.n

            th_mark = 0
            p_mark = 0
            pt_mark = 0
            for g0 in range(0, NT, 4):
                ghi = min(g0 + 4, NT)
                for sg in range(g0, ghi):
                    i0 = sg * 128
                    for jt in range(NS):
                        jl = slice(jt * 512, jt * 512 + 512)
                        pb = pp.get()
                        tv = None
                        for c in range(NCH):
                            fn = partial(i_mm, out=pb.t[:, :],
                                         lhsT=QT[:, c * S + i0:c * S + i0 + 128],
                                         rhs=KT[:, c * S:(c + 1) * S][:, jl],
                                         start=(c == 0), stop=(c == NCH - 1))
                            tv = ctx.inc("tensor", fn, ts) if c == NCH - 1 \
                                else ctx.emit("tensor", fn)
                        ctx.wait("scalar", ts, tv)
                        ctx.wait("scalar", vs, th_mark)
                        s1 = ctx.inc("scalar", partial(i_act, out=sc1[:, :], src=pb.t[:, :],
                                                       func=AF.Tanh, scale=0.5), ss)
                        s2 = ctx.inc("scalar", partial(i_act, out=sc2[:, :], src=pb.t[:, :],
                                                       func=AF.Tanh), ss)
                        ctx.wait("vector", ss, s2)
                        v1 = ctx.inc("vector", partial(
                            i_stt, out=sc2[:, :], a=sc1[:, :], s=2.0 / 3.0, b=sc2[:, :],
                            op0=OP.mult, op1=OP.add), vs)
                        ctx.wait("scalar", vs, v1)
                        s3 = ctx.inc("scalar", partial(i_act, out=sc1[:, :], src=pb.t[:, :],
                                                       func=AF.Relu, scale=0.1), ss)
                        pp.put(pb, ss, s3)
                        ctx.wait("vector", ss, s3)
                        ctx.dr("vector")
                        th_mark = ctx.inc("vector", partial(
                            i_stt, out=TH[:, jl], a=sc2[:, :], s=0.15, b=sc1[:, :],
                            op0=OP.mult, op1=OP.add), vs)
                    ctx.wait("scalar", vs, th_mark)
                    ctx.wait("scalar", ts, p_mark)
                    sv = ctx.inc("scalar", partial(i_act, out=P[:, :], src=TH[:, :],
                                                   func=AF.Exp, accum=sums[:, 0:1]), ss)
                    ctx.wait("vector", ss, sv)
                    ctx.inc("vector", partial(i_recip, out=rc[:, 0:1], src=sums[:, 0:1]), vs)
                    ctx.dr("vector")
                    vn = ctx.inc("vector", partial(i_ts, out=P[:, :], a=P[:, :], s1=rc[:, 0:1],
                                                   s2=None, op0=OP.mult, op1=None), vs)
                    ctx.wait("tensor", vs, vn)
                    ctx.wait("tensor", vs, v_vtok)
                    ctx.wait("vector", ts, pt_mark)
                    for j0 in range(0, NT, 8):
                        hi = min(j0 + 8, NT)
                        pb = pp.get(BF16)
                        tv = None
                        for j in range(j0, hi):
                            fn = partial(i_tr, out=pb.t[:, (j - j0) * 128:(j - j0 + 1) * 128],
                                         src=P[:, j * 128:(j + 1) * 128],
                                         iden=cst["id_bf"][:, :])
                            tv = ctx.inc("tensor", fn, ts) if j == hi - 1 \
                                else ctx.emit("tensor", fn)
                        p_mark = tv
                        ctx.wait("vector", ts, tv)
                        vv = None
                        for j in range(j0, hi):
                            vv = ctx.inc("vector", partial(
                                i_copy,
                                out=PT[:, j * 512 + (sg - g0) * 128:
                                       j * 512 + (sg - g0 + 1) * 128],
                                src=pb.t[:, (j - j0) * 128:(j - j0 + 1) * 128]), vs)
                        pp.put(pb, vs, vv)
                vptg = vs.n
                gw = (ghi - g0) * 128
                gi0 = g0 * 128
                for e in range(NCH):
                    pb = pp.get()
                    ctx.wait("tensor", vs, vptg)
                    tv = None
                    for j in range(NT):
                        fn = partial(i_mm, out=pb.t[:, 0:gw],
                                     lhsT=vtok[:, j * 1024 + e * 128:j * 1024 + (e + 1) * 128],
                                     rhs=PT[:, j * 512:j * 512 + gw],
                                     start=(j == 0), stop=(j == NT - 1))
                        tv = ctx.inc("tensor", fn, ts) if j == NT - 1 else ctx.emit("tensor", fn)
                    pt_mark = tv
                    ctx.wait("scalar", ts, tv)
                    sv = ctx.inc("scalar", partial(
                        i_act, out=OT[:, e * S + gi0:e * S + gi0 + gw], src=pb.t[:, 0:gw],
                        func=AF.Copy), ss)
                    pp.put(pb, ss, sv)
            v_ot = ss.n

            for e in range(NCH):
                ctx.wait_marks("sync", self.xres_w)
                ctx.wait("sync", sxo, sxo.n)
                xsem = self.next_xl()
                vx = ctx.inc("sync", partial(
                    i_dma, out=Xc[:, :],
                    src=xres.rearrange("(c p) s -> p c s", p=128)[:, e, :]),
                    xsem, 16)
                def evr(pb, st, tv, e=e, vx=vx, xsem=xsem):
                    sl = slice(st * 512, st * 512 + 512)
                    ctx.wait("vector", ts, tv)
                    ctx.wait("vector", xsem, vx)
                    v1 = ctx.inc("vector", partial(
                        i_ts, out=sc1[:, :], a=pb.t[:, :], s1=cst["b_bo"][:, e:e + 1],
                        s2=None, op0=OP.add, op1=None), vs)
                    ctx.dr("vector")
                    ctx.inc("vector", partial(i_tt, out=Xc[:, sl], a=sc1[:, :], b=Xc[:, sl],
                                              op=OP.add), vs)
                    pp.put(pb, vs, v1)
                ctx.wait("tensor", ss, v_ot)
                self.proj_echunk(w["b_woT"][e], OT, 0, evr, ws=self.ws_b)
                ctx.wait("sync", vs, vs.n)
                vw = ctx.inc("sync", partial(
                    i_dma,
                    out=xres.rearrange("(c p) s -> p c s", p=128)[:, e, :],
                    src=Xc[:, :]), sxo, 16)
                ctx.wait("sync", sxo, vw)
                if e == NCH - 1:
                    self.xres_w = [(sxo, vw)]
        ctx.flush()


_CACHE = {}


def build(S=2048, sublayers=("attn_a", "inner1", "attn_b", "inner2"), n_inner=DEPTH_IN):
    key = (S, tuple(sublayers), n_inner)
    if key not in _CACHE:
        b = Builder(S, tuple(sublayers), n_inner)
        b.nc.finalize()
        _CACHE[key] = b
    return _CACHE[key]


def make_inmaps(b, inputs, S=2048, n_cores=8):
    sh, per_core = host_prepare(inputs, S, n_cores)
    m = {}
    for wname in ("a_wqT", "a_wkT", "a_wvT", "a_wogT", "a_woutT", "a_wifT",
                  "b_wqT", "b_wkT", "b_wvT", "b_woT",
                  "f_wqkv", "f_wo", "f_w1", "f_w2"):
        m[wname] = sh[wname]
    for dname, key in b.feeds.items():
        arr = sh[key]
        if arr.ndim == 3 and key.startswith(("selA", "selV")) is False:
            pass
        m[dname] = arr
    return [dict(m, xin=pc["xin"]) for pc in per_core]


def kernel(**inputs):
    S = 2048
    n_cores = 8
    b = build(S)
    in_maps = make_inmaps(b, inputs, S, n_cores)
    res = run_bass_kernel_spmd(b.nc, in_maps, core_ids=list(range(n_cores)))
    outs = [np.asarray(r["yout"]).T for r in res.results]
    return np.ascontiguousarray(np.stack(outs, 0).astype(np.float32))
